# version 02
# baseline (speedup 1.0000x reference)
import math
import numpy as np
import ml_dtypes
import concourse.bass as bass
import concourse.mybir as mybir
from concourse.bass_utils import run_bass_kernel_spmd

F32 = mybir.dt.float32
BF16 = mybir.dt.bfloat16
I32 = mybir.dt.int32
AF = mybir.ActivationFunctionType
AL = mybir.AluOpType
AX = mybir.AxisListType
bfdt = ml_dtypes.bfloat16

T = 2048
D = 2048
NH = 16
HD = 128
HID = 5632
NT = 16
NDI = 16
NHT = 44
QC = 512
NCH = 4
NM = NH * NCH
C_SHIFT = 24.0
ASCL = 1.0 / math.sqrt(HD)
EPS = 1e-5

DEBUG = False
PH = 4

ARENA_F32_COLS = 53040

OFF_SG    = 202704
OFF_SU    = 202896
OFF_CVEC  = 203088
OFF_ONESR = 203120
OFF_ONESB = 203632
OFF_SDB   = 203888
OFF_SPC   = 199168
OFF_SPC2  = 200192

OFF_XT    = 0
OFF_QT    = 65536
OFF_COS   = 131072
OFF_SING  = 135168
OFF_WBUF  = 139264
OFF_KSTG  = 147456
OFF_RT1   = 155648
OFF_TMPG  = 163840
OFF_RT2   = 172032
OFF_VW    = 180224
OFF_VSTG  = 196608

OFF_YT    = 0
OFF_KVK   = 131072
OFF_KVV   = 139264
OFF_PT    = 147456
OFF_LSB   = 155648
OFF_LQ    = 160000
OFF_RLQ   = 160064
OFF_RLROW = 160128
OFF_YRAW  = 164224

OFF_XRES  = 131072
OFF_ROUT  = 147456
OFF_STAT  = 172032
OFF_X1O   = 172608

OFF_X1T   = 0
OFF_HT    = 32768
OFF_GBUF  = 122880
OFF_UBUF  = 131072
OFF_GDQ   = 139264
OFF_UDQ   = 147456
OFF_GS    = 155648
OFF_DSB   = 122880
OFF_WDB   = 188416
OFF_WDQ   = 190464
OFF_STAT4 = 192512
OFF_XRES4 = 0
OFF_ROUT4 = 16384
OFF_SUMD4 = 40960
OFF_SUMD3 = 188992

S_SUM, S_SQ, S_S2, S_VN, S_VE, S_MN, S_SD, S_RS, S_NM = range(9)


def build_program():
    nc = bass.Bass("TRN2", target_bir_lowering=False)

    x_h   = nc.dram_tensor("x",   [T, D], F32, kind="ExternalInput").ap()
    xT_h  = nc.dram_tensor("xT",  [D, T], F32, kind="ExternalInput").ap()
    wq_h  = nc.dram_tensor("wq_c", [NT, 128, D], F32, kind="ExternalInput").ap()
    wk_h  = nc.dram_tensor("wk_c", [NT, 128, D], F32, kind="ExternalInput").ap()
    wv_h  = nc.dram_tensor("wv_c", [4, 128, NDI * 512], F32, kind="ExternalInput").ap()
    wo_h  = nc.dram_tensor("wo_T", [D, D], F32, kind="ExternalInput").ap()
    wg_h  = nc.dram_tensor("wg_c", [NHT, 128, 2048], I32, kind="ExternalInput").ap()
    wu_h  = nc.dram_tensor("wu_c", [NHT, 128, 2048], I32, kind="ExternalInput").ap()
    wd_h  = nc.dram_tensor("wd_c", [NHT, 4, 128, 512], I32, kind="ExternalInput").ap()
    sg_h  = nc.dram_tensor("sg_c", [128, NHT], F32, kind="ExternalInput").ap()
    su_h  = nc.dram_tensor("su_c", [128, NHT], F32, kind="ExternalInput").ap()
    sd_h  = nc.dram_tensor("sd_b", [128, 2048], F32, kind="ExternalInput").ap()
    out_h = nc.dram_tensor("out", [T, D], F32, kind="ExternalOutput").ap()

    scr_kind = "ExternalOutput" if DEBUG else "Internal"
    kt_sp = nc.dram_tensor("kt_sp", [NH, 128, T], BF16, kind=scr_kind).ap()
    v_sp  = nc.dram_tensor("v_sp",  [NT, 128, D], BF16, kind=scr_kind).ap()
    x1f   = nc.dram_tensor("x1f", [T, D], F32, kind=scr_kind).ap()
    x1b   = nc.dram_tensor("x1b", [T, D], BF16, kind=scr_kind).ap()
    qt_dbg = nc.dram_tensor("qt_dbg", [NH, 128, T], BF16, kind="ExternalOutput").ap() if DEBUG else None
    yt_dbg = nc.dram_tensor("yt_dbg", [NH, 128, T], BF16, kind="ExternalOutput").ap() if DEBUG else None

    inv_freq = (1.0 / 10000.0 ** (np.arange(0, HD, 2, dtype=np.float32) / HD)).astype(np.float32)
    ang = np.arange(T, dtype=np.float32)[:, None] * inv_freq[None, :]
    cosT = np.cos(ang).T.astype(bfdt)
    sinT = np.sin(ang).T.astype(np.float32)
    gtab = np.concatenate([sinT[32:64], -sinT[0:32]], axis=0).astype(bfdt)
    cos_h  = nc.inline_tensor(np.ascontiguousarray(cosT), name="cosT")
    g_h    = nc.inline_tensor(np.ascontiguousarray(gtab), name="gtab")
    ones_h = nc.inline_tensor(np.ones((128, 128), dtype=bfdt), name="onesb")
    onesr_h = nc.inline_tensor(np.ones((1, 128), dtype=np.float32), name="onesr")
    cvec = np.zeros((128, 4), dtype=np.float32)
    cvec[:, 0] = -C_SHIFT
    cvec[:, 1] = -128.0
    cvec[:, 2] = EPS
    cvec_h = nc.inline_tensor(cvec, name="cvec")
    zeros_h = nc.inline_tensor(np.zeros((128, 320), dtype=np.float32), name="zeros")

    AB = nc.alloc_sbuf_tensor("AB", [128, ARENA_F32_COLS], F32)
    PS = nc.alloc_psum_tensor("PS", [128, 4096], F32)
    ps_ap = PS.ap()

    def f32(off_b, p, cols):
        return AB.ap()[0:p, off_b // 4: off_b // 4 + cols]

    def bf16(off_b, p, cols):
        return AB.ap()[0:p, off_b // 4: off_b // 4 + cols // 2].bitcast(BF16)

    def bank(b, n=1):
        return ps_ap[:, b * 512:(b + n) * 512]

    XT  = bf16(OFF_XT, 128, NDI * 2048)
    QT  = bf16(OFF_QT, 128, NH * 2048)
    COS = bf16(OFF_COS, 64, 2048)
    GTB = bf16(OFF_SING, 64, 2048)
    WB  = [bf16(OFF_WBUF + i * 4096, 128, 2048) for i in range(2)]
    KST = [bf16(OFF_KSTG + i * 4096, 128, 2048) for i in range(2)]
    RT1 = [bf16(OFF_RT1 + i * 4096, 64, 2048) for i in range(2)]
    TMG = [bf16(OFF_TMPG + i * 4096, 64, 2048) for i in range(2)]
    RT2 = [bf16(OFF_RT2 + i * 4096, 64, 2048) for i in range(2)]
    VW  = bf16(OFF_VW, 128, 8192)
    VST = [bf16(OFF_VSTG + i * 1024, 128, 512) for i in range(2)]

    YT  = bf16(OFF_YT, 128, NH * 2048)
    KVK = [bf16(OFF_KVK + i * 4096, 128, 2048) for i in range(2)]
    KVV = [bf16(OFF_KVV + i * 4096, 128, 2048) for i in range(2)]
    PT  = [bf16(OFF_PT + i * 2048, 128, 1024) for i in range(4)]
    LSB = [f32(OFF_LSB + i * 2048, 1, 512) for i in range(2)]
    LQ  = [f32(OFF_LQ + i * 16, 128, 4) for i in range(2)]
    RLQ = [f32(OFF_RLQ + i * 16, 128, 4) for i in range(2)]
    RLR = [f32(OFF_RLROW + i * 2048, 1, 512) for i in range(2)]
    YRW = [f32(OFF_YRAW + i * 2048, 128, 512) for i in range(2)]
    WOT = QT

    XRS = [f32(OFF_XRES + i * 8192, 128, 2048) for i in range(2)]
    ROU = [f32(OFF_ROUT + i * 8192, 128, 2048) for i in range(3)]
    X1O = [f32(OFF_X1O + i * 8192, 128, 2048) for i in range(2)]

    X1T = bf16(OFF_X1T, 128, NDI * 1024)
    HT  = bf16(OFF_HT, 128, NHT * 1024)
    GBF = [bf16(OFF_GBUF + i * 4096, 128, 2048) for i in range(2)]
    UBF = [bf16(OFF_UBUF + i * 4096, 128, 2048) for i in range(2)]
    GDQ = [bf16(OFF_GDQ + i * 4096, 128, 2048) for i in range(2)]
    UDQ = [bf16(OFF_UDQ + i * 4096, 128, 2048) for i in range(2)]
    GS  = [f32(OFF_GS + i * 4096, 128, 1024) for i in range(2)]
    DSB = f32(OFF_DSB, 128, 8 * 2048)
    WDB = [bf16(OFF_WDB + i * 1024, 128, 512) for i in range(2)]
    WDQ = [bf16(OFF_WDQ + i * 1024, 128, 512) for i in range(2)]
    RO4 = [f32(OFF_ROUT4 + i * 8192, 128, 2048) for i in range(3)]
    XR4 = [f32(OFF_XRES4 + i * 8192, 128, 2048) for i in range(2)]
    SD3 = f32(OFF_SUMD3, 128, 2048)
    SD4 = f32(OFF_SUMD4, 128, 2048)

    SGC = f32(OFF_SG, 128, NHT)
    SUC = f32(OFF_SU, 128, NHT)
    CVC = f32(OFF_CVEC, 128, 4)
    ONR = f32(OFF_ONESR, 1, 128)
    ONB = bf16(OFF_ONESB, 128, 128)
    ONRB = bf16(OFF_ONESB, 1, 128)
    RLRB = [bf16(OFF_RLROW + i * 2048, 1, 512) for i in range(2)]
    SDB = f32(OFF_SDB, 128, 2048)
    SPC = f32(OFF_SPC, 128, 256)
    SPC2 = f32(OFF_SPC2, 128, 64)

    vf_sem = nc.alloc_semaphore("vfence")
    vf_n = [0]
    def vstep(v, instr, spacers=2):
        instr.then_inc(vf_sem, 1)
        vf_n[0] += 1
        for _ in range(spacers):
            vspace(v)
        v.wait_ge(vf_sem, vf_n[0])

    vsp_n = [0]
    def vspace(v):
        n = vsp_n[0] % 8
        vsp_n[0] += 1
        return v.tensor_copy(SPC[:, 32 * n:32 * n + 32], SDB[:, 0:32])

    ssp_n = [0]
    def sspace(sc):
        n = ssp_n[0] % 2
        ssp_n[0] += 1
        return sc.copy(SPC2[:, 32 * n:32 * n + 32], SDB[:, 32:64])

    def stat(par, idx):
        return f32(OFF_STAT + par * 144, 128, 36)[:, idx:idx + 1]

    def stat4(par, idx):
        return f32(OFF_STAT4 + par * 144, 128, 36)[:, idx:idx + 1]

    def sem(name):
        return nc.alloc_semaphore(name)

    s0c = sem("s0c"); s0x = sem("s0x"); s0sd = sem("s0sd"); s0d = sem("s0d")
    s1w = [sem("s1wA"), sem("s1wB")]; s1pe = sem("s1pe"); s1sc = sem("s1sc")
    s1v1 = sem("s1v1"); s1sw = sem("s1sw"); s1v2 = sem("s1v2"); s1sp = sem("s1sp")
    s1vw = sem("s1vw"); s1pb = sem("s1pb"); s1scb = sem("s1scb"); s1spb = sem("s1spb")
    s2kv = [sem("s2kvA"), sem("s2kvB")]; s2st = sem("s2st"); s2pv = sem("s2pv"); s2sc = sem("s2sc")
    s2yr = sem("s2yr"); s2l = sem("s2l"); s2r = sem("s2r"); s2g1 = sem("s2g1")
    s2g2 = sem("s2g2"); s2bc = sem("s2bc"); s2mu = sem("s2mu")
    s2wo = sem("s2wo")
    s3x = sem("s3x"); s3pe = sem("s3pe"); s3ra = sem("s3ra"); s3sq = sem("s3sq")
    s3v1 = sem("s3v1"); s3v2 = sem("s3v2"); s3no = sem("s3no")
    s3spf = sem("s3spf"); s3spb = sem("s3spb")
    s4xt = sem("s4xt"); s4w = [sem("s4wA"), sem("s4wB")]; s4dq = sem("s4dq"); s4pe = sem("s4pe")
    s4sil = sem("s4sil"); s4stt = sem("s4stt"); s4wd = [sem("s4wdA"), sem("s4wdB")]; s4wq = sem("s4wq")
    s4dn = sem("s4dn"); s4de = sem("s4de"); s4x = sem("s4x"); s4ra = sem("s4ra")
    s4sq = sem("s4sq"); s4v1 = sem("s4v1"); s4v2 = sem("s4v2")
    s4no = sem("s4no"); s4out = sem("s4out")

    with nc.Block() as blk:

        @blk.sync
        def _(s):
            s.dma_start(out=COS, in_=cos_h.ap()).then_inc(s0c, 16)
            s.dma_start(out=GTB, in_=g_h.ap()).then_inc(s0c, 16)
            s.dma_start(out=ONB, in_=ones_h.ap()).then_inc(s0c, 16)
            s.dma_start(out=ONR, in_=onesr_h.ap()).then_inc(s0c, 16)
            s.dma_start(out=CVC, in_=cvec_h.ap()).then_inc(s0c, 16)
            s.dma_start(out=SGC, in_=sg_h).then_inc(s0c, 16)
            s.dma_start(out=SUC, in_=su_h).then_inc(s0c, 16)
            s.dma_start(out=SDB, in_=sd_h).then_inc(s0c, 16)
            s.dma_start(out=f32(OFF_SPC, 128, 320), in_=zeros_h.ap()).then_inc(s0c, 16)
            if PH < 1:
                s.wait_ge(s0x, 16 * NDI)
                s.wait_ge(s0sd, 1)
                return
            for j in range(32):
                par = j % 2
                s.wait_ge(s1v1, j + 1)
                if j >= 2:
                    s.wait_ge(s1v2, j - 1)
                s.dma_start(out=RT2[par][0:32, :], in_=TMG[par][32:64, :]).then_inc(s1sw, 16)
                s.dma_start(out=RT2[par][32:64, :], in_=TMG[par][0:32, :]).then_inc(s1sw, 16)
                if j >= 16:
                    ot = j - 16
                    s.wait_ge(s1v2, j + 1)
                    if ot >= 1:
                        s.wait_ge(s1sp, 16 * ot)
                    s.dma_start(out=kt_sp[ot], in_=KST[par]).then_inc(s1sp, 16)
            for i in range(64):
                jb, tt = divmod(i, 8)
                dc, th = divmod(jb, 2)
                s.wait_ge(s1scb, i + 1)
                if i >= 1:
                    s.wait_ge(s1spb, 16 * i)
                s.dma_start(out=v_sp[th * 8 + tt][:, dc * 512:(dc + 1) * 512],
                            in_=VST[i % 2]).then_inc(s1spb, 16)
            if DEBUG:
                s.wait_ge(s1v2, 32)
                for h in range(NH):
                    s.dma_start(out=qt_dbg[h], in_=QT[:, h * 2048:(h + 1) * 2048]).then_inc(s0d, 16)
            if PH < 2:
                s.wait_ge(s1sp, 16 * 16)
                s.wait_ge(s1spb, 16 * 64)
                if DEBUG:
                    s.wait_ge(s0d, 16 * NH)
                return
            s.wait_ge(s1sp, 16 * 16)
            s.wait_ge(s1spb, 16 * 64)
            for h in range(NH):
                if h >= 2:
                    s.wait_ge(s2pv, 32 * (h - 1))
                    s.wait_ge(s2kv[h % 2], 32 * (h // 2))
                s.dma_start(out=KVK[h % 2], in_=kt_sp[h]).then_inc(s2kv[h % 2], 16)
                s.dma_start(out=KVV[h % 2],
                            in_=v_sp[:, :, h * 128:(h + 1) * 128].rearrange("k p b -> p k b")).then_inc(s2kv[h % 2], 16)
            if DEBUG:
                s.wait_ge(s2mu, NM)
                for h in range(NH):
                    s.dma_start(out=yt_dbg[h], in_=YT[:, h * 2048:(h + 1) * 2048]).then_inc(s0d, 16)
            if PH < 3:
                s.wait_ge(s2mu, NM)
                if DEBUG:
                    s.wait_ge(s0d, 16 * 2 * NH)
                return
            s.wait_ge(s2mu, NM)
            for it in range(NT + 4):
                if it < NT:
                    if it >= 2:
                        s.wait_ge(s3ra, it - 1)
                        s.wait_ge(s3sq, it - 1)
                    if it >= 1:
                        s.wait_ge(s3x, 16 * it)
                    s.dma_start(out=XRS[it % 2], in_=x_h[it * 128:(it + 1) * 128, :]).then_inc(s3x, 16)
                sp = it - 3
                if 0 <= sp < NT:
                    s.wait_ge(s3no, sp + 1)
                    if sp >= 1:
                        s.wait_ge(s3spf, 16 * sp)
                    s.dma_start(out=x1f[sp * 128:(sp + 1) * 128, :], in_=X1O[sp % 2]).then_inc(s3spf, 16)
            if PH < 4:
                s.wait_ge(s3spf, 16 * NT)
                s.wait_ge(s3spb, 16 * NT)
                if DEBUG:
                    s.wait_ge(s0d, 16 * 2 * NH)
                return
            for th in range(2):
                if th == 0:
                    s.wait_ge(s3pe, NT)
                    s.wait_ge(s3spb, 16 * NT)
                    s.wait_ge(s3spf, 16 * NT)
                    if DEBUG:
                        s.wait_ge(s0d, 16 * 2 * NH)
                else:
                    s.wait_ge(s4out, 16 * 8)
                for dt in range(NDI):
                    gi = th * NDI + dt
                    if gi >= 1:
                        s.wait_ge(s4xt, 16 * gi)
                    s.dma_start_transpose(
                        X1T[:, dt * 1024:(dt + 1) * 1024],
                        x1b[th * 1024:(th + 1) * 1024, dt * 128:(dt + 1) * 128],
                    ).then_inc(s4xt, 16)
                for it in range(8 + 2):
                    if it >= 2:
                        g2 = th * 8 + it - 2
                        s.wait_ge(s4no, g2 + 1)
                        if g2 >= 1:
                            s.wait_ge(s4out, 16 * g2)
                        s.dma_start(out=out_h[g2 * 128:(g2 + 1) * 128, :],
                                    in_=XR4[g2 % 2]).then_inc(s4out, 16)
                    if it < 8:
                        g = th * 8 + it
                        if it < 2:
                            s.wait_ge(s4pe, 2 * NHT * (th + 1))
                        if it >= 2:
                            s.wait_ge(s4ra, g - 1)
                            s.wait_ge(s4out, 16 * (g - 1))
                        if g >= 1:
                            s.wait_ge(s4x, 16 * g)
                        s.dma_start(out=XR4[g % 2], in_=x1f[g * 128:(g + 1) * 128, :]).then_inc(s4x, 16)
            s.wait_ge(s4out, 16 * NT)
            if DEBUG:
                s.wait_ge(s0d, 16 * (2 * NH))

        @blk.gpsimd
        def _(g):
            for dt in range(NDI):
                g.dma_start(out=XT[:, dt * 2048:(dt + 1) * 2048],
                            in_=xT_h[dt * 128:(dt + 1) * 128, :]).then_inc(s0x, 16)
            if PH < 1:
                return
            for j in range(32):
                src = wq_h[j] if j < 16 else wk_h[j - 16]
                if j >= 2:
                    g.wait_ge(s1pe, j - 1)
                    g.wait_ge(s1w[j % 2], 16 * (j // 2))
                g.dma_start(out=WB[j % 2], in_=src).then_inc(s1w[j % 2], 16)
            for dc in range(4):
                if dc >= 1:
                    g.wait_ge(s1pb, 2 * dc)
                    g.wait_ge(s1vw, 16 * dc)
                g.dma_start(out=VW, in_=wv_h[dc]).then_inc(s1vw, 16)
            if PH < 2:
                return
            for m in range(NM):
                h, c = divmod(m, NCH)
                g.wait_ge(s2l, m + 1)
                if m >= 1:
                    g.wait_ge(s2g1, 16 * m)
                if m >= 2:
                    g.wait_ge(s2r, m - 1)
                g.dma_start(out=LQ[m % 2], in_=LSB[m % 2]).then_inc(s2g1, 16)
                g.wait_ge(s2r, m + 1)
                if m >= 1:
                    g.wait_ge(s2g2, 16 * m)
                if m >= 2:
                    g.wait_ge(s2bc, m - 1)
                g.dma_start(out=RLRB[m % 2], in_=RLQ[m % 2]).then_inc(s2g2, 16)
                if c == NCH - 1:
                    g.wait_ge(s2pv, 32 * (h + 1))
                    if DEBUG and h == 0:
                        g.wait_ge(s0d, 16 * NH)
                    if h >= 1:
                        g.wait_ge(s2wo, 16 * h)
                    g.dma_start(out=WOT[:, h * 2048:(h + 1) * 2048],
                                in_=wo_h[h * 128:(h + 1) * 128, :]).then_inc(s2wo, 16)
            if PH < 3:
                return
            for tt in range(NT):
                g.wait_ge(s3no, tt + 1)
                if tt >= 1:
                    g.wait_ge(s3spb, 16 * tt)
                g.dma_start(out=x1b[tt * 128:(tt + 1) * 128, :], in_=X1O[tt % 2]).then_inc(s3spb, 16)
            if PH < 4:
                return
            for th in range(2):
                for ht in range(NHT):
                    i = th * NHT + ht
                    if th == 1 and ht == 0:
                        g.wait_ge(s4ra, 8)
                    if i >= 2:
                        g.wait_ge(s4dq, 2 * (i - 1))
                        g.wait_ge(s4w[i % 2], 32 * (i // 2))
                    g.dma_start(out=GBF[i % 2], in_=wg_h[ht]).then_inc(s4w[i % 2], 16)
                    g.dma_start(out=UBF[i % 2], in_=wu_h[ht]).then_inc(s4w[i % 2], 16)
                for dc in range(4):
                    for h2 in range(NHT):
                        i = (th * 4 + dc) * NHT + h2
                        if i >= 2:
                            g.wait_ge(s4wq, i - 1)
                            g.wait_ge(s4wd[i % 2], 16 * (i // 2))
                        g.dma_start(out=WDB[i % 2], in_=wd_h[h2, dc]).then_inc(s4wd[i % 2], 16)

        @blk.tensor
        def _(t):
            t.wait_ge(s0c, 16 * 9)
            if PH < 1:
                return
            t.wait_ge(s0x, 16 * NDI)
            for j in range(32):
                t.wait_ge(s1w[j % 2], 16 * (j // 2 + 1))
                if j >= 2:
                    t.wait_ge(s1sc, j - 1)
                grp = (j % 2) * 4
                for di in range(NDI):
                    for cch in range(4):
                        mm = t.matmul(bank(grp + cch)[:, 0:512],
                                      WB[j % 2][:, di * 128:(di + 1) * 128],
                                      XT[:, di * 2048 + cch * 512: di * 2048 + (cch + 1) * 512],
                                      start=(di == 0), stop=(di == NDI - 1))
                mm.then_inc(s1pe, 1)
            t.wait_ge(s1sc, 32)
            for jb in range(8):
                dc, th = divmod(jb, 2)
                if th == 0:
                    t.wait_ge(s1vw, 16 * (dc + 1))
                if jb >= 1:
                    t.wait_ge(s1scb, jb * 8)
                for di in range(NDI):
                    for tt in range(8):
                        tg = th * 8 + tt
                        mm = t.matmul(bank(tt)[:, 0:512],
                                      XT[:, di * 2048 + tg * 128: di * 2048 + (tg + 1) * 128],
                                      VW[:, di * 512:(di + 1) * 512],
                                      start=(di == 0), stop=(di == NDI - 1))
                mm.then_inc(s1pb, 1)
            if PH < 2:
                return
            for m in range(NM):
                h, c = divmod(m, NCH)
                if c == 0:
                    t.wait_ge(s2kv[h % 2], 32 * (h // 2 + 1))
                if m >= 2:
                    t.wait_ge(s2yr, m - 1)
                    t.wait_ge(s2mu, m - 1)
                qs = h * 2048 + c * 512
                for u in range(8):
                    k0, k1 = 2 * u, 2 * u + 1
                    slot = u % 2
                    if u >= 2:
                        t.wait_ge(s2sc, 8 * m + u - 1)
                    elif m >= 1:
                        t.wait_ge(s2sc, 8 * (m - 1) + 7 + (1 if u == 1 else 0))
                    t.matmul(bank(2 * slot)[:, 0:512],
                             KVK[h % 2][:, k0 * 128:(k0 + 1) * 128],
                             QT[:, qs:qs + 512], start=True, stop=True)
                    t.matmul(bank(2 * slot + 1)[:, 0:512],
                             KVK[h % 2][:, k1 * 128:(k1 + 1) * 128],
                             QT[:, qs:qs + 512], start=True,
                             stop=True).then_inc(s2st, 1)
                    if u == 4 and m >= 1:
                        t.wait_ge(s2g2, 16 * m)
                        t.matmul(bank(6 + (m - 1) % 2)[:, 0:512], ONRB, RLRB[(m - 1) % 2],
                                 start=True, stop=True).then_inc(s2bc, 1)
                    if u >= 1:
                        up = u - 1
                        t.wait_ge(s2sc, 8 * m + up + 1)
                        for kk in (2 * up, 2 * up + 1):
                            t.matmul(bank(4 + m % 2)[:, 0:512],
                                     KVV[h % 2][:, kk * 128:(kk + 1) * 128],
                                     PT[up % 4][:, (kk % 2) * 512:(kk % 2 + 1) * 512],
                                     start=(kk == 0), stop=False)
                            mm = t.matmul(bank(6 + m % 2)[0:1, 0:512],
                                          ONB[:, 0:1],
                                          PT[up % 4][:, (kk % 2) * 512:(kk % 2 + 1) * 512],
                                          start=(kk == 0), stop=False)
                        mm.then_inc(s2pv, 1)
                t.wait_ge(s2sc, 8 * m + 8)
                for kk in (14, 15):
                    t.matmul(bank(4 + m % 2)[:, 0:512],
                             KVV[h % 2][:, kk * 128:(kk + 1) * 128],
                             PT[3][:, (kk % 2) * 512:(kk % 2 + 1) * 512],
                             start=False, stop=(kk == 15))
                    mm = t.matmul(bank(6 + m % 2)[0:1, 0:512],
                                  ONB[:, 0:1],
                                  PT[3][:, (kk % 2) * 512:(kk % 2 + 1) * 512],
                                  start=False, stop=(kk == 15))
                mm.then_inc(s2pv, 1)
            t.wait_ge(s2g2, 16 * NM)
            t.matmul(bank(6 + (NM - 1) % 2)[:, 0:512], ONRB, RLRB[(NM - 1) % 2],
                     start=True, stop=True).then_inc(s2bc, 1)
            if PH < 3:
                return
            t.wait_ge(s2mu, NM)
            for tt in range(NT):
                if tt >= 2:
                    t.wait_ge(s3ra, tt - 1)
                for y in range(NH):
                    t.wait_ge(s2wo, 16 * (y + 1))
                    for cch in range(4):
                        mm = t.matmul(bank((tt % 2) * 4 + cch)[:, 0:512],
                                      YT[:, y * 2048 + tt * 128: y * 2048 + (tt + 1) * 128],
                                      WOT[:, y * 2048 + cch * 512: y * 2048 + (cch + 1) * 512],
                                      start=(y == 0), stop=(y == NH - 1))
                mm.then_inc(s3pe, 1)
            if PH < 4:
                return
            t.wait_ge(s3ra, NT)
            for th in range(2):
                t.wait_ge(s4xt, 16 * NDI * (th + 1))
                for ht in range(NHT):
                    i = th * NHT + ht
                    t.wait_ge(s4dq, 2 * (i + 1))
                    if i >= 2:
                        t.wait_ge(s4stt, i - 1)
                    par = i % 2
                    for di in range(NDI):
                        for cch in range(2):
                            mm = t.matmul(bank(par * 4 + cch)[:, 0:512],
                                          GDQ[par][:, di * 128:(di + 1) * 128],
                                          X1T[:, di * 1024 + cch * 512: di * 1024 + (cch + 1) * 512],
                                          start=(di == 0), stop=(di == NDI - 1))
                    mm.then_inc(s4pe, 1)
                    for di in range(NDI):
                        for cch in range(2):
                            mm = t.matmul(bank(par * 4 + 2 + cch)[:, 0:512],
                                          UDQ[par][:, di * 128:(di + 1) * 128],
                                          X1T[:, di * 1024 + cch * 512: di * 1024 + (cch + 1) * 512],
                                          start=(di == 0), stop=(di == NDI - 1))
                    mm.then_inc(s4pe, 1)
                t.wait_ge(s4stt, NHT * (th + 1))
                for dc in range(4):
                    gidx = th * 4 + dc
                    if gidx >= 1:
                        t.wait_ge(s4de, 8 * gidx)
                    for h2 in range(NHT):
                        i = gidx * NHT + h2
                        t.wait_ge(s4wq, i + 1)
                        for tt in range(8):
                            mm = t.matmul(bank(tt)[:, 0:512],
                                          HT[:, h2 * 1024 + tt * 128: h2 * 1024 + (tt + 1) * 128],
                                          WDQ[i % 2][:, 0:512],
                                          start=(h2 == 0), stop=(h2 == NHT - 1))
                        mm.then_inc(s4dn, 1)

        @blk.scalar
        def _(sc):
            zero_b = CVC[:, 3:4]
            if PH < 1:
                return
            for j in range(32):
                sc.wait_ge(s1pe, j + 1)
                if j >= 18:
                    sc.wait_ge(s1sp, 16 * (j - 17))
                grp = (j % 2) * 4
                dst = QT[:, (j % 16) * 2048:((j % 16) + 1) * 2048] if j < 16 else KST[j % 2]
                sc.copy(dst, bank(grp, 4)[:, 0:2048]).then_inc(s1sc, 1)
            for jb in range(8):
                sc.wait_ge(s1pb, jb + 1)
                for tt in range(8):
                    i = jb * 8 + tt
                    if i >= 2:
                        sc.wait_ge(s1spb, 16 * (i - 1))
                    sc.copy(VST[i % 2], bank(tt)[:, 0:512]).then_inc(s1scb, 1)
            if PH < 2:
                return
            sc.wait_ge(s1sp, 16 * 16)
            for m in range(NM):
                for u in range(8):
                    sc.wait_ge(s2st, 8 * m + u + 1)
                    if 8 * m + u >= 4:
                        sc.wait_ge(s2pv, 8 * m + u - 3)
                    sc.activation(PT[u % 4], bank(2 * (u % 2), 2)[:, 0:1024],
                                  AF.Exp, bias=CVC[:, 0:1],
                                  scale=ASCL).then_inc(s2sc, 1)
            if PH < 3:
                return
            for it in range(NT + 3):
                if 1 <= it <= NT:
                    t1 = it - 1
                    p1 = t1 % 4
                    sc.wait_ge(s3v1, t1 + 1)
                    sc.activation(stat(p1, S_SD), stat(p1, S_VE), AF.Sqrt)
                if 2 <= it <= NT + 1:
                    t2 = it - 2
                    p2 = t2 % 4
                    sc.wait_ge(s3v2, t2 + 1)
                    if t2 >= 2:
                        sc.wait_ge(s3spf, 16 * (t2 - 1))
                        sc.wait_ge(s3spb, 16 * (t2 - 1))
                    sc.activation(X1O[t2 % 2], ROU[t2 % 3], AF.Identity,
                                  bias=stat(p2, S_NM), scale=stat(p2, S_RS)).then_inc(s3no, 1)
                if it < NT:
                    sc.wait_ge(s3ra, it + 1)
                    sc.activation(SD3, ROU[it % 3], AF.Identity, bias=zero_b, scale=1.0,
                                  accum_out=stat(it % 4, S_SUM))
                    sc.activation(XRS[it % 2], ROU[it % 3], AF.Square,
                                  accum_out=stat(it % 4, S_SQ)).then_inc(s3sq, 1)
                elif it == NT:
                    sspace(sc).then_inc(s3sq, 1)
            if PH < 4:
                return
            for th in range(2):
                for it in range(NHT + 1):
                    if it < NHT:
                        i = th * NHT + it
                        sc.wait_ge(s4w[i % 2], 32 * (i // 2 + 1))
                        if i >= 2:
                            sc.wait_ge(s4pe, 2 * (i - 1))
                        sc.activation(GDQ[i % 2], GBF[i % 2], AF.Identity, bias=CVC[:, 1:2], scale=1.0)
                        sc.activation(UDQ[i % 2], UBF[i % 2], AF.Identity, bias=CVC[:, 1:2],
                                      scale=1.0).then_inc(s4dq, 2)
                    if it >= 1:
                        i1 = th * NHT + it - 1
                        ht1 = it - 1
                        sc.wait_ge(s4pe, 2 * i1 + 1)
                        sc.activation(GS[i1 % 2], bank((i1 % 2) * 4, 2)[:, 0:1024], AF.Silu,
                                      bias=zero_b, scale=SGC[:, ht1:ht1 + 1]).then_inc(s4sil, 1)
                for it in range(8 + 3):
                    g_ = th * 8 + it
                    if 1 <= it <= 8:
                        g1 = th * 8 + it - 1
                        p1 = g1 % 4
                        sc.wait_ge(s4v1, g1 + 1)
                        sc.activation(stat4(p1, S_SD), stat4(p1, S_VE), AF.Sqrt)
                    if 2 <= it <= 9:
                        g2 = th * 8 + it - 2
                        p2 = g2 % 4
                        sc.wait_ge(s4v2, g2 + 1)
                        if g2 >= 2:
                            sc.wait_ge(s4out, 16 * (g2 - 1))
                        sc.activation(XR4[g2 % 2], RO4[g2 % 3], AF.Identity,
                                      bias=stat4(p2, S_NM), scale=stat4(p2, S_RS)).then_inc(s4no, 1)
                    if it < 8:
                        sc.wait_ge(s4ra, g_ + 1)
                        sc.activation(SD4, RO4[g_ % 3], AF.Identity, bias=zero_b, scale=1.0,
                                      accum_out=stat4(g_ % 4, S_SUM))
                        sc.activation(DSB[:, it * 2048:(it + 1) * 2048], RO4[g_ % 3], AF.Square,
                                      accum_out=stat4(g_ % 4, S_SQ)).then_inc(s4sq, 1)
                    elif it == 8:
                        sspace(sc).then_inc(s4sq, 1)

        @blk.vector
        def _(v):
            if PH < 1:
                return
            for j in range(32):
                par = j % 2
                v.wait_ge(s1sc, j + 1)
                if j >= 2:
                    v.wait_ge(s1sw, 32 * (j - 1))
                blk_ap = QT[:, (j % 16) * 2048:((j % 16) + 1) * 2048] if j < 16 else KST[par]
                v.tensor_tensor(RT1[par], blk_ap[0:64, :], COS, AL.mult)
                v.tensor_tensor(TMG[par], blk_ap[0:64, :], GTB, AL.mult).then_inc(s1v1, 1)
                v.wait_ge(s1sw, 32 * (j + 1))
                v.tensor_tensor(blk_ap[0:64, :], RT1[par], RT2[par], AL.add).then_inc(s1v2, 1)
            if PH < 2:
                return
            for m in range(NM + 1):
                if m < NM:
                    v.wait_ge(s2pv, 8 * m + 8)
                    if m >= 2:
                        v.wait_ge(s2mu, m - 1)
                    v.tensor_copy(YRW[m % 2], bank(4 + m % 2)[:, 0:512]).then_inc(s2yr, 1)
                    if m >= 2:
                        v.wait_ge(s2g1, 16 * (m - 1))
                    v.tensor_copy(LSB[m % 2], bank(6 + m % 2)[0:1, 0:512]).then_inc(s2l, 1)
                    v.wait_ge(s2g1, 16 * (m + 1))
                    if m >= 2:
                        v.wait_ge(s2g2, 16 * (m - 1))
                    v.reciprocal(RLQ[m % 2], LQ[m % 2])
                    vspace(v).then_inc(s2r, 1)
                if 1 <= m:
                    mp = m - 1
                    hh, cc = divmod(mp, NCH)
                    v.wait_ge(s2bc, mp + 1)
                    v.tensor_tensor(YT[:, hh * 2048 + cc * 512: hh * 2048 + (cc + 1) * 512],
                                    YRW[mp % 2], bank(6 + mp % 2)[:, 0:512],
                                    AL.mult).then_inc(s2mu, 1)
            if PH < 3:
                return
            for it in range(NT + 3):
                if it < NT:
                    v.wait_ge(s3pe, it + 1)
                    v.wait_ge(s3x, 16 * (it + 1))
                    if it >= 3:
                        v.wait_ge(s3no, it - 2)
                    v.tensor_tensor(ROU[it % 3], bank((it % 2) * 4, 4)[:, 0:2048],
                                    XRS[it % 2], AL.add).then_inc(s3ra, 1)
                if 1 <= it <= NT:
                    t1 = it - 1
                    p1 = t1 % 4
                    v.wait_ge(s3sq, t1 + 1)
                    vstep(v, v.tensor_tensor(stat(p1, S_S2), stat(p1, S_SUM), stat(p1, S_SUM), AL.mult))
                    vstep(v, v.tensor_scalar(stat(p1, S_MN), stat(p1, S_SUM), 1.0 / D, None, AL.mult))
                    vstep(v, v.scalar_tensor_tensor(stat(p1, S_VN), stat(p1, S_S2), -1.0 / D,
                                                    stat(p1, S_SQ), AL.mult, AL.add))
                    vstep(v, v.tensor_scalar(stat(p1, S_VE), stat(p1, S_VN), 1.0 / D, EPS,
                                             AL.mult, AL.add))
                    vspace(v).then_inc(s3v1, 1)
                if 2 <= it <= NT + 1:
                    t2 = it - 2
                    p2 = t2 % 4
                    v.wait_ge(s3sq, min(t2 + 2, NT + 1))
                    vstep(v, v.reciprocal(stat(p2, S_RS), stat(p2, S_SD)))
                    vstep(v, v.scalar_tensor_tensor(stat(p2, S_NM), stat(p2, S_MN), -1.0,
                                                    stat(p2, S_RS), AL.mult, AL.mult))
                    vspace(v).then_inc(s3v2, 1)
            if PH < 4:
                return
            for th in range(2):
                for ht in range(NHT):
                    i = th * NHT + ht
                    v.wait_ge(s4sil, i + 1)
                    v.wait_ge(s4pe, 2 * i + 2)
                    v.scalar_tensor_tensor(HT[:, ht * 1024:(ht + 1) * 1024],
                                           bank((i % 2) * 4 + 2, 2)[:, 0:1024],
                                           SUC[:, ht:ht + 1], GS[i % 2],
                                           AL.mult, AL.mult).then_inc(s4stt, 1)
                for dc in range(4):
                    gidx = th * 4 + dc
                    for h2 in range(NHT):
                        i = gidx * NHT + h2
                        v.wait_ge(s4wd[i % 2], 16 * (i // 2 + 1))
                        if i >= 2:
                            v.wait_ge(s4dn, i - 1)
                        v.tensor_scalar(WDQ[i % 2], WDB[i % 2], -128.0, None,
                                        AL.add).then_inc(s4wq, 1)
                    v.wait_ge(s4dn, (gidx + 1) * NHT)
                    for tt in range(8):
                        v.tensor_tensor(DSB[:, tt * 2048 + dc * 512: tt * 2048 + (dc + 1) * 512],
                                        bank(tt)[:, 0:512], SDB[:, dc * 512:(dc + 1) * 512],
                                        AL.mult).then_inc(s4de, 1)
                for it in range(8 + 3):
                    g_ = th * 8 + it
                    if 2 <= it <= 9:
                        g2 = th * 8 + it - 2
                        p2 = g2 % 4
                        v.wait_ge(s4sq, th * 9 + min(it, 9))
                        vstep(v, v.reciprocal(stat4(p2, S_RS), stat4(p2, S_SD)))
                        vstep(v, v.scalar_tensor_tensor(stat4(p2, S_NM), stat4(p2, S_MN), -1.0,
                                                        stat4(p2, S_RS), AL.mult, AL.mult))
                        vspace(v).then_inc(s4v2, 1)
                    if 1 <= it <= 8:
                        g1 = th * 8 + it - 1
                        p1 = g1 % 4
                        v.wait_ge(s4sq, th * 9 + it)
                        vstep(v, v.tensor_tensor(stat4(p1, S_S2), stat4(p1, S_SUM), stat4(p1, S_SUM), AL.mult))
                        vstep(v, v.tensor_scalar(stat4(p1, S_MN), stat4(p1, S_SUM), 1.0 / D, None, AL.mult))
                        vstep(v, v.scalar_tensor_tensor(stat4(p1, S_VN), stat4(p1, S_S2), -1.0 / D,
                                                        stat4(p1, S_SQ), AL.mult, AL.add))
                        vstep(v, v.tensor_scalar(stat4(p1, S_VE), stat4(p1, S_VN), 1.0 / D, EPS,
                                                 AL.mult, AL.add))
                        vspace(v).then_inc(s4v1, 1)
                    if it < 8:
                        v.wait_ge(s4x, 16 * (g_ + 1))
                        v.wait_ge(s4de, 32 * (th + 1))
                        if g_ >= 3:
                            v.wait_ge(s4no, g_ - 2)
                        v.tensor_tensor(RO4[g_ % 3], DSB[:, it * 2048:(it + 1) * 2048],
                                        XR4[g_ % 2], AL.add).then_inc(s4ra, 1)

    return nc


def _host_prep(inputs):
    x = np.ascontiguousarray(np.asarray(inputs["L0_attn_x"], dtype=np.float32))
    wq = np.asarray(inputs["L0_attn_wq"], dtype=np.float32)
    wk = np.asarray(inputs["L0_attn_wk"], dtype=np.float32)
    wv = np.asarray(inputs["L0_attn_wv"], dtype=np.float32)
    wo = np.asarray(inputs["L0_attn_wo"], dtype=np.float32)
    wg = np.asarray(inputs["L0_ffn_wq_gate"], dtype=np.int32)
    wu = np.asarray(inputs["L0_ffn_wq_up"], dtype=np.int32)
    wd = np.asarray(inputs["L0_ffn_wq_down"], dtype=np.int32)
    sg = np.asarray(inputs["L0_ffn_s_gate"], dtype=np.float32)
    su = np.asarray(inputs["L0_ffn_s_up"], dtype=np.float32)
    sd = np.asarray(inputs["L0_ffn_s_down"], dtype=np.float32)

    def wblk(w):
        return np.ascontiguousarray(
            w.T.reshape(NDI, 128, NT, 128).transpose(2, 1, 0, 3).reshape(NT, 128, D))

    shared = {
        "wq_c": wblk(wq), "wk_c": wblk(wk),
        "wv_c": np.ascontiguousarray(
            wv.T.reshape(NDI, 128, 4, 512).transpose(2, 1, 0, 3).reshape(4, 128, NDI * 512)),
        "wo_T": np.ascontiguousarray(wo.T),
        "wg_c": np.ascontiguousarray(
            wg.T.reshape(NDI, 128, NHT, 128).transpose(2, 1, 0, 3).reshape(NHT, 128, 2048)),
        "wu_c": np.ascontiguousarray(
            wu.T.reshape(NDI, 128, NHT, 128).transpose(2, 1, 0, 3).reshape(NHT, 128, 2048)),
        "wd_c": np.ascontiguousarray(
            wd.T.reshape(NHT, 128, 4, 512).transpose(0, 2, 1, 3)),
        "sg_c": np.ascontiguousarray(sg.reshape(NHT, 128).T),
        "su_c": np.ascontiguousarray(su.reshape(NHT, 128).T),
        "sd_b": np.ascontiguousarray(np.tile(sd.reshape(1, 2048), (128, 1))),
    }
    in_maps = []
    for b in range(8):
        m = dict(shared)
        m["x"] = np.ascontiguousarray(x[b])
        m["xT"] = np.ascontiguousarray(x[b].T)
        in_maps.append(m)
    return in_maps


def kernel(**inputs):
    in_maps = _host_prep(inputs)
    nc = build_program()
    res = run_bass_kernel_spmd(nc, in_maps, core_ids=list(range(8)))
    return np.stack([r["out"] for r in res.results])


# version 03
# speedup vs baseline: 1.0421x; 1.0421x over previous
import math
import numpy as np
import ml_dtypes
import concourse.bass as bass
import concourse.mybir as mybir
from concourse.bass_utils import run_bass_kernel_spmd

F32 = mybir.dt.float32
BF16 = mybir.dt.bfloat16
I32 = mybir.dt.int32
AF = mybir.ActivationFunctionType
AL = mybir.AluOpType
AX = mybir.AxisListType
bfdt = ml_dtypes.bfloat16

T = 2048
D = 2048
NH = 16
HD = 128
HID = 5632
NT = 16
NDI = 16
NHT = 44
QC = 512
NCH = 4
NM = NH * NCH
C_SHIFT = 24.0
ASCL = 1.0 / math.sqrt(HD)
EPS = 1e-5

DEBUG = False
PH = 4

ARENA_F32_COLS = 53040

OFF_SG    = 202704
OFF_SU    = 202896
OFF_CVEC  = 203088
OFF_ONESR = 203120
OFF_ONESB = 203632
OFF_SDB   = 203888
OFF_SPC   = 199168
OFF_SPC2  = 200192

OFF_XT    = 0
OFF_QT    = 65536
OFF_COS   = 131072
OFF_SING  = 135168
OFF_WBUF  = 139264
OFF_KSTG  = 147456
OFF_RT1   = 155648
OFF_TMPG  = 163840
OFF_RT2   = 172032
OFF_VW    = 180224
OFF_VSTG  = 196608

OFF_YT    = 0
OFF_KVK   = 131072
OFF_KVV   = 139264
OFF_PT    = 147456
OFF_LSB   = 155648
OFF_LQ    = 160000
OFF_RLQ   = 160064
OFF_RLROW = 160128
OFF_YRAW  = 164224

OFF_XRES  = 131072
OFF_ROUT  = 147456
OFF_STAT  = 172032
OFF_X1O   = 172608

OFF_X1T   = 0
OFF_HT    = 32768
OFF_GBUF  = 122880
OFF_UBUF  = 131072
OFF_GDQ   = 139264
OFF_UDQ   = 147456
OFF_GS    = 155648
OFF_DSB   = 122880
OFF_WDB   = 188416
OFF_WDQ   = 190464
OFF_STAT4 = 192512
OFF_XRES4 = 0
OFF_ROUT4 = 32768
OFF_SUMD4 = 57344
OFF_SUMD3 = 188992

S_SUM, S_SQ, S_S2, S_VN, S_VE, S_MN, S_SD, S_RS, S_NM = range(9)


def build_program():
    nc = bass.Bass("TRN2", target_bir_lowering=False)

    x_h   = nc.dram_tensor("x",   [T, D], F32, kind="ExternalInput").ap()
    xT_h  = nc.dram_tensor("xT",  [D, T], F32, kind="ExternalInput").ap()
    wq_h  = nc.dram_tensor("wq_c", [NT, 128, D], F32, kind="ExternalInput").ap()
    wk_h  = nc.dram_tensor("wk_c", [NT, 128, D], F32, kind="ExternalInput").ap()
    wv_h  = nc.dram_tensor("wv_c", [4, 128, NDI * 512], F32, kind="ExternalInput").ap()
    wo_h  = nc.dram_tensor("wo_T", [D, D], F32, kind="ExternalInput").ap()
    wg_h  = nc.dram_tensor("wg_c", [NHT, 128, 2048], I32, kind="ExternalInput").ap()
    wu_h  = nc.dram_tensor("wu_c", [NHT, 128, 2048], I32, kind="ExternalInput").ap()
    wd_h  = nc.dram_tensor("wd_c", [NHT, 4, 128, 512], I32, kind="ExternalInput").ap()
    sg_h  = nc.dram_tensor("sg_c", [128, NHT], F32, kind="ExternalInput").ap()
    su_h  = nc.dram_tensor("su_c", [128, NHT], F32, kind="ExternalInput").ap()
    sd_h  = nc.dram_tensor("sd_b", [128, 2048], F32, kind="ExternalInput").ap()
    out_h = nc.dram_tensor("out", [T, D], F32, kind="ExternalOutput").ap()

    scr_kind = "ExternalOutput" if DEBUG else "Internal"
    kt_sp = nc.dram_tensor("kt_sp", [NH, 128, T], BF16, kind=scr_kind).ap()
    v_sp  = nc.dram_tensor("v_sp",  [NT, 128, D], BF16, kind=scr_kind).ap()
    x1f   = nc.dram_tensor("x1f", [T, D], F32, kind=scr_kind).ap()
    x1b   = nc.dram_tensor("x1b", [T, D], BF16, kind=scr_kind).ap()
    qt_dbg = nc.dram_tensor("qt_dbg", [NH, 128, T], BF16, kind="ExternalOutput").ap() if DEBUG else None
    yt_dbg = nc.dram_tensor("yt_dbg", [NH, 128, T], BF16, kind="ExternalOutput").ap() if DEBUG else None

    inv_freq = (1.0 / 10000.0 ** (np.arange(0, HD, 2, dtype=np.float32) / HD)).astype(np.float32)
    ang = np.arange(T, dtype=np.float32)[:, None] * inv_freq[None, :]
    cosT = np.cos(ang).T.astype(bfdt)
    sinT = np.sin(ang).T.astype(np.float32)
    gtab = np.concatenate([sinT[32:64], -sinT[0:32]], axis=0).astype(bfdt)
    cos_h  = nc.inline_tensor(np.ascontiguousarray(cosT), name="cosT")
    g_h    = nc.inline_tensor(np.ascontiguousarray(gtab), name="gtab")
    ones_h = nc.inline_tensor(np.ones((128, 128), dtype=bfdt), name="onesb")
    onesr_h = nc.inline_tensor(np.ones((1, 128), dtype=np.float32), name="onesr")
    cvec = np.zeros((128, 4), dtype=np.float32)
    cvec[:, 0] = -C_SHIFT
    cvec[:, 1] = -128.0
    cvec[:, 2] = EPS
    cvec_h = nc.inline_tensor(cvec, name="cvec")
    zeros_h = nc.inline_tensor(np.zeros((128, 320), dtype=np.float32), name="zeros")

    AB = nc.alloc_sbuf_tensor("AB", [128, ARENA_F32_COLS], F32)
    PS = nc.alloc_psum_tensor("PS", [128, 4096], F32)
    ps_ap = PS.ap()

    def f32(off_b, p, cols):
        return AB.ap()[0:p, off_b // 4: off_b // 4 + cols]

    def bf16(off_b, p, cols):
        return AB.ap()[0:p, off_b // 4: off_b // 4 + cols // 2].bitcast(BF16)

    def bank(b, n=1):
        return ps_ap[:, b * 512:(b + n) * 512]

    XT  = bf16(OFF_XT, 128, NDI * 2048)
    QT  = bf16(OFF_QT, 128, NH * 2048)
    COS = bf16(OFF_COS, 64, 2048)
    GTB = bf16(OFF_SING, 64, 2048)
    WB  = [bf16(OFF_WBUF + i * 4096, 128, 2048) for i in range(2)]
    KST = [bf16(OFF_KSTG + i * 4096, 128, 2048) for i in range(2)]
    RT1 = [bf16(OFF_RT1 + i * 4096, 64, 2048) for i in range(2)]
    TMG = [bf16(OFF_TMPG + i * 4096, 64, 2048) for i in range(2)]
    RT2 = [bf16(OFF_RT2 + i * 4096, 64, 2048) for i in range(2)]
    VW  = [bf16(OFF_VW, 128, 8192), bf16(OFF_WBUF, 128, 8192)]
    VST = [bf16(OFF_VSTG + i * 1024, 128, 512) for i in range(2)]

    YT  = bf16(OFF_YT, 128, NH * 2048)
    KVK = [bf16(OFF_KVK + i * 4096, 128, 2048) for i in range(2)]
    KVV = [bf16(OFF_KVV + i * 4096, 128, 2048) for i in range(2)]
    PT  = [bf16(OFF_PT + i * 2048, 128, 1024) for i in range(4)]
    LSB = [f32(OFF_LSB + i * 2048, 1, 512) for i in range(2)]
    LQ  = [f32(OFF_LQ + i * 16, 128, 4) for i in range(2)]
    RLQ = [f32(OFF_RLQ + i * 16, 128, 4) for i in range(2)]
    RLR = [f32(OFF_RLROW + i * 2048, 1, 512) for i in range(2)]
    YRW = [f32(OFF_YRAW + i * 2048, 128, 512) for i in range(2)]
    WOT = QT

    XRS = [f32(OFF_XRES + i * 8192, 128, 2048) for i in range(2)]
    ROU = [f32(OFF_ROUT + i * 8192, 128, 2048) for i in range(3)]
    X1O = [f32(OFF_X1O + i * 8192, 128, 2048) for i in range(3)]

    X1T = bf16(OFF_X1T, 128, NDI * 1024)
    HT  = bf16(OFF_HT, 128, NHT * 1024)
    GBF = [bf16(OFF_GBUF + i * 4096, 128, 2048) for i in range(2)]
    UBF = [bf16(OFF_UBUF + i * 4096, 128, 2048) for i in range(2)]
    GDQ = [bf16(OFF_GDQ + i * 4096, 128, 2048) for i in range(2)]
    UDQ = [bf16(OFF_UDQ + i * 4096, 128, 2048) for i in range(2)]
    GS  = [f32(OFF_GS + i * 4096, 128, 1024) for i in range(2)]
    DSB = f32(OFF_DSB, 128, 8 * 2048)
    WDB = [bf16(OFF_WDB + i * 1024, 128, 512) for i in range(2)]
    WDQ = [bf16(OFF_WDQ + i * 1024, 128, 512) for i in range(2)]
    RO4 = [f32(OFF_ROUT4 + i * 8192, 128, 2048) for i in range(3)]
    XR4 = [f32(OFF_XRES4 + i * 8192, 128, 2048) for i in range(4)]
    SD3 = f32(OFF_SUMD3, 128, 2048)
    SD4 = f32(OFF_SUMD4, 128, 2048)

    SGC = f32(OFF_SG, 128, NHT)
    SUC = f32(OFF_SU, 128, NHT)
    CVC = f32(OFF_CVEC, 128, 4)
    ONR = f32(OFF_ONESR, 1, 128)
    ONB = bf16(OFF_ONESB, 128, 128)
    ONRB = bf16(OFF_ONESB, 1, 128)
    RLRB = [bf16(OFF_RLROW + i * 2048, 1, 512) for i in range(2)]
    SDB = f32(OFF_SDB, 128, 2048)
    SPC = f32(OFF_SPC, 128, 256)
    SPC2 = f32(OFF_SPC2, 128, 64)

    vf_sem = nc.alloc_semaphore("vfence")
    vf_n = [0]
    def vstep(v, instr, spacers=2):
        instr.then_inc(vf_sem, 1)
        vf_n[0] += 1
        for _ in range(spacers):
            vspace(v)
        v.wait_ge(vf_sem, vf_n[0])

    vsp_n = [0]
    def vspace(v):
        n = vsp_n[0] % 8
        vsp_n[0] += 1
        return v.tensor_copy(SPC[:, 32 * n:32 * n + 32], SDB[:, 0:32])

    ssp_n = [0]
    def sspace(sc):
        n = ssp_n[0] % 2
        ssp_n[0] += 1
        return sc.copy(SPC2[:, 32 * n:32 * n + 32], SDB[:, 32:64])

    def stat(par, idx):
        return f32(OFF_STAT + par * 144, 128, 36)[:, idx:idx + 1]

    def stat4(par, idx):
        return f32(OFF_STAT4 + par * 144, 128, 36)[:, idx:idx + 1]

    def sem(name):
        return nc.alloc_semaphore(name)

    s0c = sem("s0c"); s0x = sem("s0x"); s0sd = sem("s0sd"); s0d = sem("s0d")
    s1w = [sem("s1wA"), sem("s1wB")]; s1pe = sem("s1pe"); s1sc = sem("s1sc")
    s1v1 = sem("s1v1"); s1sw = sem("s1sw"); s1v2 = sem("s1v2"); s1sp = sem("s1sp")
    s1vw = [sem("s1vwA"), sem("s1vwB")]; s1pb = sem("s1pb"); s1scb = sem("s1scb"); s1spb = sem("s1spb")
    s2kv = [sem("s2kvA"), sem("s2kvB")]; s2st = sem("s2st"); s2pv = sem("s2pv"); s2sc = sem("s2sc")
    s2yr = sem("s2yr"); s2l = sem("s2l"); s2r = sem("s2r"); s2g1 = sem("s2g1")
    s2g2 = sem("s2g2"); s2bc = sem("s2bc"); s2mu = sem("s2mu")
    s2wo = sem("s2wo")
    s3x = sem("s3x"); s3pe = sem("s3pe"); s3ra = sem("s3ra"); s3sq = sem("s3sq")
    s3v1 = sem("s3v1"); s3v2 = sem("s3v2"); s3no = sem("s3no")
    s3spf = sem("s3spf"); s3spb = sem("s3spb")
    s4xt = [sem("s4xtA"), sem("s4xtB")]; s4w = [sem("s4wA"), sem("s4wB")]; s4dq = sem("s4dq"); s4pe = sem("s4pe")
    s4sil = sem("s4sil"); s4stt = sem("s4stt"); s4wd = [sem("s4wdA"), sem("s4wdB")]; s4wq = sem("s4wq")
    s4dn = sem("s4dn"); s4de = sem("s4de"); s4x = sem("s4x"); s4ra = sem("s4ra")
    s4sq = sem("s4sq"); s4v1 = sem("s4v1"); s4v2 = sem("s4v2")
    s4no = sem("s4no"); s4out = sem("s4out")

    with nc.Block() as blk:

        @blk.sync
        def _(s):
            s.dma_start(out=COS, in_=cos_h.ap()).then_inc(s0c, 16)
            s.dma_start(out=GTB, in_=g_h.ap()).then_inc(s0c, 16)
            s.dma_start(out=ONB, in_=ones_h.ap()).then_inc(s0c, 16)
            s.dma_start(out=ONR, in_=onesr_h.ap()).then_inc(s0c, 16)
            s.dma_start(out=CVC, in_=cvec_h.ap()).then_inc(s0c, 16)
            s.dma_start(out=SGC, in_=sg_h).then_inc(s0c, 16)
            s.dma_start(out=SUC, in_=su_h).then_inc(s0c, 16)
            s.dma_start(out=SDB, in_=sd_h).then_inc(s0c, 16)
            s.dma_start(out=f32(OFF_SPC, 128, 320), in_=zeros_h.ap()).then_inc(s0c, 16)
            if PH < 1:
                s.wait_ge(s0x, 16 * NDI)
                s.wait_ge(s0sd, 1)
                return
            for j in range(32):
                par = j % 2
                s.wait_ge(s1v1, j + 1)
                if j >= 2:
                    s.wait_ge(s1v2, j - 1)
                s.dma_start(out=RT2[par][0:32, :], in_=TMG[par][32:64, :]).then_inc(s1sw, 16)
                s.dma_start(out=RT2[par][32:64, :], in_=TMG[par][0:32, :]).then_inc(s1sw, 16)
                if j >= 16:
                    ot = j - 16
                    s.wait_ge(s1v2, j + 1)
                    if ot >= 1:
                        s.wait_ge(s1sp, 16 * ot)
                    s.dma_start(out=kt_sp[ot], in_=KST[par]).then_inc(s1sp, 16)
            for i in range(64):
                jb, tt = divmod(i, 8)
                dc, th = divmod(jb, 2)
                s.wait_ge(s1scb, i + 1)
                if i >= 1:
                    s.wait_ge(s1spb, 16 * i)
                s.dma_start(out=v_sp[th * 8 + tt][:, dc * 512:(dc + 1) * 512],
                            in_=VST[i % 2]).then_inc(s1spb, 16)
            if DEBUG:
                s.wait_ge(s1v2, 32)
                for h in range(NH):
                    s.dma_start(out=qt_dbg[h], in_=QT[:, h * 2048:(h + 1) * 2048]).then_inc(s0d, 16)
            if PH < 2:
                s.wait_ge(s1sp, 16 * 16)
                s.wait_ge(s1spb, 16 * 64)
                if DEBUG:
                    s.wait_ge(s0d, 16 * NH)
                return
            s.wait_ge(s1sp, 16 * 16)
            s.wait_ge(s1spb, 16 * 64)
            for h in (0, 1):
                s.dma_start(out=KVK[h % 2], in_=kt_sp[h]).then_inc(s2kv[h % 2], 16)
                s.dma_start(out=KVV[h % 2],
                            in_=v_sp[:, :, h * 128:(h + 1) * 128].rearrange("k p b -> p k b")).then_inc(s2kv[h % 2], 16)
            for m in range(NM):
                h, c = divmod(m, NCH)
                s.wait_ge(s2l, m + 1)
                if m >= 1:
                    s.wait_ge(s2g1, 16 * m)
                if m >= 2:
                    s.wait_ge(s2r, m - 1)
                s.dma_start(out=LQ[m % 2], in_=LSB[m % 2]).then_inc(s2g1, 16)
                if c == NCH - 1 and h + 2 < NH:
                    hh = h + 2
                    s.wait_ge(s2pv, 32 * (hh - 1))
                    s.wait_ge(s2kv[hh % 2], 32 * (hh // 2))
                    s.dma_start(out=KVK[hh % 2], in_=kt_sp[hh]).then_inc(s2kv[hh % 2], 16)
                    s.dma_start(out=KVV[hh % 2],
                                in_=v_sp[:, :, hh * 128:(hh + 1) * 128].rearrange("k p b -> p k b")).then_inc(s2kv[hh % 2], 16)
            if DEBUG:
                s.wait_ge(s2mu, NM)
                for h in range(NH):
                    s.dma_start(out=yt_dbg[h], in_=YT[:, h * 2048:(h + 1) * 2048]).then_inc(s0d, 16)
            if PH < 3:
                s.wait_ge(s2mu, NM)
                if DEBUG:
                    s.wait_ge(s0d, 16 * 2 * NH)
                return
            s.wait_ge(s2mu, NM)
            for it in range(NT + 4):
                if it < NT:
                    if it >= 2:
                        s.wait_ge(s3ra, it - 1)
                        s.wait_ge(s3sq, it - 1)
                    if it >= 1:
                        s.wait_ge(s3x, 16 * it)
                    s.dma_start(out=XRS[it % 2], in_=x_h[it * 128:(it + 1) * 128, :]).then_inc(s3x, 16)
                sp = it - 3
                if 0 <= sp < NT:
                    s.wait_ge(s3no, sp + 1)
                    if sp >= 1:
                        s.wait_ge(s3spf, 16 * sp)
                    s.dma_start(out=x1f[sp * 128:(sp + 1) * 128, :], in_=X1O[sp % 3]).then_inc(s3spf, 16)
            if PH < 4:
                s.wait_ge(s3spf, 16 * NT)
                s.wait_ge(s3spb, 16 * NT)
                if DEBUG:
                    s.wait_ge(s0d, 16 * 2 * NH)
                return
            for th in range(2):
                if th == 0:
                    s.wait_ge(s3pe, NT)
                    s.wait_ge(s3spb, 16 * 8)
                    if DEBUG:
                        s.wait_ge(s0d, 16 * 2 * NH)
                else:
                    s.wait_ge(s4out, 16 * 8)
                for dt in range(NDI):
                    gi = th * NDI + dt
                    if th == 1 and dt == 0:
                        s.wait_ge(s3spb, 16 * NT)
                    if gi >= 2:
                        s.wait_ge(s4xt[gi % 2], 16 * (gi // 2))
                    s.dma_start_transpose(
                        X1T[:, dt * 1024:(dt + 1) * 1024],
                        x1b[th * 1024:(th + 1) * 1024, dt * 128:(dt + 1) * 128],
                    ).then_inc(s4xt[gi % 2], 16)
                if th == 0:
                    s.wait_ge(s3spf, 16 * NT)
                for it in range(8 + 2):
                    if it >= 2:
                        g2 = th * 8 + it - 2
                        s.wait_ge(s4no, g2 + 1)
                        if g2 >= 1:
                            s.wait_ge(s4out, 16 * g2)
                        s.dma_start(out=out_h[g2 * 128:(g2 + 1) * 128, :],
                                    in_=XR4[g2 % 4]).then_inc(s4out, 16)
                    if it < 8:
                        g = th * 8 + it
                        if it < 4:
                            s.wait_ge(s4pe, 2 * NHT * (th + 1))
                        if it >= 4:
                            s.wait_ge(s4ra, g - 3)
                            s.wait_ge(s4out, 16 * (g - 3))
                        if g >= 1:
                            s.wait_ge(s4x, 16 * g)
                        s.dma_start(out=XR4[g % 4], in_=x1f[g * 128:(g + 1) * 128, :]).then_inc(s4x, 16)
            s.wait_ge(s4out, 16 * NT)
            if DEBUG:
                s.wait_ge(s0d, 16 * (2 * NH))

        @blk.gpsimd
        def _(g):
            for dt in range(NDI):
                g.dma_start(out=XT[:, dt * 2048:(dt + 1) * 2048],
                            in_=xT_h[dt * 128:(dt + 1) * 128, :]).then_inc(s0x, 16)
            if PH < 1:
                return
            for j in range(32):
                src = wq_h[j] if j < 16 else wk_h[j - 16]
                if j >= 2:
                    g.wait_ge(s1pe, j - 1)
                    g.wait_ge(s1w[j % 2], 16 * (j // 2))
                g.dma_start(out=WB[j % 2], in_=src).then_inc(s1w[j % 2], 16)
            for dc in range(4):
                if dc == 1:
                    g.wait_ge(s1pe, 32)
                    g.wait_ge(s1sp, 16 * 16)
                if dc >= 2:
                    g.wait_ge(s1pb, 2 * (dc - 1))
                    g.wait_ge(s1vw[dc % 2], 16 * (dc // 2))
                g.dma_start(out=VW[dc % 2], in_=wv_h[dc]).then_inc(s1vw[dc % 2], 16)
            if PH < 2:
                return
            for m in range(NM):
                h, c = divmod(m, NCH)
                g.wait_ge(s2r, m + 1)
                if m >= 1:
                    g.wait_ge(s2g2, 16 * m)
                if m >= 2:
                    g.wait_ge(s2bc, m - 1)
                g.dma_start(out=RLRB[m % 2], in_=RLQ[m % 2]).then_inc(s2g2, 16)
                if c == NCH - 1:
                    g.wait_ge(s2pv, 32 * (h + 1))
                    if DEBUG and h == 0:
                        g.wait_ge(s0d, 16 * NH)
                    if h >= 1:
                        g.wait_ge(s2wo, 16 * h)
                    g.dma_start(out=WOT[:, h * 2048:(h + 1) * 2048],
                                in_=wo_h[h * 128:(h + 1) * 128, :]).then_inc(s2wo, 16)
            if PH < 3:
                return
            for tt in range(NT):
                g.wait_ge(s3no, tt + 1)
                if tt >= 1:
                    g.wait_ge(s3spb, 16 * tt)
                g.dma_start(out=x1b[tt * 128:(tt + 1) * 128, :], in_=X1O[tt % 3]).then_inc(s3spb, 16)
            if PH < 4:
                return
            for th in range(2):
                for ht in range(NHT):
                    i = th * NHT + ht
                    if th == 1 and ht == 0:
                        g.wait_ge(s4ra, 8)
                    if i >= 2:
                        g.wait_ge(s4dq, 2 * (i - 1))
                        g.wait_ge(s4w[i % 2], 32 * (i // 2))
                    g.dma_start(out=GBF[i % 2], in_=wg_h[ht]).then_inc(s4w[i % 2], 16)
                    g.dma_start(out=UBF[i % 2], in_=wu_h[ht]).then_inc(s4w[i % 2], 16)
                for dc in range(4):
                    for h2 in range(NHT):
                        i = (th * 4 + dc) * NHT + h2
                        if i >= 2:
                            g.wait_ge(s4wq, i - 1)
                            g.wait_ge(s4wd[i % 2], 16 * (i // 2))
                        g.dma_start(out=WDB[i % 2], in_=wd_h[h2, dc]).then_inc(s4wd[i % 2], 16)

        @blk.tensor
        def _(t):
            t.wait_ge(s0c, 16 * 9)
            if PH < 1:
                return
            t.wait_ge(s0x, 16 * NDI)
            for j in range(32):
                t.wait_ge(s1w[j % 2], 16 * (j // 2 + 1))
                if j >= 2:
                    t.wait_ge(s1sc, j - 1)
                grp = (j % 2) * 4
                for di in range(NDI):
                    for cch in range(4):
                        mm = t.matmul(bank(grp + cch)[:, 0:512],
                                      WB[j % 2][:, di * 128:(di + 1) * 128],
                                      XT[:, di * 2048 + cch * 512: di * 2048 + (cch + 1) * 512],
                                      start=(di == 0), stop=(di == NDI - 1))
                mm.then_inc(s1pe, 1)
            t.wait_ge(s1sc, 32)
            for jb in range(8):
                dc, th = divmod(jb, 2)
                if th == 0:
                    t.wait_ge(s1vw[dc % 2], 16 * (dc // 2 + 1))
                if jb >= 1:
                    t.wait_ge(s1scb, jb * 8)
                for di in range(NDI):
                    for tt in range(8):
                        tg = th * 8 + tt
                        mm = t.matmul(bank(tt)[:, 0:512],
                                      XT[:, di * 2048 + tg * 128: di * 2048 + (tg + 1) * 128],
                                      VW[dc % 2][:, di * 512:(di + 1) * 512],
                                      start=(di == 0), stop=(di == NDI - 1))
                mm.then_inc(s1pb, 1)
            if PH < 2:
                return
            for m in range(NM):
                h, c = divmod(m, NCH)
                if c == 0:
                    t.wait_ge(s2kv[h % 2], 32 * (h // 2 + 1))
                if m >= 2:
                    t.wait_ge(s2yr, m - 1)
                    t.wait_ge(s2mu, m - 1)
                qs = h * 2048 + c * 512
                for u in range(8):
                    k0, k1 = 2 * u, 2 * u + 1
                    slot = u % 2
                    if u >= 2:
                        t.wait_ge(s2sc, 8 * m + u - 1)
                    elif m >= 1:
                        t.wait_ge(s2sc, 8 * (m - 1) + 7 + (1 if u == 1 else 0))
                    t.matmul(bank(2 * slot)[:, 0:512],
                             KVK[h % 2][:, k0 * 128:(k0 + 1) * 128],
                             QT[:, qs:qs + 512], start=True, stop=True)
                    t.matmul(bank(2 * slot + 1)[:, 0:512],
                             KVK[h % 2][:, k1 * 128:(k1 + 1) * 128],
                             QT[:, qs:qs + 512], start=True,
                             stop=True).then_inc(s2st, 1)
                    if u >= 1:
                        up = u - 1
                        t.wait_ge(s2sc, 8 * m + up + 1)
                        for kk in (2 * up, 2 * up + 1):
                            t.matmul(bank(4 + m % 2)[:, 0:512],
                                     KVV[h % 2][:, kk * 128:(kk + 1) * 128],
                                     PT[up % 4][:, (kk % 2) * 512:(kk % 2 + 1) * 512],
                                     start=(kk == 0), stop=False)
                            mm = t.matmul(bank(6 + m % 2)[0:1, 0:512],
                                          ONB[:, 0:1],
                                          PT[up % 4][:, (kk % 2) * 512:(kk % 2 + 1) * 512],
                                          start=(kk == 0), stop=False)
                        mm.then_inc(s2pv, 1)
                t.wait_ge(s2sc, 8 * m + 8)
                for kk in (14, 15):
                    t.matmul(bank(4 + m % 2)[:, 0:512],
                             KVV[h % 2][:, kk * 128:(kk + 1) * 128],
                             PT[3][:, (kk % 2) * 512:(kk % 2 + 1) * 512],
                             start=False, stop=(kk == 15))
                    mm = t.matmul(bank(6 + m % 2)[0:1, 0:512],
                                  ONB[:, 0:1],
                                  PT[3][:, (kk % 2) * 512:(kk % 2 + 1) * 512],
                                  start=False, stop=(kk == 15))
                mm.then_inc(s2pv, 1)
                if m >= 1:
                    t.wait_ge(s2g2, 16 * m)
                    t.matmul(bank(6 + (m - 1) % 2)[:, 0:512], ONRB, RLRB[(m - 1) % 2],
                             start=True, stop=True).then_inc(s2bc, 1)
            t.wait_ge(s2g2, 16 * NM)
            t.matmul(bank(6 + (NM - 1) % 2)[:, 0:512], ONRB, RLRB[(NM - 1) % 2],
                     start=True, stop=True).then_inc(s2bc, 1)
            if PH < 3:
                return
            t.wait_ge(s2mu, NM)
            for tt in range(NT):
                if tt >= 2:
                    t.wait_ge(s3ra, tt - 1)
                    t.wait_ge(s3sq, tt - 1)
                for y in range(NH):
                    t.wait_ge(s2wo, 16 * (y + 1))
                    for cch in range(4):
                        mm = t.matmul(bank((tt % 2) * 4 + cch)[:, 0:512],
                                      YT[:, y * 2048 + tt * 128: y * 2048 + (tt + 1) * 128],
                                      WOT[:, y * 2048 + cch * 512: y * 2048 + (cch + 1) * 512],
                                      start=(y == 0), stop=(y == NH - 1))
                mm.then_inc(s3pe, 1)
            if PH < 4:
                return
            t.wait_ge(s3ra, NT)
            for th in range(2):
                t.wait_ge(s4xt[0], 16 * (NDI // 2) * (th + 1))
                t.wait_ge(s4xt[1], 16 * (NDI // 2) * (th + 1))
                if th == 0:
                    t.wait_ge(s3spf, 16 * NT)
                    t.wait_ge(s3spb, 16 * NT)
                for ht in range(NHT):
                    i = th * NHT + ht
                    t.wait_ge(s4dq, 2 * (i + 1))
                    if i >= 2:
                        t.wait_ge(s4stt, i - 1)
                    par = i % 2
                    for di in range(NDI):
                        for cch in range(2):
                            mm = t.matmul(bank(par * 4 + cch)[:, 0:512],
                                          GDQ[par][:, di * 128:(di + 1) * 128],
                                          X1T[:, di * 1024 + cch * 512: di * 1024 + (cch + 1) * 512],
                                          start=(di == 0), stop=(di == NDI - 1))
                    mm.then_inc(s4pe, 1)
                    for di in range(NDI):
                        for cch in range(2):
                            mm = t.matmul(bank(par * 4 + 2 + cch)[:, 0:512],
                                          UDQ[par][:, di * 128:(di + 1) * 128],
                                          X1T[:, di * 1024 + cch * 512: di * 1024 + (cch + 1) * 512],
                                          start=(di == 0), stop=(di == NDI - 1))
                    mm.then_inc(s4pe, 1)
                t.wait_ge(s4stt, NHT * (th + 1))
                for dc in range(4):
                    gidx = th * 4 + dc
                    if gidx >= 1:
                        t.wait_ge(s4de, 8 * gidx)
                    for h2 in range(NHT):
                        i = gidx * NHT + h2
                        t.wait_ge(s4wq, i + 1)
                        for tt in range(8):
                            mm = t.matmul(bank(tt)[:, 0:512],
                                          HT[:, h2 * 1024 + tt * 128: h2 * 1024 + (tt + 1) * 128],
                                          WDQ[i % 2][:, 0:512],
                                          start=(h2 == 0), stop=(h2 == NHT - 1))
                        mm.then_inc(s4dn, 1)

        @blk.scalar
        def _(sc):
            zero_b = CVC[:, 3:4]
            if PH < 1:
                return
            for j in range(32):
                sc.wait_ge(s1pe, j + 1)
                if j >= 18:
                    sc.wait_ge(s1sp, 16 * (j - 17))
                grp = (j % 2) * 4
                dst = QT[:, (j % 16) * 2048:((j % 16) + 1) * 2048] if j < 16 else KST[j % 2]
                sc.copy(dst, bank(grp, 4)[:, 0:2048]).then_inc(s1sc, 1)
            for jb in range(8):
                sc.wait_ge(s1pb, jb + 1)
                for tt in range(8):
                    i = jb * 8 + tt
                    if i >= 2:
                        sc.wait_ge(s1spb, 16 * (i - 1))
                    sc.copy(VST[i % 2], bank(tt)[:, 0:512]).then_inc(s1scb, 1)
            if PH < 2:
                return
            sc.wait_ge(s1sp, 16 * 16)
            for m in range(NM):
                for u in range(8):
                    sc.wait_ge(s2st, 8 * m + u + 1)
                    if 8 * m + u >= 4:
                        sc.wait_ge(s2pv, 8 * m + u - 3)
                    sc.activation(PT[u % 4], bank(2 * (u % 2), 2)[:, 0:1024],
                                  AF.Exp, bias=CVC[:, 0:1],
                                  scale=ASCL).then_inc(s2sc, 1)
            if PH < 3:
                return
            for it in range(NT + 3):
                if 1 <= it <= NT:
                    t1 = it - 1
                    p1 = t1 % 4
                    sc.wait_ge(s3v1, t1 + 1)
                    sc.activation(stat(p1, S_SD), stat(p1, S_VE), AF.Sqrt)
                if 2 <= it <= NT + 1:
                    t2 = it - 2
                    p2 = t2 % 4
                    sc.wait_ge(s3v2, t2 + 1)
                    if t2 >= 3:
                        sc.wait_ge(s3spf, 16 * (t2 - 2))
                        sc.wait_ge(s3spb, 16 * (t2 - 2))
                    sc.activation(X1O[t2 % 3], ROU[t2 % 3], AF.Identity,
                                  bias=stat(p2, S_NM), scale=stat(p2, S_RS)).then_inc(s3no, 1)
                if it < NT:
                    sc.wait_ge(s3ra, it + 1)
                    sc.activation(bank((it % 2) * 4, 4)[:, 0:2048], ROU[it % 3], AF.Identity,
                                  bias=zero_b, scale=1.0, accum_out=stat(it % 4, S_SUM))
                    sc.activation(XRS[it % 2], ROU[it % 3], AF.Square,
                                  accum_out=stat(it % 4, S_SQ)).then_inc(s3sq, 1)
                elif it == NT:
                    sspace(sc).then_inc(s3sq, 1)
            if PH < 4:
                return
            for th in range(2):
                for it in range(NHT + 1):
                    if it < NHT:
                        i = th * NHT + it
                        sc.wait_ge(s4w[i % 2], 32 * (i // 2 + 1))
                        if i >= 2:
                            sc.wait_ge(s4pe, 2 * (i - 1))
                        sc.activation(GDQ[i % 2], GBF[i % 2], AF.Identity, bias=CVC[:, 1:2], scale=1.0)
                        sc.activation(UDQ[i % 2], UBF[i % 2], AF.Identity, bias=CVC[:, 1:2],
                                      scale=1.0).then_inc(s4dq, 2)
                    if it >= 1:
                        i1 = th * NHT + it - 1
                        ht1 = it - 1
                        sc.wait_ge(s4pe, 2 * i1 + 1)
                        sc.activation(GS[i1 % 2], bank((i1 % 2) * 4, 2)[:, 0:1024], AF.Silu,
                                      bias=zero_b, scale=SGC[:, ht1:ht1 + 1]).then_inc(s4sil, 1)
                for it in range(8 + 3):
                    g_ = th * 8 + it
                    if 1 <= it <= 8:
                        g1 = th * 8 + it - 1
                        p1 = g1 % 4
                        sc.wait_ge(s4v1, g1 + 1)
                        sc.activation(stat4(p1, S_SD), stat4(p1, S_VE), AF.Sqrt)
                    if 2 <= it <= 9:
                        g2 = th * 8 + it - 2
                        p2 = g2 % 4
                        sc.wait_ge(s4v2, g2 + 1)
                        sc.activation(XR4[g2 % 4], RO4[g2 % 3], AF.Identity,
                                      bias=stat4(p2, S_NM), scale=stat4(p2, S_RS)).then_inc(s4no, 1)
                    if it < 8:
                        sc.wait_ge(s4ra, g_ + 1)
                        sc.activation(SD4, RO4[g_ % 3], AF.Identity, bias=zero_b, scale=1.0,
                                      accum_out=stat4(g_ % 4, S_SUM))
                        sc.activation(DSB[:, it * 2048:(it + 1) * 2048], RO4[g_ % 3], AF.Square,
                                      accum_out=stat4(g_ % 4, S_SQ)).then_inc(s4sq, 1)
                    elif it == 8:
                        sspace(sc).then_inc(s4sq, 1)

        @blk.vector
        def _(v):
            if PH < 1:
                return
            for j in range(32):
                par = j % 2
                v.wait_ge(s1sc, j + 1)
                if j >= 2:
                    v.wait_ge(s1sw, 32 * (j - 1))
                blk_ap = QT[:, (j % 16) * 2048:((j % 16) + 1) * 2048] if j < 16 else KST[par]
                v.tensor_tensor(RT1[par], blk_ap[0:64, :], COS, AL.mult)
                v.tensor_tensor(TMG[par], blk_ap[0:64, :], GTB, AL.mult).then_inc(s1v1, 1)
                v.wait_ge(s1sw, 32 * (j + 1))
                v.tensor_tensor(blk_ap[0:64, :], RT1[par], RT2[par], AL.add).then_inc(s1v2, 1)
            if PH < 2:
                return
            for m in range(NM + 1):
                if m < NM:
                    v.wait_ge(s2pv, 8 * m + 8)
                    if m >= 2:
                        v.wait_ge(s2mu, m - 1)
                    v.tensor_copy(YRW[m % 2], bank(4 + m % 2)[:, 0:512]).then_inc(s2yr, 1)
                    if m >= 2:
                        v.wait_ge(s2g1, 16 * (m - 1))
                    v.tensor_copy(LSB[m % 2], bank(6 + m % 2)[0:1, 0:512]).then_inc(s2l, 1)
                    v.wait_ge(s2g1, 16 * (m + 1))
                    if m >= 2:
                        v.wait_ge(s2g2, 16 * (m - 1))
                    v.reciprocal(RLQ[m % 2], LQ[m % 2])
                    vspace(v).then_inc(s2r, 1)
                if 1 <= m:
                    mp = m - 1
                    hh, cc = divmod(mp, NCH)
                    v.wait_ge(s2bc, mp + 1)
                    v.tensor_tensor(YT[:, hh * 2048 + cc * 512: hh * 2048 + (cc + 1) * 512],
                                    YRW[mp % 2], bank(6 + mp % 2)[:, 0:512],
                                    AL.mult).then_inc(s2mu, 1)
            if PH < 3:
                return
            for it in range(NT + 3):
                if it < NT:
                    v.wait_ge(s3pe, it + 1)
                    v.wait_ge(s3x, 16 * (it + 1))
                    if it >= 3:
                        v.wait_ge(s3no, it - 2)
                    v.tensor_tensor(ROU[it % 3], bank((it % 2) * 4, 4)[:, 0:2048],
                                    XRS[it % 2], AL.add).then_inc(s3ra, 1)
                if 1 <= it <= NT:
                    t1 = it - 1
                    p1 = t1 % 4
                    v.wait_ge(s3sq, t1 + 1)
                    vstep(v, v.tensor_tensor(stat(p1, S_S2), stat(p1, S_SUM), stat(p1, S_SUM), AL.mult))
                    vstep(v, v.tensor_scalar(stat(p1, S_MN), stat(p1, S_SUM), 1.0 / D, None, AL.mult))
                    vstep(v, v.scalar_tensor_tensor(stat(p1, S_VN), stat(p1, S_S2), -1.0 / D,
                                                    stat(p1, S_SQ), AL.mult, AL.add))
                    vstep(v, v.tensor_scalar(stat(p1, S_VE), stat(p1, S_VN), 1.0 / D, EPS,
                                             AL.mult, AL.add))
                    vspace(v).then_inc(s3v1, 1)
                if 2 <= it <= NT + 1:
                    t2 = it - 2
                    p2 = t2 % 4
                    v.wait_ge(s3sq, min(t2 + 2, NT + 1))
                    vstep(v, v.reciprocal(stat(p2, S_RS), stat(p2, S_SD)))
                    vstep(v, v.scalar_tensor_tensor(stat(p2, S_NM), stat(p2, S_MN), -1.0,
                                                    stat(p2, S_RS), AL.mult, AL.mult))
                    vspace(v).then_inc(s3v2, 1)
            if PH < 4:
                return
            for th in range(2):
                for ht in range(NHT):
                    i = th * NHT + ht
                    v.wait_ge(s4sil, i + 1)
                    v.wait_ge(s4pe, 2 * i + 2)
                    v.scalar_tensor_tensor(HT[:, ht * 1024:(ht + 1) * 1024],
                                           bank((i % 2) * 4 + 2, 2)[:, 0:1024],
                                           SUC[:, ht:ht + 1], GS[i % 2],
                                           AL.mult, AL.mult).then_inc(s4stt, 1)
                for dc in range(4):
                    gidx = th * 4 + dc
                    for h2 in range(NHT):
                        i = gidx * NHT + h2
                        v.wait_ge(s4wd[i % 2], 16 * (i // 2 + 1))
                        if i >= 2:
                            v.wait_ge(s4dn, i - 1)
                        v.tensor_scalar(WDQ[i % 2], WDB[i % 2], -128.0, None,
                                        AL.add).then_inc(s4wq, 1)
                    v.wait_ge(s4dn, (gidx + 1) * NHT)
                    for tt in range(8):
                        v.tensor_tensor(DSB[:, tt * 2048 + dc * 512: tt * 2048 + (dc + 1) * 512],
                                        bank(tt)[:, 0:512], SDB[:, dc * 512:(dc + 1) * 512],
                                        AL.mult).then_inc(s4de, 1)
                for it in range(8 + 3):
                    g_ = th * 8 + it
                    if 2 <= it <= 9:
                        g2 = th * 8 + it - 2
                        p2 = g2 % 4
                        v.wait_ge(s4sq, th * 9 + min(it, 9))
                        vstep(v, v.reciprocal(stat4(p2, S_RS), stat4(p2, S_SD)))
                        vstep(v, v.scalar_tensor_tensor(stat4(p2, S_NM), stat4(p2, S_MN), -1.0,
                                                        stat4(p2, S_RS), AL.mult, AL.mult))
                        vspace(v).then_inc(s4v2, 1)
                    if 1 <= it <= 8:
                        g1 = th * 8 + it - 1
                        p1 = g1 % 4
                        v.wait_ge(s4sq, th * 9 + it)
                        vstep(v, v.tensor_tensor(stat4(p1, S_S2), stat4(p1, S_SUM), stat4(p1, S_SUM), AL.mult))
                        vstep(v, v.tensor_scalar(stat4(p1, S_MN), stat4(p1, S_SUM), 1.0 / D, None, AL.mult))
                        vstep(v, v.scalar_tensor_tensor(stat4(p1, S_VN), stat4(p1, S_S2), -1.0 / D,
                                                        stat4(p1, S_SQ), AL.mult, AL.add))
                        vstep(v, v.tensor_scalar(stat4(p1, S_VE), stat4(p1, S_VN), 1.0 / D, EPS,
                                                 AL.mult, AL.add))
                        vspace(v).then_inc(s4v1, 1)
                    if it < 8:
                        v.wait_ge(s4x, 16 * (g_ + 1))
                        v.wait_ge(s4de, 32 * (th + 1))
                        if g_ >= 3:
                            v.wait_ge(s4no, g_ - 2)
                        v.tensor_tensor(RO4[g_ % 3], DSB[:, it * 2048:(it + 1) * 2048],
                                        XR4[g_ % 4], AL.add).then_inc(s4ra, 1)

    return nc


def _host_prep(inputs):
    x = np.ascontiguousarray(np.asarray(inputs["L0_attn_x"], dtype=np.float32))
    wq = np.asarray(inputs["L0_attn_wq"], dtype=np.float32)
    wk = np.asarray(inputs["L0_attn_wk"], dtype=np.float32)
    wv = np.asarray(inputs["L0_attn_wv"], dtype=np.float32)
    wo = np.asarray(inputs["L0_attn_wo"], dtype=np.float32)
    wg = np.asarray(inputs["L0_ffn_wq_gate"], dtype=np.int32)
    wu = np.asarray(inputs["L0_ffn_wq_up"], dtype=np.int32)
    wd = np.asarray(inputs["L0_ffn_wq_down"], dtype=np.int32)
    sg = np.asarray(inputs["L0_ffn_s_gate"], dtype=np.float32)
    su = np.asarray(inputs["L0_ffn_s_up"], dtype=np.float32)
    sd = np.asarray(inputs["L0_ffn_s_down"], dtype=np.float32)

    def wblk(w):
        return np.ascontiguousarray(
            w.T.reshape(NDI, 128, NT, 128).transpose(2, 1, 0, 3).reshape(NT, 128, D))

    shared = {
        "wq_c": wblk(wq), "wk_c": wblk(wk),
        "wv_c": np.ascontiguousarray(
            wv.T.reshape(NDI, 128, 4, 512).transpose(2, 1, 0, 3).reshape(4, 128, NDI * 512)),
        "wo_T": np.ascontiguousarray(wo.T),
        "wg_c": np.ascontiguousarray(
            wg.T.reshape(NDI, 128, NHT, 128).transpose(2, 1, 0, 3).reshape(NHT, 128, 2048)),
        "wu_c": np.ascontiguousarray(
            wu.T.reshape(NDI, 128, NHT, 128).transpose(2, 1, 0, 3).reshape(NHT, 128, 2048)),
        "wd_c": np.ascontiguousarray(
            wd.T.reshape(NHT, 128, 4, 512).transpose(0, 2, 1, 3)),
        "sg_c": np.ascontiguousarray(sg.reshape(NHT, 128).T),
        "su_c": np.ascontiguousarray(su.reshape(NHT, 128).T),
        "sd_b": np.ascontiguousarray(np.tile(sd.reshape(1, 2048), (128, 1))),
    }
    in_maps = []
    for b in range(8):
        m = dict(shared)
        m["x"] = np.ascontiguousarray(x[b])
        m["xT"] = np.ascontiguousarray(x[b].T)
        in_maps.append(m)
    return in_maps


def kernel(**inputs):
    in_maps = _host_prep(inputs)
    nc = build_program()
    res = run_bass_kernel_spmd(nc, in_maps, core_ids=list(range(8)))
    return np.stack([r["out"] for r in res.results])


# version 04
# speedup vs baseline: 1.0426x; 1.0004x over previous
import math
import numpy as np
import ml_dtypes
import concourse.bass as bass
import concourse.mybir as mybir
from concourse.bass_utils import run_bass_kernel_spmd

F32 = mybir.dt.float32
BF16 = mybir.dt.bfloat16
I32 = mybir.dt.int32
AF = mybir.ActivationFunctionType
AL = mybir.AluOpType
AX = mybir.AxisListType
bfdt = ml_dtypes.bfloat16

T = 2048
D = 2048
NH = 16
HD = 128
HID = 5632
NT = 16
NDI = 16
NHT = 44
QC = 512
NCH = 4
NM = NH * NCH
C_SHIFT = 24.0
ASCL = 1.0 / math.sqrt(HD)
EPS = 1e-5

DEBUG = False
PH = 4

ARENA_F32_COLS = 53040

OFF_SG    = 202704
OFF_SU    = 202896
OFF_CVEC  = 203088
OFF_ONESR = 203120
OFF_ONESB = 203632
OFF_SDB   = 203888
OFF_SPC   = 199168
OFF_SPC2  = 200192

OFF_XT    = 0
OFF_QT    = 65536
OFF_COS   = 131072
OFF_SING  = 135168
OFF_WBUF  = 139264
OFF_KSTG  = 147456
OFF_RT1   = 155648
OFF_TMPG  = 163840
OFF_RT2   = 172032
OFF_VW    = 180224
OFF_VSTG  = 196608

OFF_YT    = 0
OFF_KVK   = 131072
OFF_KVV   = 139264
OFF_PT    = 147456
OFF_LSB   = 155648
OFF_LQ    = 160000
OFF_RLQ   = 160064
OFF_RLROW = 160128
OFF_YRAW  = 164224

OFF_XRES  = 131072
OFF_ROUT  = 147456
OFF_STAT  = 172032
OFF_X1O   = 172608

OFF_X1T   = 0
OFF_HT    = 32768
OFF_GBUF  = 122880
OFF_UBUF  = 131072
OFF_GDQ   = 139264
OFF_UDQ   = 147456
OFF_GS    = 155648
OFF_DSB   = 122880
OFF_WDB   = 188416
OFF_WDQ   = 190464
OFF_STAT4 = 192512
OFF_XRES4 = 0
OFF_ROUT4 = 32768
OFF_SUMD4 = 57344
OFF_SUMD3 = 188992

S_SUM, S_SQ, S_S2, S_VN, S_VE, S_MN, S_SD, S_RS, S_NM = range(9)


def build_program():
    nc = bass.Bass("TRN2", target_bir_lowering=False)

    x_h   = nc.dram_tensor("x",   [T, D], F32, kind="ExternalInput").ap()
    xT_h  = nc.dram_tensor("xT",  [D, T], F32, kind="ExternalInput").ap()
    wq_h  = nc.dram_tensor("wq_c", [NT, 128, D], F32, kind="ExternalInput").ap()
    wk_h  = nc.dram_tensor("wk_c", [NT, 128, D], F32, kind="ExternalInput").ap()
    wv_h  = nc.dram_tensor("wv_c", [4, 128, NDI * 512], F32, kind="ExternalInput").ap()
    wo_h  = nc.dram_tensor("wo_T", [D, D], F32, kind="ExternalInput").ap()
    wg_h  = nc.dram_tensor("wg_c", [NHT, 128, 2048], I32, kind="ExternalInput").ap()
    wu_h  = nc.dram_tensor("wu_c", [NHT, 128, 2048], I32, kind="ExternalInput").ap()
    wd_h  = nc.dram_tensor("wd_c", [NHT, 4, 128, 512], I32, kind="ExternalInput").ap()
    sg_h  = nc.dram_tensor("sg_c", [128, NHT], F32, kind="ExternalInput").ap()
    su_h  = nc.dram_tensor("su_c", [128, NHT], F32, kind="ExternalInput").ap()
    sd_h  = nc.dram_tensor("sd_b", [128, 2048], F32, kind="ExternalInput").ap()
    out_h = nc.dram_tensor("out", [T, D], F32, kind="ExternalOutput").ap()

    scr_kind = "ExternalOutput" if DEBUG else "Internal"
    kt_sp = nc.dram_tensor("kt_sp", [NH, 128, T], BF16, kind=scr_kind).ap()
    v_sp  = nc.dram_tensor("v_sp",  [NT, 128, D], BF16, kind=scr_kind).ap()
    x1f   = nc.dram_tensor("x1f", [T, D], F32, kind=scr_kind).ap()
    x1b   = nc.dram_tensor("x1b", [T, D], BF16, kind=scr_kind).ap()
    qt_dbg = nc.dram_tensor("qt_dbg", [NH, 128, T], BF16, kind="ExternalOutput").ap() if DEBUG else None
    yt_dbg = nc.dram_tensor("yt_dbg", [NH, 128, T], BF16, kind="ExternalOutput").ap() if DEBUG else None

    inv_freq = (1.0 / 10000.0 ** (np.arange(0, HD, 2, dtype=np.float32) / HD)).astype(np.float32)
    ang = np.arange(T, dtype=np.float32)[:, None] * inv_freq[None, :]
    cosT = np.cos(ang).T.astype(bfdt)
    sinT = np.sin(ang).T.astype(np.float32)
    gtab = np.concatenate([sinT[32:64], -sinT[0:32]], axis=0).astype(bfdt)
    cos_h  = nc.inline_tensor(np.ascontiguousarray(cosT), name="cosT")
    g_h    = nc.inline_tensor(np.ascontiguousarray(gtab), name="gtab")
    ones_h = nc.inline_tensor(np.ones((128, 128), dtype=bfdt), name="onesb")
    onesr_h = nc.inline_tensor(np.ones((1, 128), dtype=np.float32), name="onesr")
    cvec = np.zeros((128, 4), dtype=np.float32)
    cvec[:, 0] = -C_SHIFT
    cvec[:, 1] = -128.0
    cvec[:, 2] = EPS
    cvec_h = nc.inline_tensor(cvec, name="cvec")
    zeros_h = nc.inline_tensor(np.zeros((128, 320), dtype=np.float32), name="zeros")

    AB = nc.alloc_sbuf_tensor("AB", [128, ARENA_F32_COLS], F32)
    PS = nc.alloc_psum_tensor("PS", [128, 4096], F32)
    ps_ap = PS.ap()

    def f32(off_b, p, cols):
        return AB.ap()[0:p, off_b // 4: off_b // 4 + cols]

    def bf16(off_b, p, cols):
        return AB.ap()[0:p, off_b // 4: off_b // 4 + cols // 2].bitcast(BF16)

    def bank(b, n=1):
        return ps_ap[:, b * 512:(b + n) * 512]

    XT  = bf16(OFF_XT, 128, NDI * 2048)
    QT  = bf16(OFF_QT, 128, NH * 2048)
    COS = bf16(OFF_COS, 64, 2048)
    GTB = bf16(OFF_SING, 64, 2048)
    WB  = [bf16(OFF_WBUF + i * 4096, 128, 2048) for i in range(2)]
    KST = [bf16(OFF_KSTG + i * 4096, 128, 2048) for i in range(2)]
    RT1 = [bf16(OFF_RT1 + i * 4096, 64, 2048) for i in range(2)]
    TMG = [bf16(OFF_TMPG + i * 4096, 64, 2048) for i in range(2)]
    RT2 = [bf16(OFF_RT2 + i * 4096, 64, 2048) for i in range(2)]
    VW  = [bf16(OFF_VW, 128, 8192), bf16(OFF_WBUF, 128, 8192)]
    VST = [bf16(OFF_VSTG + i * 1024, 128, 512) for i in range(2)]

    YT  = bf16(OFF_YT, 128, NH * 2048)
    KVK = [bf16(OFF_KVK + i * 4096, 128, 2048) for i in range(2)]
    KVV = [bf16(OFF_KVV + i * 4096, 128, 2048) for i in range(2)]
    PT  = [bf16(OFF_PT + i * 2048, 128, 1024) for i in range(4)]
    LSB = [f32(OFF_LSB + i * 2048, 1, 512) for i in range(2)]
    LQ  = [f32(OFF_LQ + i * 16, 128, 4) for i in range(2)]
    RLQ = [f32(OFF_RLQ + i * 16, 128, 4) for i in range(2)]
    RLR = [f32(OFF_RLROW + i * 2048, 1, 512) for i in range(2)]
    YRW = [f32(OFF_YRAW + i * 2048, 128, 512) for i in range(2)]
    WOT = QT

    XRS = [f32(OFF_XRES + i * 8192, 128, 2048) for i in range(2)]
    ROU = [f32(OFF_ROUT + i * 8192, 128, 2048) for i in range(3)]
    X1O = [f32(OFF_X1O + i * 8192, 128, 2048) for i in range(3)]

    X1T = bf16(OFF_X1T, 128, NDI * 1024)
    HT  = bf16(OFF_HT, 128, NHT * 1024)
    GBF = [bf16(OFF_GBUF + i * 4096, 128, 2048) for i in range(2)]
    UBF = [bf16(OFF_UBUF + i * 4096, 128, 2048) for i in range(2)]
    GDQ = [bf16(OFF_GDQ + i * 4096, 128, 2048) for i in range(2)]
    UDQ = [bf16(OFF_UDQ + i * 4096, 128, 2048) for i in range(2)]
    GS  = [f32(OFF_GS + i * 4096, 128, 1024) for i in range(2)]
    DSB = f32(OFF_DSB, 128, 8 * 2048)
    WDB = [bf16(OFF_WDB + i * 1024, 128, 512) for i in range(2)]
    WDQ = [bf16(OFF_WDQ + i * 1024, 128, 512) for i in range(2)]
    RO4 = [f32(OFF_ROUT4 + i * 8192, 128, 2048) for i in range(3)]
    XR4 = [f32(OFF_XRES4 + i * 8192, 128, 2048) for i in range(4)]
    SD3 = f32(OFF_SUMD3, 128, 2048)
    SD4 = f32(OFF_SUMD4, 128, 2048)

    SGC = f32(OFF_SG, 128, NHT)
    SUC = f32(OFF_SU, 128, NHT)
    CVC = f32(OFF_CVEC, 128, 4)
    ONR = f32(OFF_ONESR, 1, 128)
    ONB = bf16(OFF_ONESB, 128, 128)
    ONRB = bf16(OFF_ONESB, 1, 128)
    RLRB = [bf16(OFF_RLROW + i * 2048, 1, 512) for i in range(2)]
    SDB = f32(OFF_SDB, 128, 2048)
    SPC = f32(OFF_SPC, 128, 256)
    SPC2 = f32(OFF_SPC2, 128, 64)

    vf_sem = nc.alloc_semaphore("vfence")
    vf_n = [0]
    def vstep(v, instr, spacers=2):
        instr.then_inc(vf_sem, 1)
        vf_n[0] += 1
        for _ in range(spacers):
            vspace(v)
        v.wait_ge(vf_sem, vf_n[0])

    vsp_n = [0]
    def vspace(v):
        n = vsp_n[0] % 8
        vsp_n[0] += 1
        return v.tensor_copy(SPC[:, 32 * n:32 * n + 32], SDB[:, 0:32])

    ssp_n = [0]
    def sspace(sc):
        n = ssp_n[0] % 2
        ssp_n[0] += 1
        return sc.copy(SPC2[:, 32 * n:32 * n + 32], SDB[:, 32:64])

    def stat(par, idx):
        return f32(OFF_STAT + par * 144, 128, 36)[:, idx:idx + 1]

    def stat4(par, idx):
        return f32(OFF_STAT4 + par * 144, 128, 36)[:, idx:idx + 1]

    def sem(name):
        return nc.alloc_semaphore(name)

    s0c = sem("s0c"); s0x = sem("s0x"); s0sd = sem("s0sd"); s0d = sem("s0d")
    s1w = [sem("s1wA"), sem("s1wB")]; s1pe = sem("s1pe"); s1sc = sem("s1sc")
    s1v1 = sem("s1v1"); s1sw = sem("s1sw"); s1v2 = sem("s1v2"); s1sp = sem("s1sp")
    s1vw = [sem("s1vwA"), sem("s1vwB")]; s1pb = sem("s1pb"); s1scb = sem("s1scb"); s1spb = sem("s1spb")
    s2kv = [sem("s2kvA"), sem("s2kvB")]; s2st = sem("s2st"); s2pv = sem("s2pv"); s2sc = sem("s2sc")
    s2yr = sem("s2yr"); s2l = sem("s2l"); s2r = sem("s2r"); s2g1 = sem("s2g1")
    s2g2 = sem("s2g2"); s2bc = sem("s2bc"); s2mu = sem("s2mu")
    s2wo = sem("s2wo")
    s3x = sem("s3x"); s3pe = sem("s3pe"); s3ra = sem("s3ra"); s3sq = sem("s3sq")
    s3v1 = sem("s3v1"); s3v2 = sem("s3v2"); s3no = sem("s3no")
    s3spf = sem("s3spf"); s3spb = sem("s3spb")
    s4xt = [sem("s4xtA"), sem("s4xtB")]; s4w = [sem("s4wA"), sem("s4wB")]; s4dq = sem("s4dq"); s4pe = sem("s4pe")
    s4sil = sem("s4sil"); s4stt = sem("s4stt"); s4wd = [sem("s4wdA"), sem("s4wdB")]; s4wq = sem("s4wq")
    s4dn = sem("s4dn"); s4de = sem("s4de"); s4x = sem("s4x"); s4ra = sem("s4ra")
    s4sq = sem("s4sq"); s4v1 = sem("s4v1"); s4v2 = sem("s4v2")
    s4no = sem("s4no"); s4out = [sem("s4outA"), sem("s4outB")]

    with nc.Block() as blk:

        @blk.sync
        def _(s):
            s.dma_start(out=COS, in_=cos_h.ap()).then_inc(s0c, 16)
            s.dma_start(out=GTB, in_=g_h.ap()).then_inc(s0c, 16)
            s.dma_start(out=ONB, in_=ones_h.ap()).then_inc(s0c, 16)
            s.dma_start(out=ONR, in_=onesr_h.ap()).then_inc(s0c, 16)
            s.dma_start(out=CVC, in_=cvec_h.ap()).then_inc(s0c, 16)
            s.dma_start(out=SGC, in_=sg_h).then_inc(s0c, 16)
            s.dma_start(out=SUC, in_=su_h).then_inc(s0c, 16)
            s.dma_start(out=SDB, in_=sd_h).then_inc(s0c, 16)
            s.dma_start(out=f32(OFF_SPC, 128, 320), in_=zeros_h.ap()).then_inc(s0c, 16)
            if PH < 1:
                s.wait_ge(s0x, 16 * NDI)
                s.wait_ge(s0sd, 1)
                return
            for j in range(32):
                par = j % 2
                s.wait_ge(s1v1, j + 1)
                if j >= 2:
                    s.wait_ge(s1v2, j - 1)
                s.dma_start(out=RT2[par][0:32, :], in_=TMG[par][32:64, :]).then_inc(s1sw, 16)
                s.dma_start(out=RT2[par][32:64, :], in_=TMG[par][0:32, :]).then_inc(s1sw, 16)
                if j >= 16:
                    ot = j - 16
                    s.wait_ge(s1v2, j + 1)
                    if ot >= 1:
                        s.wait_ge(s1sp, 16 * ot)
                    s.dma_start(out=kt_sp[ot], in_=KST[par]).then_inc(s1sp, 16)
            for i in range(64):
                jb, tt = divmod(i, 8)
                dc, th = divmod(jb, 2)
                s.wait_ge(s1scb, i + 1)
                if i >= 1:
                    s.wait_ge(s1spb, 16 * i)
                s.dma_start(out=v_sp[th * 8 + tt][:, dc * 512:(dc + 1) * 512],
                            in_=VST[i % 2]).then_inc(s1spb, 16)
            if DEBUG:
                s.wait_ge(s1v2, 32)
                for h in range(NH):
                    s.dma_start(out=qt_dbg[h], in_=QT[:, h * 2048:(h + 1) * 2048]).then_inc(s0d, 16)
            if PH < 2:
                s.wait_ge(s1sp, 16 * 16)
                s.wait_ge(s1spb, 16 * 64)
                if DEBUG:
                    s.wait_ge(s0d, 16 * NH)
                return
            s.wait_ge(s1sp, 16 * 16)
            s.wait_ge(s1spb, 16 * 64)
            for h in (0, 1):
                s.dma_start(out=KVK[h % 2], in_=kt_sp[h]).then_inc(s2kv[h % 2], 16)
                s.dma_start(out=KVV[h % 2],
                            in_=v_sp[:, :, h * 128:(h + 1) * 128].rearrange("k p b -> p k b")).then_inc(s2kv[h % 2], 16)
            for m in range(NM):
                h, c = divmod(m, NCH)
                s.wait_ge(s2l, m + 1)
                if m >= 1:
                    s.wait_ge(s2g1, 16 * m)
                if m >= 2:
                    s.wait_ge(s2r, m - 1)
                s.dma_start(out=LQ[m % 2], in_=LSB[m % 2]).then_inc(s2g1, 16)
                if c == NCH - 1 and h + 2 < NH:
                    hh = h + 2
                    s.wait_ge(s2pv, 32 * (hh - 1))
                    s.wait_ge(s2kv[hh % 2], 32 * (hh // 2))
                    s.dma_start(out=KVK[hh % 2], in_=kt_sp[hh]).then_inc(s2kv[hh % 2], 16)
                    s.dma_start(out=KVV[hh % 2],
                                in_=v_sp[:, :, hh * 128:(hh + 1) * 128].rearrange("k p b -> p k b")).then_inc(s2kv[hh % 2], 16)
            if DEBUG:
                s.wait_ge(s2mu, NM)
                for h in range(NH):
                    s.dma_start(out=yt_dbg[h], in_=YT[:, h * 2048:(h + 1) * 2048]).then_inc(s0d, 16)
            if PH < 3:
                s.wait_ge(s2mu, NM)
                if DEBUG:
                    s.wait_ge(s0d, 16 * 2 * NH)
                return
            s.wait_ge(s2mu, NM)
            for it in range(NT + 4):
                if it < NT:
                    if it >= 2:
                        s.wait_ge(s3ra, it - 1)
                        s.wait_ge(s3sq, it - 1)
                    if it >= 1:
                        s.wait_ge(s3x, 16 * it)
                    s.dma_start(out=XRS[it % 2], in_=x_h[it * 128:(it + 1) * 128, :]).then_inc(s3x, 16)
                sp = it - 3
                if 0 <= sp < NT:
                    s.wait_ge(s3no, sp + 1)
                    if sp >= 1:
                        s.wait_ge(s3spf, 16 * sp)
                    s.dma_start(out=x1f[sp * 128:(sp + 1) * 128, :], in_=X1O[sp % 3]).then_inc(s3spf, 16)
            if PH < 4:
                s.wait_ge(s3spf, 16 * NT)
                s.wait_ge(s3spb, 16 * NT)
                if DEBUG:
                    s.wait_ge(s0d, 16 * 2 * NH)
                return
            for th in range(2):
                if th == 0:
                    s.wait_ge(s3pe, NT)
                    s.wait_ge(s3spb, 16 * 8)
                    if DEBUG:
                        s.wait_ge(s0d, 16 * 2 * NH)
                else:
                    s.wait_ge(s4out[0], 16 * 4)
                    s.wait_ge(s4out[1], 16 * 4)
                for dt in range(NDI):
                    gi = th * NDI + dt
                    if th == 1 and dt == 0:
                        s.wait_ge(s3spb, 16 * NT)
                    if gi >= 2:
                        s.wait_ge(s4xt[gi % 2], 16 * (gi // 2))
                    s.dma_start_transpose(
                        X1T[:, dt * 1024:(dt + 1) * 1024],
                        x1b[th * 1024:(th + 1) * 1024, dt * 128:(dt + 1) * 128],
                    ).then_inc(s4xt[gi % 2], 16)
                if th == 0:
                    s.wait_ge(s3spf, 16 * NT)
                for it in range(8 + 2):
                    if it >= 2:
                        g2 = th * 8 + it - 2
                        s.wait_ge(s4no, g2 + 1)
                        if g2 >= 2:
                            s.wait_ge(s4out[g2 % 2], 16 * (g2 // 2))
                        s.dma_start(out=out_h[g2 * 128:(g2 + 1) * 128, :],
                                    in_=XR4[g2 % 4]).then_inc(s4out[g2 % 2], 16)
                    if it < 8:
                        g = th * 8 + it
                        if it < 4:
                            s.wait_ge(s4pe, 2 * NHT * (th + 1))
                        if it >= 4:
                            s.wait_ge(s4ra, g - 3)
                            j = g - 3
                            s.wait_ge(s4out[0], 16 * (j // 2 + 1))
                            s.wait_ge(s4out[1], 16 * ((j + 1) // 2))
                        if g >= 1:
                            s.wait_ge(s4x, 16 * g)
                        s.dma_start(out=XR4[g % 4], in_=x1f[g * 128:(g + 1) * 128, :]).then_inc(s4x, 16)
            s.wait_ge(s4out[0], 16 * 8)
            s.wait_ge(s4out[1], 16 * 8)
            if DEBUG:
                s.wait_ge(s0d, 16 * (2 * NH))

        @blk.gpsimd
        def _(g):
            for dt in range(NDI):
                g.dma_start(out=XT[:, dt * 2048:(dt + 1) * 2048],
                            in_=xT_h[dt * 128:(dt + 1) * 128, :]).then_inc(s0x, 16)
            if PH < 1:
                return
            g.dma_start(out=VW[0], in_=wv_h[0]).then_inc(s1vw[0], 16)
            for j in range(32):
                src = wq_h[j] if j < 16 else wk_h[j - 16]
                if j >= 2:
                    g.wait_ge(s1pe, j - 1)
                    g.wait_ge(s1w[j % 2], 16 * (j // 2))
                g.dma_start(out=WB[j % 2], in_=src).then_inc(s1w[j % 2], 16)
            for dc in range(1, 4):
                if dc == 1:
                    g.wait_ge(s1pe, 32)
                    g.wait_ge(s1sp, 16 * 16)
                if dc >= 2:
                    g.wait_ge(s1pb, 2 * (dc - 1))
                    g.wait_ge(s1vw[dc % 2], 16 * (dc // 2))
                g.dma_start(out=VW[dc % 2], in_=wv_h[dc]).then_inc(s1vw[dc % 2], 16)
            if PH < 2:
                return
            for m in range(NM):
                h, c = divmod(m, NCH)
                g.wait_ge(s2r, m + 1)
                if m >= 1:
                    g.wait_ge(s2g2, 16 * m)
                if m >= 2:
                    g.wait_ge(s2bc, m - 1)
                g.dma_start(out=RLRB[m % 2], in_=RLQ[m % 2]).then_inc(s2g2, 16)
                if c == NCH - 1:
                    g.wait_ge(s2pv, 32 * (h + 1))
                    if DEBUG and h == 0:
                        g.wait_ge(s0d, 16 * NH)
                    if h >= 1:
                        g.wait_ge(s2wo, 16 * h)
                    g.dma_start(out=WOT[:, h * 2048:(h + 1) * 2048],
                                in_=wo_h[h * 128:(h + 1) * 128, :]).then_inc(s2wo, 16)
            if PH < 3:
                return
            for tt in range(NT):
                g.wait_ge(s3no, tt + 1)
                if tt >= 1:
                    g.wait_ge(s3spb, 16 * tt)
                g.dma_start(out=x1b[tt * 128:(tt + 1) * 128, :], in_=X1O[tt % 3]).then_inc(s3spb, 16)
                if tt == 12:
                    g.wait_ge(s3pe, NT)
                    g.wait_ge(s3ra, NT - 1)
                    g.wait_ge(s3sq, NT - 1)
                    for i0 in range(2):
                        g.dma_start(out=GBF[i0 % 2], in_=wg_h[i0]).then_inc(s4w[i0 % 2], 16)
                        g.dma_start(out=UBF[i0 % 2], in_=wu_h[i0]).then_inc(s4w[i0 % 2], 16)
            if PH < 4:
                return
            for th in range(2):
                for ht in range(NHT):
                    i = th * NHT + ht
                    if i < 2:
                        continue
                    if th == 1 and ht == 0:
                        g.wait_ge(s4ra, 8)
                    g.wait_ge(s4dq, 2 * (i - 1))
                    g.wait_ge(s4w[i % 2], 32 * (i // 2))
                    g.dma_start(out=GBF[i % 2], in_=wg_h[ht]).then_inc(s4w[i % 2], 16)
                    g.dma_start(out=UBF[i % 2], in_=wu_h[ht]).then_inc(s4w[i % 2], 16)
                for dc in range(4):
                    for h2 in range(NHT):
                        i = (th * 4 + dc) * NHT + h2
                        if i >= 2:
                            g.wait_ge(s4wq, i - 1)
                            g.wait_ge(s4wd[i % 2], 16 * (i // 2))
                        g.dma_start(out=WDB[i % 2], in_=wd_h[h2, dc]).then_inc(s4wd[i % 2], 16)

        @blk.tensor
        def _(t):
            t.wait_ge(s0c, 16 * 9)
            if PH < 1:
                return
            t.wait_ge(s0x, 16 * NDI)
            for j in range(32):
                t.wait_ge(s1w[j % 2], 16 * (j // 2 + 1))
                if j >= 2:
                    t.wait_ge(s1sc, j - 1)
                grp = (j % 2) * 4
                for di in range(NDI):
                    for cch in range(4):
                        mm = t.matmul(bank(grp + cch)[:, 0:512],
                                      WB[j % 2][:, di * 128:(di + 1) * 128],
                                      XT[:, di * 2048 + cch * 512: di * 2048 + (cch + 1) * 512],
                                      start=(di == 0), stop=(di == NDI - 1))
                mm.then_inc(s1pe, 1)
            t.wait_ge(s1sc, 32)
            for jb in range(8):
                dc, th = divmod(jb, 2)
                if th == 0:
                    t.wait_ge(s1vw[dc % 2], 16 * (dc // 2 + 1))
                if jb >= 1:
                    t.wait_ge(s1scb, jb * 8)
                for di in range(NDI):
                    for tt in range(8):
                        tg = th * 8 + tt
                        mm = t.matmul(bank(tt)[:, 0:512],
                                      XT[:, di * 2048 + tg * 128: di * 2048 + (tg + 1) * 128],
                                      VW[dc % 2][:, di * 512:(di + 1) * 512],
                                      start=(di == 0), stop=(di == NDI - 1))
                mm.then_inc(s1pb, 1)
            if PH < 2:
                return
            for m in range(NM):
                h, c = divmod(m, NCH)
                if c == 0:
                    t.wait_ge(s2kv[h % 2], 32 * (h // 2 + 1))
                if m >= 2:
                    t.wait_ge(s2yr, m - 1)
                    t.wait_ge(s2mu, m - 1)
                qs = h * 2048 + c * 512
                for u in range(8):
                    k0, k1 = 2 * u, 2 * u + 1
                    slot = u % 2
                    if u >= 2:
                        t.wait_ge(s2sc, 8 * m + u - 1)
                    elif m >= 1:
                        t.wait_ge(s2sc, 8 * (m - 1) + 7 + (1 if u == 1 else 0))
                    t.matmul(bank(2 * slot)[:, 0:512],
                             KVK[h % 2][:, k0 * 128:(k0 + 1) * 128],
                             QT[:, qs:qs + 512], start=True, stop=True)
                    t.matmul(bank(2 * slot + 1)[:, 0:512],
                             KVK[h % 2][:, k1 * 128:(k1 + 1) * 128],
                             QT[:, qs:qs + 512], start=True,
                             stop=True).then_inc(s2st, 1)
                    if u >= 1:
                        up = u - 1
                        t.wait_ge(s2sc, 8 * m + up + 1)
                        for kk in (2 * up, 2 * up + 1):
                            t.matmul(bank(4 + m % 2)[:, 0:512],
                                     KVV[h % 2][:, kk * 128:(kk + 1) * 128],
                                     PT[up % 4][:, (kk % 2) * 512:(kk % 2 + 1) * 512],
                                     start=(kk == 0), stop=False)
                            mm = t.matmul(bank(6 + m % 2)[0:1, 0:512],
                                          ONB[:, 0:1],
                                          PT[up % 4][:, (kk % 2) * 512:(kk % 2 + 1) * 512],
                                          start=(kk == 0), stop=False)
                        mm.then_inc(s2pv, 1)
                t.wait_ge(s2sc, 8 * m + 8)
                for kk in (14, 15):
                    t.matmul(bank(4 + m % 2)[:, 0:512],
                             KVV[h % 2][:, kk * 128:(kk + 1) * 128],
                             PT[3][:, (kk % 2) * 512:(kk % 2 + 1) * 512],
                             start=False, stop=(kk == 15))
                    mm = t.matmul(bank(6 + m % 2)[0:1, 0:512],
                                  ONB[:, 0:1],
                                  PT[3][:, (kk % 2) * 512:(kk % 2 + 1) * 512],
                                  start=False, stop=(kk == 15))
                mm.then_inc(s2pv, 1)
                if m >= 1:
                    t.wait_ge(s2g2, 16 * m)
                    t.matmul(bank(6 + (m - 1) % 2)[:, 0:512], ONRB, RLRB[(m - 1) % 2],
                             start=True, stop=True).then_inc(s2bc, 1)
            t.wait_ge(s2g2, 16 * NM)
            t.matmul(bank(6 + (NM - 1) % 2)[:, 0:512], ONRB, RLRB[(NM - 1) % 2],
                     start=True, stop=True).then_inc(s2bc, 1)
            if PH < 3:
                return
            t.wait_ge(s2mu, NM)
            for tt in range(NT):
                if tt >= 2:
                    t.wait_ge(s3ra, tt - 1)
                    t.wait_ge(s3sq, tt - 1)
                for y in range(NH):
                    t.wait_ge(s2wo, 16 * (y + 1))
                    for cch in range(4):
                        mm = t.matmul(bank((tt % 2) * 4 + cch)[:, 0:512],
                                      YT[:, y * 2048 + tt * 128: y * 2048 + (tt + 1) * 128],
                                      WOT[:, y * 2048 + cch * 512: y * 2048 + (cch + 1) * 512],
                                      start=(y == 0), stop=(y == NH - 1))
                mm.then_inc(s3pe, 1)
            if PH < 4:
                return
            t.wait_ge(s3ra, NT)
            for th in range(2):
                t.wait_ge(s4xt[0], 16 * (NDI // 2) * (th + 1))
                t.wait_ge(s4xt[1], 16 * (NDI // 2) * (th + 1))
                if th == 0:
                    t.wait_ge(s3spf, 16 * NT)
                    t.wait_ge(s3spb, 16 * NT)
                for ht in range(NHT):
                    i = th * NHT + ht
                    t.wait_ge(s4dq, 2 * (i + 1))
                    if i >= 2:
                        t.wait_ge(s4stt, i - 1)
                    par = i % 2
                    for di in range(NDI):
                        for cch in range(2):
                            mm = t.matmul(bank(par * 4 + cch)[:, 0:512],
                                          GDQ[par][:, di * 128:(di + 1) * 128],
                                          X1T[:, di * 1024 + cch * 512: di * 1024 + (cch + 1) * 512],
                                          start=(di == 0), stop=(di == NDI - 1))
                    mm.then_inc(s4pe, 1)
                    for di in range(NDI):
                        for cch in range(2):
                            mm = t.matmul(bank(par * 4 + 2 + cch)[:, 0:512],
                                          UDQ[par][:, di * 128:(di + 1) * 128],
                                          X1T[:, di * 1024 + cch * 512: di * 1024 + (cch + 1) * 512],
                                          start=(di == 0), stop=(di == NDI - 1))
                    mm.then_inc(s4pe, 1)
                t.wait_ge(s4stt, NHT * (th + 1))
                for dc in range(4):
                    gidx = th * 4 + dc
                    if gidx >= 1:
                        t.wait_ge(s4de, 8 * gidx)
                    for h2 in range(NHT):
                        i = gidx * NHT + h2
                        t.wait_ge(s4wq, i + 1)
                        for tt in range(8):
                            mm = t.matmul(bank(tt)[:, 0:512],
                                          HT[:, h2 * 1024 + tt * 128: h2 * 1024 + (tt + 1) * 128],
                                          WDQ[i % 2][:, 0:512],
                                          start=(h2 == 0), stop=(h2 == NHT - 1))
                        mm.then_inc(s4dn, 1)

        @blk.scalar
        def _(sc):
            zero_b = CVC[:, 3:4]
            if PH < 1:
                return
            for j in range(32):
                sc.wait_ge(s1pe, j + 1)
                if j >= 18:
                    sc.wait_ge(s1sp, 16 * (j - 17))
                grp = (j % 2) * 4
                dst = QT[:, (j % 16) * 2048:((j % 16) + 1) * 2048] if j < 16 else KST[j % 2]
                sc.copy(dst, bank(grp, 4)[:, 0:2048]).then_inc(s1sc, 1)
            for jb in range(8):
                sc.wait_ge(s1pb, jb + 1)
                for tt in range(8):
                    i = jb * 8 + tt
                    if i >= 2:
                        sc.wait_ge(s1spb, 16 * (i - 1))
                    sc.copy(VST[i % 2], bank(tt)[:, 0:512]).then_inc(s1scb, 1)
            if PH < 2:
                return
            sc.wait_ge(s1sp, 16 * 16)
            for m in range(NM):
                for u in range(8):
                    sc.wait_ge(s2st, 8 * m + u + 1)
                    if 8 * m + u >= 4:
                        sc.wait_ge(s2pv, 8 * m + u - 3)
                    sc.activation(PT[u % 4], bank(2 * (u % 2), 2)[:, 0:1024],
                                  AF.Exp, bias=CVC[:, 0:1],
                                  scale=ASCL).then_inc(s2sc, 1)
            if PH < 3:
                return
            for it in range(NT + 3):
                if 1 <= it <= NT:
                    t1 = it - 1
                    p1 = t1 % 4
                    sc.wait_ge(s3v1, t1 + 1)
                    sc.activation(stat(p1, S_SD), stat(p1, S_VE), AF.Sqrt)
                if 2 <= it <= NT + 1:
                    t2 = it - 2
                    p2 = t2 % 4
                    sc.wait_ge(s3v2, t2 + 1)
                    if t2 >= 3:
                        sc.wait_ge(s3spf, 16 * (t2 - 2))
                        sc.wait_ge(s3spb, 16 * (t2 - 2))
                    sc.activation(X1O[t2 % 3], ROU[t2 % 3], AF.Identity,
                                  bias=stat(p2, S_NM), scale=stat(p2, S_RS)).then_inc(s3no, 1)
                if it < NT:
                    sc.wait_ge(s3ra, it + 1)
                    sc.activation(bank((it % 2) * 4, 4)[:, 0:2048], ROU[it % 3], AF.Identity,
                                  bias=zero_b, scale=1.0, accum_out=stat(it % 4, S_SUM))
                    sc.activation(XRS[it % 2], ROU[it % 3], AF.Square,
                                  accum_out=stat(it % 4, S_SQ)).then_inc(s3sq, 1)
                elif it == NT:
                    sspace(sc).then_inc(s3sq, 1)
            if PH < 4:
                return
            sc.wait_ge(s3no, NT)
            sc.wait_ge(s3ra, NT)
            sc.wait_ge(s3sq, NT)
            for th in range(2):
                for it in range(NHT + 1):
                    if it < NHT:
                        i = th * NHT + it
                        sc.wait_ge(s4w[i % 2], 32 * (i // 2 + 1))
                        if i >= 2:
                            sc.wait_ge(s4pe, 2 * (i - 1))
                        sc.activation(GDQ[i % 2], GBF[i % 2], AF.Identity, bias=CVC[:, 1:2], scale=1.0)
                        sc.activation(UDQ[i % 2], UBF[i % 2], AF.Identity, bias=CVC[:, 1:2],
                                      scale=1.0).then_inc(s4dq, 2)
                    if it >= 1:
                        i1 = th * NHT + it - 1
                        ht1 = it - 1
                        sc.wait_ge(s4pe, 2 * i1 + 1)
                        sc.activation(GS[i1 % 2], bank((i1 % 2) * 4, 2)[:, 0:1024], AF.Silu,
                                      bias=zero_b, scale=SGC[:, ht1:ht1 + 1]).then_inc(s4sil, 1)
                for it in range(8 + 3):
                    g_ = th * 8 + it
                    if 1 <= it <= 8:
                        g1 = th * 8 + it - 1
                        p1 = g1 % 4
                        sc.wait_ge(s4v1, g1 + 1)
                        sc.activation(stat4(p1, S_SD), stat4(p1, S_VE), AF.Sqrt)
                    if 2 <= it <= 9:
                        g2 = th * 8 + it - 2
                        p2 = g2 % 4
                        sc.wait_ge(s4v2, g2 + 1)
                        sc.activation(XR4[g2 % 4], RO4[g2 % 3], AF.Identity,
                                      bias=stat4(p2, S_NM), scale=stat4(p2, S_RS)).then_inc(s4no, 1)
                    if it < 8:
                        sc.wait_ge(s4ra, g_ + 1)
                        sc.activation(SD4, RO4[g_ % 3], AF.Identity, bias=zero_b, scale=1.0,
                                      accum_out=stat4(g_ % 4, S_SUM))
                        sc.activation(DSB[:, it * 2048:(it + 1) * 2048], RO4[g_ % 3], AF.Square,
                                      accum_out=stat4(g_ % 4, S_SQ)).then_inc(s4sq, 1)
                    elif it == 8:
                        sspace(sc).then_inc(s4sq, 1)

        @blk.vector
        def _(v):
            if PH < 1:
                return
            for j in range(32):
                par = j % 2
                v.wait_ge(s1sc, j + 1)
                if j >= 2:
                    v.wait_ge(s1sw, 32 * (j - 1))
                blk_ap = QT[:, (j % 16) * 2048:((j % 16) + 1) * 2048] if j < 16 else KST[par]
                v.tensor_tensor(RT1[par], blk_ap[0:64, :], COS, AL.mult)
                v.tensor_tensor(TMG[par], blk_ap[0:64, :], GTB, AL.mult).then_inc(s1v1, 1)
                v.wait_ge(s1sw, 32 * (j + 1))
                v.tensor_tensor(blk_ap[0:64, :], RT1[par], RT2[par], AL.add).then_inc(s1v2, 1)
            if PH < 2:
                return
            for m in range(NM + 1):
                if m < NM:
                    v.wait_ge(s2pv, 8 * m + 8)
                    if m >= 2:
                        v.wait_ge(s2mu, m - 1)
                    v.tensor_copy(YRW[m % 2], bank(4 + m % 2)[:, 0:512]).then_inc(s2yr, 1)
                    if m >= 2:
                        v.wait_ge(s2g1, 16 * (m - 1))
                    v.tensor_copy(LSB[m % 2], bank(6 + m % 2)[0:1, 0:512]).then_inc(s2l, 1)
                    v.wait_ge(s2g1, 16 * (m + 1))
                    if m >= 2:
                        v.wait_ge(s2g2, 16 * (m - 1))
                    v.reciprocal(RLQ[m % 2], LQ[m % 2])
                    vspace(v).then_inc(s2r, 1)
                if 1 <= m:
                    mp = m - 1
                    hh, cc = divmod(mp, NCH)
                    v.wait_ge(s2bc, mp + 1)
                    v.tensor_tensor(YT[:, hh * 2048 + cc * 512: hh * 2048 + (cc + 1) * 512],
                                    YRW[mp % 2], bank(6 + mp % 2)[:, 0:512],
                                    AL.mult).then_inc(s2mu, 1)
            if PH < 3:
                return
            for it in range(NT + 3):
                if it < NT:
                    v.wait_ge(s3pe, it + 1)
                    v.wait_ge(s3x, 16 * (it + 1))
                    if it >= 3:
                        v.wait_ge(s3no, it - 2)
                    v.tensor_tensor(ROU[it % 3], bank((it % 2) * 4, 4)[:, 0:2048],
                                    XRS[it % 2], AL.add).then_inc(s3ra, 1)
                if 1 <= it <= NT:
                    t1 = it - 1
                    p1 = t1 % 4
                    v.wait_ge(s3sq, t1 + 1)
                    vstep(v, v.tensor_tensor(stat(p1, S_S2), stat(p1, S_SUM), stat(p1, S_SUM), AL.mult))
                    vstep(v, v.tensor_scalar(stat(p1, S_MN), stat(p1, S_SUM), 1.0 / D, None, AL.mult))
                    vstep(v, v.scalar_tensor_tensor(stat(p1, S_VN), stat(p1, S_S2), -1.0 / D,
                                                    stat(p1, S_SQ), AL.mult, AL.add))
                    vstep(v, v.tensor_scalar(stat(p1, S_VE), stat(p1, S_VN), 1.0 / D, EPS,
                                             AL.mult, AL.add))
                    vspace(v).then_inc(s3v1, 1)
                if 2 <= it <= NT + 1:
                    t2 = it - 2
                    p2 = t2 % 4
                    v.wait_ge(s3sq, min(t2 + 2, NT + 1))
                    vstep(v, v.reciprocal(stat(p2, S_RS), stat(p2, S_SD)))
                    vstep(v, v.scalar_tensor_tensor(stat(p2, S_NM), stat(p2, S_MN), -1.0,
                                                    stat(p2, S_RS), AL.mult, AL.mult))
                    vspace(v).then_inc(s3v2, 1)
            if PH < 4:
                return
            for th in range(2):
                for ht in range(NHT):
                    i = th * NHT + ht
                    v.wait_ge(s4sil, i + 1)
                    v.wait_ge(s4pe, 2 * i + 2)
                    v.scalar_tensor_tensor(HT[:, ht * 1024:(ht + 1) * 1024],
                                           bank((i % 2) * 4 + 2, 2)[:, 0:1024],
                                           SUC[:, ht:ht + 1], GS[i % 2],
                                           AL.mult, AL.mult).then_inc(s4stt, 1)
                for dc in range(4):
                    gidx = th * 4 + dc
                    for h2 in range(NHT):
                        i = gidx * NHT + h2
                        v.wait_ge(s4wd[i % 2], 16 * (i // 2 + 1))
                        if i >= 2:
                            v.wait_ge(s4dn, i - 1)
                        v.tensor_scalar(WDQ[i % 2], WDB[i % 2], -128.0, None,
                                        AL.add).then_inc(s4wq, 1)
                    v.wait_ge(s4dn, (gidx + 1) * NHT)
                    for tt in range(8):
                        v.tensor_tensor(DSB[:, tt * 2048 + dc * 512: tt * 2048 + (dc + 1) * 512],
                                        bank(tt)[:, 0:512], SDB[:, dc * 512:(dc + 1) * 512],
                                        AL.mult).then_inc(s4de, 1)
                for it in range(8 + 3):
                    g_ = th * 8 + it
                    if 2 <= it <= 9:
                        g2 = th * 8 + it - 2
                        p2 = g2 % 4
                        v.wait_ge(s4sq, th * 9 + min(it, 9))
                        vstep(v, v.reciprocal(stat4(p2, S_RS), stat4(p2, S_SD)))
                        vstep(v, v.scalar_tensor_tensor(stat4(p2, S_NM), stat4(p2, S_MN), -1.0,
                                                        stat4(p2, S_RS), AL.mult, AL.mult))
                        vspace(v).then_inc(s4v2, 1)
                    if 1 <= it <= 8:
                        g1 = th * 8 + it - 1
                        p1 = g1 % 4
                        v.wait_ge(s4sq, th * 9 + it)
                        vstep(v, v.tensor_tensor(stat4(p1, S_S2), stat4(p1, S_SUM), stat4(p1, S_SUM), AL.mult))
                        vstep(v, v.tensor_scalar(stat4(p1, S_MN), stat4(p1, S_SUM), 1.0 / D, None, AL.mult))
                        vstep(v, v.scalar_tensor_tensor(stat4(p1, S_VN), stat4(p1, S_S2), -1.0 / D,
                                                        stat4(p1, S_SQ), AL.mult, AL.add))
                        vstep(v, v.tensor_scalar(stat4(p1, S_VE), stat4(p1, S_VN), 1.0 / D, EPS,
                                                 AL.mult, AL.add))
                        vspace(v).then_inc(s4v1, 1)
                    if it < 8:
                        v.wait_ge(s4x, 16 * (g_ + 1))
                        v.wait_ge(s4de, 32 * (th + 1))
                        if g_ >= 3:
                            v.wait_ge(s4no, g_ - 2)
                        v.tensor_tensor(RO4[g_ % 3], DSB[:, it * 2048:(it + 1) * 2048],
                                        XR4[g_ % 4], AL.add).then_inc(s4ra, 1)

    return nc


def _host_prep(inputs):
    x = np.ascontiguousarray(np.asarray(inputs["L0_attn_x"], dtype=np.float32))
    wq = np.asarray(inputs["L0_attn_wq"], dtype=np.float32)
    wk = np.asarray(inputs["L0_attn_wk"], dtype=np.float32)
    wv = np.asarray(inputs["L0_attn_wv"], dtype=np.float32)
    wo = np.asarray(inputs["L0_attn_wo"], dtype=np.float32)
    wg = np.asarray(inputs["L0_ffn_wq_gate"], dtype=np.int32)
    wu = np.asarray(inputs["L0_ffn_wq_up"], dtype=np.int32)
    wd = np.asarray(inputs["L0_ffn_wq_down"], dtype=np.int32)
    sg = np.asarray(inputs["L0_ffn_s_gate"], dtype=np.float32)
    su = np.asarray(inputs["L0_ffn_s_up"], dtype=np.float32)
    sd = np.asarray(inputs["L0_ffn_s_down"], dtype=np.float32)

    def wblk(w):
        return np.ascontiguousarray(
            w.T.reshape(NDI, 128, NT, 128).transpose(2, 1, 0, 3).reshape(NT, 128, D))

    shared = {
        "wq_c": wblk(wq), "wk_c": wblk(wk),
        "wv_c": np.ascontiguousarray(
            wv.T.reshape(NDI, 128, 4, 512).transpose(2, 1, 0, 3).reshape(4, 128, NDI * 512)),
        "wo_T": np.ascontiguousarray(wo.T),
        "wg_c": np.ascontiguousarray(
            wg.T.reshape(NDI, 128, NHT, 128).transpose(2, 1, 0, 3).reshape(NHT, 128, 2048)),
        "wu_c": np.ascontiguousarray(
            wu.T.reshape(NDI, 128, NHT, 128).transpose(2, 1, 0, 3).reshape(NHT, 128, 2048)),
        "wd_c": np.ascontiguousarray(
            wd.T.reshape(NHT, 128, 4, 512).transpose(0, 2, 1, 3)),
        "sg_c": np.ascontiguousarray(sg.reshape(NHT, 128).T),
        "su_c": np.ascontiguousarray(su.reshape(NHT, 128).T),
        "sd_b": np.ascontiguousarray(np.tile(sd.reshape(1, 2048), (128, 1))),
    }
    in_maps = []
    for b in range(8):
        m = dict(shared)
        m["x"] = np.ascontiguousarray(x[b])
        m["xT"] = np.ascontiguousarray(x[b].T)
        in_maps.append(m)
    return in_maps


def kernel(**inputs):
    in_maps = _host_prep(inputs)
    nc = build_program()
    res = run_bass_kernel_spmd(nc, in_maps, core_ids=list(range(8)))
    return np.stack([r["out"] for r in res.results])


# version 06
# speedup vs baseline: 1.0808x; 1.0366x over previous
import math
import numpy as np
import ml_dtypes
import concourse.bass as bass
import concourse.mybir as mybir
from concourse.bass_utils import run_bass_kernel_spmd

F32 = mybir.dt.float32
BF16 = mybir.dt.bfloat16
I32 = mybir.dt.int32
AF = mybir.ActivationFunctionType
AL = mybir.AluOpType
AX = mybir.AxisListType
bfdt = ml_dtypes.bfloat16

T = 2048
D = 2048
NH = 16
HD = 128
HID = 5632
NT = 16
NDI = 16
NHT = 44
QC = 512
NCH = 4
NM = NH * NCH
C_SHIFT = 24.0
ASCL = 1.0 / math.sqrt(HD)
EPS = 1e-5

DEBUG = False
PH = 4

ARENA_F32_COLS = 53040

OFF_SG    = 202704
OFF_SU    = 202896
OFF_CVEC  = 203088
OFF_ONESR = 203120
OFF_ONESB = 203632
OFF_SDB   = 203888
OFF_SPC   = 200704
OFF_SPC2  = 201728

OFF_XT    = 0
OFF_QT    = 65536
OFF_COS   = 131072
OFF_SING  = 135168
OFF_WBUF  = 139264
OFF_KSTG  = 147456
OFF_RT1   = 155648
OFF_TMPG  = 163840
OFF_RT2   = 172032
OFF_VW    = 180224
OFF_VSTG  = 196608

OFF_YT    = 0
OFF_KVK   = 131072
OFF_KVV   = 139264
OFF_PT    = 147456
OFF_LSB   = 155648
OFF_LQ    = 160000
OFF_RLQ   = 160064
OFF_RLROW = 160128
OFF_YRAW  = 164224

OFF_XRES  = 131072
OFF_ROUT  = 147456
OFF_STAT  = 172032
OFF_X1O   = 172608

OFF_X1T   = 0
OFF_HT    = 32768
OFF_GBUF  = 122880
OFF_UBUF  = 131072
OFF_GDQ   = 139264
OFF_UDQ   = 147456
OFF_GS    = 155648
OFF_DSB   = 122880
OFF_WDB   = 188416
OFF_WDQ   = 190464
OFF_STAT4 = 192512
OFF_XRES4 = 0
OFF_ROUT4 = 32768
OFF_SUMD4 = 57344
OFF_SUMD3 = 188992

S_SUM, S_SQ, S_S2, S_VN, S_VE, S_MN, S_SD, S_RS, S_NM = range(9)


def build_program():
    nc = bass.Bass("TRN2", target_bir_lowering=False)

    x_h   = nc.dram_tensor("x",   [T, D], F32, kind="ExternalInput").ap()
    xT_h  = nc.dram_tensor("xT",  [D, T], F32, kind="ExternalInput").ap()
    wq_h  = nc.dram_tensor("wq_c", [NT, 128, D], F32, kind="ExternalInput").ap()
    wk_h  = nc.dram_tensor("wk_c", [NT, 128, D], F32, kind="ExternalInput").ap()
    wv_h  = nc.dram_tensor("wv_c", [4, 128, NDI * 512], F32, kind="ExternalInput").ap()
    wo_h  = nc.dram_tensor("wo_T", [D, D], F32, kind="ExternalInput").ap()
    wg_h  = nc.dram_tensor("wg_c", [NHT, 128, 2048], I32, kind="ExternalInput").ap()
    wu_h  = nc.dram_tensor("wu_c", [NHT, 128, 2048], I32, kind="ExternalInput").ap()
    wd_h  = nc.dram_tensor("wd_c", [NHT, 4, 128, 512], I32, kind="ExternalInput").ap()
    sg_h  = nc.dram_tensor("sg_c", [128, NHT], F32, kind="ExternalInput").ap()
    su_h  = nc.dram_tensor("su_c", [128, NHT], F32, kind="ExternalInput").ap()
    sd_h  = nc.dram_tensor("sd_b", [128, 2048], F32, kind="ExternalInput").ap()
    out_h = nc.dram_tensor("out", [T, D], F32, kind="ExternalOutput").ap()

    scr_kind = "ExternalOutput" if DEBUG else "Internal"
    kt_sp = nc.dram_tensor("kt_sp", [NH, 128, T], BF16, kind=scr_kind).ap()
    v_sp  = nc.dram_tensor("v_sp",  [NT, 128, D], BF16, kind=scr_kind).ap()
    x1f   = nc.dram_tensor("x1f", [T, D], F32, kind=scr_kind).ap()
    x1b   = nc.dram_tensor("x1b", [T, D], BF16, kind=scr_kind).ap()
    qt_dbg = nc.dram_tensor("qt_dbg", [NH, 128, T], BF16, kind="ExternalOutput").ap() if DEBUG else None
    yt_dbg = nc.dram_tensor("yt_dbg", [NH, 128, T], BF16, kind="ExternalOutput").ap() if DEBUG else None

    inv_freq = (1.0 / 10000.0 ** (np.arange(0, HD, 2, dtype=np.float32) / HD)).astype(np.float32)
    ang = np.arange(T, dtype=np.float32)[:, None] * inv_freq[None, :]
    cosT = np.cos(ang).T.astype(bfdt)
    sinT = np.sin(ang).T.astype(np.float32)
    gtab = np.concatenate([sinT[32:64], -sinT[0:32]], axis=0).astype(bfdt)
    cos_h  = nc.inline_tensor(np.ascontiguousarray(cosT), name="cosT")
    g_h    = nc.inline_tensor(np.ascontiguousarray(gtab), name="gtab")
    ones_h = nc.inline_tensor(np.ones((128, 128), dtype=bfdt), name="onesb")
    onesr_h = nc.inline_tensor(np.ones((1, 128), dtype=np.float32), name="onesr")
    cvec = np.zeros((128, 4), dtype=np.float32)
    cvec[:, 0] = -C_SHIFT
    cvec[:, 1] = -128.0
    cvec[:, 2] = EPS
    cvec_h = nc.inline_tensor(cvec, name="cvec")
    zeros_h = nc.inline_tensor(np.zeros((128, 320), dtype=np.float32), name="zeros")

    AB = nc.alloc_sbuf_tensor("AB", [128, ARENA_F32_COLS], F32)
    PS = nc.alloc_psum_tensor("PS", [128, 4096], F32)
    ps_ap = PS.ap()

    def f32(off_b, p, cols):
        return AB.ap()[0:p, off_b // 4: off_b // 4 + cols]

    def bf16(off_b, p, cols):
        return AB.ap()[0:p, off_b // 4: off_b // 4 + cols // 2].bitcast(BF16)

    def bank(b, n=1):
        return ps_ap[:, b * 512:(b + n) * 512]

    XT  = bf16(OFF_XT, 128, NDI * 2048)
    QT  = bf16(OFF_QT, 128, NH * 2048)
    COS = bf16(OFF_COS, 64, 2048)
    GTB = bf16(OFF_SING, 64, 2048)
    WB  = [bf16(OFF_WBUF + i * 4096, 128, 2048) for i in range(2)]
    KST = [bf16(OFF_KSTG + i * 4096, 128, 2048) for i in range(2)]
    RT1 = [bf16(OFF_RT1 + i * 4096, 64, 2048) for i in range(2)]
    TMG = [bf16(OFF_TMPG + i * 4096, 64, 2048) for i in range(2)]
    RT2 = [bf16(OFF_RT2 + i * 4096, 64, 2048) for i in range(2)]
    VW  = [bf16(OFF_VW, 128, 8192), bf16(OFF_WBUF, 128, 8192)]
    VST = [bf16(OFF_VSTG + i * 1024, 128, 512) for i in range(4)]

    YT  = bf16(OFF_YT, 128, NH * 2048)
    KVK = [bf16(OFF_KVK + i * 4096, 128, 2048) for i in range(2)]
    KVV = [bf16(OFF_KVV + i * 4096, 128, 2048) for i in range(2)]
    PT  = [bf16(OFF_PT + i * 2048, 128, 1024) for i in range(4)]
    LSB = [f32(OFF_LSB + i * 2048, 1, 512) for i in range(2)]
    LQ  = [f32(OFF_LQ + i * 16, 128, 4) for i in range(2)]
    RLQ = [f32(OFF_RLQ + i * 16, 128, 4) for i in range(2)]
    RLR = [f32(OFF_RLROW + i * 2048, 1, 512) for i in range(2)]
    YRW = [f32(OFF_YRAW + i * 2048, 128, 512) for i in range(2)]
    WOT = QT

    XRS = [f32(OFF_XRES + i * 8192, 128, 2048) for i in range(2)]
    ROU = [f32(OFF_ROUT + i * 8192, 128, 2048) for i in range(3)]
    X1O = [f32(OFF_X1O + i * 8192, 128, 2048) for i in range(3)]

    X1T = bf16(OFF_X1T, 128, NDI * 1024)
    HT  = bf16(OFF_HT, 128, NHT * 1024)
    GBF = [bf16(OFF_GBUF + i * 4096, 128, 2048) for i in range(2)]
    UBF = [bf16(OFF_UBUF + i * 4096, 128, 2048) for i in range(2)]
    GDQ = [bf16(OFF_GDQ + i * 4096, 128, 2048) for i in range(2)]
    UDQ = [bf16(OFF_UDQ + i * 4096, 128, 2048) for i in range(2)]
    GS  = [f32(OFF_GS + i * 4096, 128, 1024) for i in range(2)]
    DSB = f32(OFF_DSB, 128, 8 * 2048)
    WDB = [bf16(OFF_WDB + i * 1024, 128, 512) for i in range(2)]
    WDQ = [bf16(OFF_WDQ + i * 1024, 128, 512) for i in range(2)]
    RO4 = [f32(OFF_ROUT4 + i * 8192, 128, 2048) for i in range(3)]
    XR4 = [f32(OFF_XRES4 + i * 8192, 128, 2048) for i in range(4)]
    SD3 = f32(OFF_SUMD3, 128, 2048)
    SD4 = f32(OFF_SUMD4, 128, 2048)

    SGC = f32(OFF_SG, 128, NHT)
    SUC = f32(OFF_SU, 128, NHT)
    CVC = f32(OFF_CVEC, 128, 4)
    ONR = f32(OFF_ONESR, 1, 128)
    ONB = bf16(OFF_ONESB, 128, 128)
    ONRB = bf16(OFF_ONESB, 1, 128)
    RLRB = [bf16(OFF_RLROW + i * 2048, 1, 512) for i in range(2)]
    SDB = f32(OFF_SDB, 128, 2048)
    SPC = f32(OFF_SPC, 128, 256)
    SPC2 = f32(OFF_SPC2, 128, 64)

    vf_sem = nc.alloc_semaphore("vfence")
    vf_n = [0]
    def vstep(v, instr, spacers=2):
        instr.then_inc(vf_sem, 1)
        vf_n[0] += 1
        for _ in range(spacers):
            vspace(v)
        v.wait_ge(vf_sem, vf_n[0])

    vsp_n = [0]
    def vspace(v):
        n = vsp_n[0] % 8
        vsp_n[0] += 1
        return v.tensor_copy(SPC[:, 32 * n:32 * n + 32], SDB[:, 0:32])

    ssp_n = [0]
    def sspace(sc):
        n = ssp_n[0] % 2
        ssp_n[0] += 1
        return sc.copy(SPC2[:, 32 * n:32 * n + 32], SDB[:, 32:64])

    def stat(par, idx):
        return f32(OFF_STAT + par * 144, 128, 36)[:, idx:idx + 1]

    def stat4(par, idx):
        return f32(OFF_STAT4 + par * 144, 128, 36)[:, idx:idx + 1]

    def sem(name):
        return nc.alloc_semaphore(name)

    s0c = sem("s0c"); s0x = sem("s0x"); s0sd = sem("s0sd"); s0d = sem("s0d")
    s1w = [sem("s1wA"), sem("s1wB")]; s1pe = sem("s1pe"); s1sc = sem("s1sc")
    s1v1 = sem("s1v1"); s1sw = sem("s1sw"); s1v2 = sem("s1v2"); s1sp = sem("s1sp")
    s1vw = [sem("s1vwA"), sem("s1vwB")]; s1pb = sem("s1pb"); s1scb = sem("s1scb"); s1spb = sem("s1spb"); s1spv = sem("s1spv")
    s2kv = [sem("s2kvA"), sem("s2kvB")]; s2st = sem("s2st"); s2pv = sem("s2pv"); s2sc = sem("s2sc")
    s2yr = sem("s2yr"); s2l = sem("s2l"); s2r = sem("s2r"); s2g1 = sem("s2g1")
    s2g2 = sem("s2g2"); s2bc = sem("s2bc"); s2mu = sem("s2mu")
    s2wo = sem("s2wo")
    s3x = sem("s3x"); s3pe = sem("s3pe"); s3ra = sem("s3ra"); s3sq = sem("s3sq")
    s3v1 = sem("s3v1"); s3v2 = sem("s3v2"); s3no = sem("s3no")
    s3spf = sem("s3spf"); s3spb = sem("s3spb")
    s4xt = [sem("s4xtA"), sem("s4xtB")]; s4w = [sem("s4wA"), sem("s4wB")]; s4dq = sem("s4dq"); s4pe = sem("s4pe")
    s4sil = sem("s4sil"); s4stt = sem("s4stt"); s4wd = [sem("s4wdA"), sem("s4wdB")]; s4wq = sem("s4wq")
    s4dn = sem("s4dn"); s4de = sem("s4de"); s4x = sem("s4x"); s4ra = sem("s4ra")
    s4sq = sem("s4sq"); s4v1 = sem("s4v1"); s4v2 = sem("s4v2")
    s4no = sem("s4no"); s4out = [sem("s4outA"), sem("s4outB")]

    with nc.Block() as blk:

        @blk.sync
        def _(s):
            s.dma_start(out=COS, in_=cos_h.ap()).then_inc(s0c, 16)
            s.dma_start(out=GTB, in_=g_h.ap()).then_inc(s0c, 16)
            s.dma_start(out=ONB, in_=ones_h.ap()).then_inc(s0c, 16)
            s.dma_start(out=ONR, in_=onesr_h.ap()).then_inc(s0c, 16)
            s.dma_start(out=CVC, in_=cvec_h.ap()).then_inc(s0c, 16)
            s.dma_start(out=SGC, in_=sg_h).then_inc(s0c, 16)
            s.dma_start(out=SUC, in_=su_h).then_inc(s0c, 16)
            s.dma_start(out=SDB, in_=sd_h).then_inc(s0c, 16)
            s.dma_start(out=f32(OFF_SPC, 128, 320), in_=zeros_h.ap()).then_inc(s0c, 16)
            if PH < 1:
                s.wait_ge(s0x, 16 * NDI)
                s.wait_ge(s0sd, 1)
                return
            for j in range(32):
                par = j % 2
                s.wait_ge(s1v1, j + 1)
                if j >= 2:
                    s.wait_ge(s1v2, j - 1)
                s.dma_start(out=RT2[par][0:32, :], in_=TMG[par][32:64, :]).then_inc(s1sw, 16)
                s.dma_start(out=RT2[par][32:64, :], in_=TMG[par][0:32, :]).then_inc(s1sw, 16)
                if j >= 16:
                    ot = j - 16
                    s.wait_ge(s1v2, j + 1)
                    if ot >= 1:
                        s.wait_ge(s1sp, 16 * ot)
                    s.dma_start(out=kt_sp[ot], in_=KST[par]).then_inc(s1sp, 16)
            for i2 in range(32):
                i = 2 * i2
                j2, t2 = divmod(i, 4)
                dc, q = divmod(j2, 4)
                s.wait_ge(s1scb, i + 1)
                if i2 >= 1:
                    s.wait_ge(s1spb, 16 * i2)
                s.dma_start(out=v_sp[q * 4 + t2][:, dc * 512:(dc + 1) * 512],
                            in_=VST[i % 4]).then_inc(s1spb, 16)
            if DEBUG:
                s.wait_ge(s1v2, 32)
                for h in range(NH):
                    s.dma_start(out=qt_dbg[h], in_=QT[:, h * 2048:(h + 1) * 2048]).then_inc(s0d, 16)
            if PH < 2:
                s.wait_ge(s1sp, 16 * 16)
                s.wait_ge(s1spb, 16 * 32)
                s.wait_ge(s1spv, 16 * 32)
                if DEBUG:
                    s.wait_ge(s0d, 16 * NH)
                return
            s.wait_ge(s1sp, 16 * 16)
            s.wait_ge(s1spb, 16 * 32)
            s.wait_ge(s1spv, 16 * 32)
            for h in (0, 1):
                s.dma_start(out=KVK[h % 2], in_=kt_sp[h]).then_inc(s2kv[h % 2], 16)
                s.dma_start(out=KVV[h % 2],
                            in_=v_sp[:, :, h * 128:(h + 1) * 128].rearrange("k p b -> p k b")).then_inc(s2kv[h % 2], 16)
            for m in range(NM):
                h, c = divmod(m, NCH)
                s.wait_ge(s2l, m + 1)
                if m >= 1:
                    s.wait_ge(s2g1, 16 * m)
                if m >= 2:
                    s.wait_ge(s2r, m - 1)
                s.dma_start(out=LQ[m % 2], in_=LSB[m % 2]).then_inc(s2g1, 16)
                if c == NCH - 1 and h + 2 < NH:
                    hh = h + 2
                    s.wait_ge(s2pv, 32 * (hh - 1))
                    s.wait_ge(s2kv[hh % 2], 32 * (hh // 2))
                    s.dma_start(out=KVK[hh % 2], in_=kt_sp[hh]).then_inc(s2kv[hh % 2], 16)
                    s.dma_start(out=KVV[hh % 2],
                                in_=v_sp[:, :, hh * 128:(hh + 1) * 128].rearrange("k p b -> p k b")).then_inc(s2kv[hh % 2], 16)
            if DEBUG:
                s.wait_ge(s2mu, NM)
                for h in range(NH):
                    s.dma_start(out=yt_dbg[h], in_=YT[:, h * 2048:(h + 1) * 2048]).then_inc(s0d, 16)
            if PH < 3:
                s.wait_ge(s2mu, NM)
                if DEBUG:
                    s.wait_ge(s0d, 16 * 2 * NH)
                return
            s.wait_ge(s2mu, NM)
            for it in range(NT + 4):
                if it < NT:
                    if it >= 2:
                        s.wait_ge(s3ra, it - 1)
                        s.wait_ge(s3sq, it - 1)
                    if it >= 1:
                        s.wait_ge(s3x, 16 * it)
                    s.dma_start(out=XRS[it % 2], in_=x_h[it * 128:(it + 1) * 128, :]).then_inc(s3x, 16)
                sp = it - 3
                if 0 <= sp < NT:
                    s.wait_ge(s3no, sp + 1)
                    if sp >= 1:
                        s.wait_ge(s3spf, 16 * sp)
                    s.dma_start(out=x1f[sp * 128:(sp + 1) * 128, :], in_=X1O[sp % 3]).then_inc(s3spf, 16)
            if PH < 4:
                s.wait_ge(s3spf, 16 * NT)
                s.wait_ge(s3spb, 16 * NT)
                if DEBUG:
                    s.wait_ge(s0d, 16 * 2 * NH)
                return
            for th in range(2):
                if th == 0:
                    s.wait_ge(s3pe, NT)
                    s.wait_ge(s3spb, 16 * 8)
                    if DEBUG:
                        s.wait_ge(s0d, 16 * 2 * NH)
                else:
                    s.wait_ge(s4out[0], 16 * 4)
                    s.wait_ge(s4out[1], 16 * 4)
                for dt in range(NDI):
                    gi = th * NDI + dt
                    if th == 1 and dt == 0:
                        s.wait_ge(s3spb, 16 * NT)
                    if gi >= 2:
                        s.wait_ge(s4xt[gi % 2], 16 * (gi // 2))
                    s.dma_start_transpose(
                        X1T[:, dt * 1024:(dt + 1) * 1024],
                        x1b[th * 1024:(th + 1) * 1024, dt * 128:(dt + 1) * 128],
                    ).then_inc(s4xt[gi % 2], 16)
                if th == 0:
                    s.wait_ge(s3spf, 16 * NT)
                for it in range(8 + 2):
                    if it >= 2:
                        g2 = th * 8 + it - 2
                        s.wait_ge(s4no, g2 + 1)
                        if g2 >= 2:
                            s.wait_ge(s4out[g2 % 2], 16 * (g2 // 2))
                        s.dma_start(out=out_h[g2 * 128:(g2 + 1) * 128, :],
                                    in_=XR4[g2 % 4]).then_inc(s4out[g2 % 2], 16)
                    if it < 8:
                        g = th * 8 + it
                        if it < 4:
                            s.wait_ge(s4pe, 2 * NHT * (th + 1))
                        if it >= 4:
                            s.wait_ge(s4ra, g - 3)
                            j = g - 3
                            s.wait_ge(s4out[0], 16 * (j // 2 + 1))
                            s.wait_ge(s4out[1], 16 * ((j + 1) // 2))
                        if g >= 1:
                            s.wait_ge(s4x, 16 * g)
                        s.dma_start(out=XR4[g % 4], in_=x1f[g * 128:(g + 1) * 128, :]).then_inc(s4x, 16)
            s.wait_ge(s4out[0], 16 * 8)
            s.wait_ge(s4out[1], 16 * 8)
            if DEBUG:
                s.wait_ge(s0d, 16 * (2 * NH))

        @blk.gpsimd
        def _(g):
            for dt in range(4):
                g.dma_start(out=XT[:, dt * 2048:(dt + 1) * 2048],
                            in_=xT_h[dt * 128:(dt + 1) * 128, :]).then_inc(s0x, 16)
            if PH >= 1:
                g.dma_start(out=WB[0], in_=wq_h[0]).then_inc(s1w[0], 16)
                g.dma_start(out=WB[1], in_=wq_h[1]).then_inc(s1w[1], 16)
            for b in range(1, 4):
                g.wait_ge(s0x, 16 * 4 * b)
                for dt in range(4 * b, 4 * b + 4):
                    g.dma_start(out=XT[:, dt * 2048:(dt + 1) * 2048],
                                in_=xT_h[dt * 128:(dt + 1) * 128, :]).then_inc(s0x, 16)
            if PH < 1:
                return
            g.dma_start(out=VW[0], in_=wv_h[0]).then_inc(s1vw[0], 16)
            for j in range(2, 32):
                src = wq_h[j] if j < 16 else wk_h[j - 16]
                g.wait_ge(s1pe, j - 1)
                g.wait_ge(s1w[j % 2], 16 * (j // 2))
                g.dma_start(out=WB[j % 2], in_=src).then_inc(s1w[j % 2], 16)
            for dc in range(1, 4):
                if dc == 1:
                    g.wait_ge(s1pe, 32)
                    g.wait_ge(s1sp, 16 * 16)
                if dc >= 2:
                    g.wait_ge(s1pb, 4 * (dc - 1))
                    g.wait_ge(s1vw[dc % 2], 16 * (dc // 2))
                g.dma_start(out=VW[dc % 2], in_=wv_h[dc]).then_inc(s1vw[dc % 2], 16)
            if PH < 2:
                return
            for m in range(NM):
                h, c = divmod(m, NCH)
                g.wait_ge(s2r, m + 1)
                if m >= 1:
                    g.wait_ge(s2g2, 16 * m)
                if m >= 2:
                    g.wait_ge(s2bc, m - 1)
                g.dma_start(out=RLRB[m % 2], in_=RLQ[m % 2]).then_inc(s2g2, 16)
                if c == NCH - 1:
                    g.wait_ge(s2pv, 32 * (h + 1))
                    if DEBUG and h == 0:
                        g.wait_ge(s0d, 16 * NH)
                    if h >= 1:
                        g.wait_ge(s2wo, 16 * h)
                    g.dma_start(out=WOT[:, h * 2048:(h + 1) * 2048],
                                in_=wo_h[h * 128:(h + 1) * 128, :]).then_inc(s2wo, 16)
            if PH < 3:
                return
            for tt in range(NT):
                g.wait_ge(s3no, tt + 1)
                if tt >= 1:
                    g.wait_ge(s3spb, 16 * tt)
                g.dma_start(out=x1b[tt * 128:(tt + 1) * 128, :], in_=X1O[tt % 3]).then_inc(s3spb, 16)
                if tt == 12:
                    g.wait_ge(s3pe, NT)
                    g.wait_ge(s3ra, NT - 1)
                    g.wait_ge(s3sq, NT - 1)
                    for i0 in range(2):
                        g.dma_start(out=GBF[i0 % 2], in_=wg_h[i0]).then_inc(s4w[i0 % 2], 16)
                        g.dma_start(out=UBF[i0 % 2], in_=wu_h[i0]).then_inc(s4w[i0 % 2], 16)
            if PH < 4:
                return
            for th in range(2):
                for ht in range(NHT):
                    i = th * NHT + ht
                    if i < 2:
                        continue
                    if th == 1 and ht == 0:
                        g.wait_ge(s4ra, 8)
                    g.wait_ge(s4dq, 2 * (i - 1))
                    g.wait_ge(s4w[i % 2], 32 * (i // 2))
                    g.dma_start(out=GBF[i % 2], in_=wg_h[ht]).then_inc(s4w[i % 2], 16)
                    g.dma_start(out=UBF[i % 2], in_=wu_h[ht]).then_inc(s4w[i % 2], 16)
                for dc in range(4):
                    for h2 in range(NHT):
                        i = (th * 4 + dc) * NHT + h2
                        if i >= 2:
                            g.wait_ge(s4wq, i - 1)
                            g.wait_ge(s4wd[i % 2], 16 * (i // 2))
                        g.dma_start(out=WDB[i % 2], in_=wd_h[h2, dc]).then_inc(s4wd[i % 2], 16)

        @blk.tensor
        def _(t):
            t.wait_ge(s0c, 16 * 9)
            if PH < 1:
                return
            for j in range(32):
                t.wait_ge(s1w[j % 2], 16 * (j // 2 + 1))
                if j >= 2:
                    t.wait_ge(s1sc, j - 1)
                grp = (j % 2) * 4
                for di in range(NDI):
                    if j == 0 and di % 4 == 0:
                        t.wait_ge(s0x, 16 * min(4 * (di // 4 + 1), NDI))
                    for cch in range(4):
                        mm = t.matmul(bank(grp + cch)[:, 0:512],
                                      WB[j % 2][:, di * 128:(di + 1) * 128],
                                      XT[:, di * 2048 + cch * 512: di * 2048 + (cch + 1) * 512],
                                      start=(di == 0), stop=(di == NDI - 1))
                mm.then_inc(s1pe, 1)
            t.wait_ge(s1sc, 32)
            for j2 in range(16):
                dc, q = divmod(j2, 4)
                if q == 0:
                    t.wait_ge(s1vw[dc % 2], 16 * (dc // 2 + 1))
                if j2 >= 2:
                    t.wait_ge(s1scb, 4 * (j2 - 1))
                for di in range(NDI):
                    for t2 in range(4):
                        tg = q * 4 + t2
                        mm = t.matmul(bank((j2 % 2) * 4 + t2)[:, 0:512],
                                      XT[:, di * 2048 + tg * 128: di * 2048 + (tg + 1) * 128],
                                      VW[dc % 2][:, di * 512:(di + 1) * 512],
                                      start=(di == 0), stop=(di == NDI - 1))
                mm.then_inc(s1pb, 1)
            if PH < 2:
                return
            for m in range(NM):
                h, c = divmod(m, NCH)
                if c == 0:
                    t.wait_ge(s2kv[h % 2], 32 * (h // 2 + 1))
                if m >= 2:
                    t.wait_ge(s2yr, m - 1)
                    t.wait_ge(s2mu, m - 1)
                qs = h * 2048 + c * 512
                for u in range(8):
                    k0, k1 = 2 * u, 2 * u + 1
                    slot = u % 2
                    if u >= 2:
                        t.wait_ge(s2sc, 8 * m + u - 1)
                    elif m >= 1:
                        t.wait_ge(s2sc, 8 * (m - 1) + 7 + (1 if u == 1 else 0))
                    t.matmul(bank(2 * slot)[:, 0:512],
                             KVK[h % 2][:, k0 * 128:(k0 + 1) * 128],
                             QT[:, qs:qs + 512], start=True, stop=True)
                    t.matmul(bank(2 * slot + 1)[:, 0:512],
                             KVK[h % 2][:, k1 * 128:(k1 + 1) * 128],
                             QT[:, qs:qs + 512], start=True,
                             stop=True).then_inc(s2st, 1)
                    if u >= 1:
                        up = u - 1
                        t.wait_ge(s2sc, 8 * m + up + 1)
                        for kk in (2 * up, 2 * up + 1):
                            t.matmul(bank(4 + m % 2)[:, 0:512],
                                     KVV[h % 2][:, kk * 128:(kk + 1) * 128],
                                     PT[up % 4][:, (kk % 2) * 512:(kk % 2 + 1) * 512],
                                     start=(kk == 0), stop=False)
                            mm = t.matmul(bank(6 + m % 2)[0:1, 0:512],
                                          ONB[:, 0:1],
                                          PT[up % 4][:, (kk % 2) * 512:(kk % 2 + 1) * 512],
                                          start=(kk == 0), stop=False)
                        mm.then_inc(s2pv, 1)
                t.wait_ge(s2sc, 8 * m + 8)
                for kk in (14, 15):
                    t.matmul(bank(4 + m % 2)[:, 0:512],
                             KVV[h % 2][:, kk * 128:(kk + 1) * 128],
                             PT[3][:, (kk % 2) * 512:(kk % 2 + 1) * 512],
                             start=False, stop=(kk == 15))
                    mm = t.matmul(bank(6 + m % 2)[0:1, 0:512],
                                  ONB[:, 0:1],
                                  PT[3][:, (kk % 2) * 512:(kk % 2 + 1) * 512],
                                  start=False, stop=(kk == 15))
                mm.then_inc(s2pv, 1)
                if m >= 1:
                    t.wait_ge(s2g2, 16 * m)
                    t.matmul(bank(6 + (m - 1) % 2)[:, 0:512], ONRB, RLRB[(m - 1) % 2],
                             start=True, stop=True).then_inc(s2bc, 1)
            t.wait_ge(s2g2, 16 * NM)
            t.matmul(bank(6 + (NM - 1) % 2)[:, 0:512], ONRB, RLRB[(NM - 1) % 2],
                     start=True, stop=True).then_inc(s2bc, 1)
            if PH < 3:
                return
            t.wait_ge(s2mu, NM)
            for tt in range(NT):
                if tt >= 2:
                    t.wait_ge(s3ra, tt - 1)
                    t.wait_ge(s3sq, tt - 1)
                for y in range(NH):
                    t.wait_ge(s2wo, 16 * (y + 1))
                    for cch in range(4):
                        mm = t.matmul(bank((tt % 2) * 4 + cch)[:, 0:512],
                                      YT[:, y * 2048 + tt * 128: y * 2048 + (tt + 1) * 128],
                                      WOT[:, y * 2048 + cch * 512: y * 2048 + (cch + 1) * 512],
                                      start=(y == 0), stop=(y == NH - 1))
                mm.then_inc(s3pe, 1)
            if PH < 4:
                return
            t.wait_ge(s3ra, NT)
            for th in range(2):
                t.wait_ge(s4xt[0], 16 * (NDI // 2) * (th + 1))
                t.wait_ge(s4xt[1], 16 * (NDI // 2) * (th + 1))
                if th == 0:
                    t.wait_ge(s3spf, 16 * NT)
                    t.wait_ge(s3spb, 16 * NT)
                for ht in range(NHT):
                    i = th * NHT + ht
                    t.wait_ge(s4dq, 2 * (i + 1))
                    if i >= 2:
                        t.wait_ge(s4stt, i - 1)
                    par = i % 2
                    for di in range(NDI):
                        for cch in range(2):
                            mm = t.matmul(bank(par * 4 + cch)[:, 0:512],
                                          GDQ[par][:, di * 128:(di + 1) * 128],
                                          X1T[:, di * 1024 + cch * 512: di * 1024 + (cch + 1) * 512],
                                          start=(di == 0), stop=(di == NDI - 1))
                    mm.then_inc(s4pe, 1)
                    for di in range(NDI):
                        for cch in range(2):
                            mm = t.matmul(bank(par * 4 + 2 + cch)[:, 0:512],
                                          UDQ[par][:, di * 128:(di + 1) * 128],
                                          X1T[:, di * 1024 + cch * 512: di * 1024 + (cch + 1) * 512],
                                          start=(di == 0), stop=(di == NDI - 1))
                    mm.then_inc(s4pe, 1)
                t.wait_ge(s4stt, NHT * (th + 1))
                for dc in range(4):
                    gidx = th * 4 + dc
                    if gidx >= 1:
                        t.wait_ge(s4de, 8 * gidx)
                    for h2 in range(NHT):
                        i = gidx * NHT + h2
                        t.wait_ge(s4wq, i + 1)
                        for tt in range(8):
                            mm = t.matmul(bank(tt)[:, 0:512],
                                          HT[:, h2 * 1024 + tt * 128: h2 * 1024 + (tt + 1) * 128],
                                          WDQ[i % 2][:, 0:512],
                                          start=(h2 == 0), stop=(h2 == NHT - 1))
                        mm.then_inc(s4dn, 1)

        @blk.scalar
        def _(sc):
            zero_b = CVC[:, 3:4]
            if PH < 1:
                return
            for j in range(32):
                sc.wait_ge(s1pe, j + 1)
                if j >= 18:
                    sc.wait_ge(s1sp, 16 * (j - 17))
                grp = (j % 2) * 4
                dst = QT[:, (j % 16) * 2048:((j % 16) + 1) * 2048] if j < 16 else KST[j % 2]
                sc.copy(dst, bank(grp, 4)[:, 0:2048]).then_inc(s1sc, 1)
            def vspill(i):
                j2s, t2s = divmod(i, 4)
                dcs, qs = divmod(j2s, 4)
                sc.wait_ge(s1scb, i + 1)
                if i >= 3:
                    sc.wait_ge(s1spv, 16 * ((i - 1) // 2))
                sc.dma_start(out=v_sp[qs * 4 + t2s][:, dcs * 512:(dcs + 1) * 512],
                             in_=VST[i % 4]).then_inc(s1spv, 16)
            for j2 in range(16):
                sc.wait_ge(s1pb, j2 + 1)
                for t2 in range(4):
                    i = j2 * 4 + t2
                    if i >= 4:
                        sc.wait_ge(s1spb if i % 2 == 0 else s1spv,
                                   16 * ((i - 4) // 2 + 1))
                    sc.copy(VST[i % 4], bank((j2 % 2) * 4 + t2)[:, 0:512]).then_inc(s1scb, 1)
                    if i >= 2 and i % 2 == 0:
                        vspill(i - 1)
            vspill(63)
            if PH < 2:
                return
            sc.wait_ge(s1sp, 16 * 16)
            for m in range(NM):
                for u in range(8):
                    sc.wait_ge(s2st, 8 * m + u + 1)
                    if 8 * m + u >= 4:
                        sc.wait_ge(s2pv, 8 * m + u - 3)
                    sc.activation(PT[u % 4], bank(2 * (u % 2), 2)[:, 0:1024],
                                  AF.Exp, bias=CVC[:, 0:1],
                                  scale=ASCL).then_inc(s2sc, 1)
            if PH < 3:
                return
            for it in range(NT + 3):
                if 1 <= it <= NT:
                    t1 = it - 1
                    p1 = t1 % 4
                    sc.wait_ge(s3v1, t1 + 1)
                    sc.activation(stat(p1, S_SD), stat(p1, S_VE), AF.Sqrt)
                if 2 <= it <= NT + 1:
                    t2 = it - 2
                    p2 = t2 % 4
                    sc.wait_ge(s3v2, t2 + 1)
                    if t2 >= 3:
                        sc.wait_ge(s3spf, 16 * (t2 - 2))
                        sc.wait_ge(s3spb, 16 * (t2 - 2))
                    sc.activation(X1O[t2 % 3], ROU[t2 % 3], AF.Identity,
                                  bias=stat(p2, S_NM), scale=stat(p2, S_RS)).then_inc(s3no, 1)
                if it < NT:
                    sc.wait_ge(s3ra, it + 1)
                    sc.activation(bank((it % 2) * 4, 4)[:, 0:2048], ROU[it % 3], AF.Identity,
                                  bias=zero_b, scale=1.0, accum_out=stat(it % 4, S_SUM))
                    sc.activation(XRS[it % 2], ROU[it % 3], AF.Square,
                                  accum_out=stat(it % 4, S_SQ)).then_inc(s3sq, 1)
                elif it == NT:
                    sspace(sc).then_inc(s3sq, 1)
            if PH < 4:
                return
            sc.wait_ge(s3no, NT)
            sc.wait_ge(s3ra, NT)
            sc.wait_ge(s3sq, NT)
            for th in range(2):
                for it in range(NHT + 1):
                    if it < NHT:
                        i = th * NHT + it
                        sc.wait_ge(s4w[i % 2], 32 * (i // 2 + 1))
                        if i >= 2:
                            sc.wait_ge(s4pe, 2 * (i - 1))
                        sc.activation(GDQ[i % 2], GBF[i % 2], AF.Identity, bias=CVC[:, 1:2], scale=1.0)
                        sc.activation(UDQ[i % 2], UBF[i % 2], AF.Identity, bias=CVC[:, 1:2],
                                      scale=1.0).then_inc(s4dq, 2)
                    if it >= 1:
                        i1 = th * NHT + it - 1
                        ht1 = it - 1
                        sc.wait_ge(s4pe, 2 * i1 + 1)
                        sc.activation(GS[i1 % 2], bank((i1 % 2) * 4, 2)[:, 0:1024], AF.Silu,
                                      bias=zero_b, scale=SGC[:, ht1:ht1 + 1]).then_inc(s4sil, 1)
                for it in range(8 + 3):
                    g_ = th * 8 + it
                    if 1 <= it <= 8:
                        g1 = th * 8 + it - 1
                        p1 = g1 % 4
                        sc.wait_ge(s4v1, g1 + 1)
                        sc.activation(stat4(p1, S_SD), stat4(p1, S_VE), AF.Sqrt)
                    if 2 <= it <= 9:
                        g2 = th * 8 + it - 2
                        p2 = g2 % 4
                        sc.wait_ge(s4v2, g2 + 1)
                        sc.activation(XR4[g2 % 4], RO4[g2 % 3], AF.Identity,
                                      bias=stat4(p2, S_NM), scale=stat4(p2, S_RS)).then_inc(s4no, 1)
                    if it < 8:
                        sc.wait_ge(s4ra, g_ + 1)
                        sc.activation(SD4, RO4[g_ % 3], AF.Identity, bias=zero_b, scale=1.0,
                                      accum_out=stat4(g_ % 4, S_SUM))
                        sc.activation(DSB[:, it * 2048:(it + 1) * 2048], RO4[g_ % 3], AF.Square,
                                      accum_out=stat4(g_ % 4, S_SQ)).then_inc(s4sq, 1)
                    elif it == 8:
                        sspace(sc).then_inc(s4sq, 1)

        @blk.vector
        def _(v):
            if PH < 1:
                return
            for j in range(32):
                par = j % 2
                v.wait_ge(s1sc, j + 1)
                if j >= 2:
                    v.wait_ge(s1sw, 32 * (j - 1))
                blk_ap = QT[:, (j % 16) * 2048:((j % 16) + 1) * 2048] if j < 16 else KST[par]
                v.tensor_tensor(RT1[par], blk_ap[0:64, :], COS, AL.mult)
                v.tensor_tensor(TMG[par], blk_ap[0:64, :], GTB, AL.mult).then_inc(s1v1, 1)
                v.wait_ge(s1sw, 32 * (j + 1))
                v.tensor_tensor(blk_ap[0:64, :], RT1[par], RT2[par], AL.add).then_inc(s1v2, 1)
            if PH < 2:
                return
            for m in range(NM + 1):
                if m < NM:
                    v.wait_ge(s2pv, 8 * m + 8)
                    if m >= 2:
                        v.wait_ge(s2mu, m - 1)
                    v.tensor_copy(YRW[m % 2], bank(4 + m % 2)[:, 0:512]).then_inc(s2yr, 1)
                    if m >= 2:
                        v.wait_ge(s2g1, 16 * (m - 1))
                    v.tensor_copy(LSB[m % 2], bank(6 + m % 2)[0:1, 0:512]).then_inc(s2l, 1)
                    v.wait_ge(s2g1, 16 * (m + 1))
                    if m >= 2:
                        v.wait_ge(s2g2, 16 * (m - 1))
                    v.reciprocal(RLQ[m % 2], LQ[m % 2])
                    vspace(v).then_inc(s2r, 1)
                if 1 <= m:
                    mp = m - 1
                    hh, cc = divmod(mp, NCH)
                    v.wait_ge(s2bc, mp + 1)
                    v.tensor_tensor(YT[:, hh * 2048 + cc * 512: hh * 2048 + (cc + 1) * 512],
                                    YRW[mp % 2], bank(6 + mp % 2)[:, 0:512],
                                    AL.mult).then_inc(s2mu, 1)
            if PH < 3:
                return
            for it in range(NT + 3):
                if it < NT:
                    v.wait_ge(s3pe, it + 1)
                    v.wait_ge(s3x, 16 * (it + 1))
                    if it >= 3:
                        v.wait_ge(s3no, it - 2)
                    v.tensor_tensor(ROU[it % 3], bank((it % 2) * 4, 4)[:, 0:2048],
                                    XRS[it % 2], AL.add).then_inc(s3ra, 1)
                if 1 <= it <= NT:
                    t1 = it - 1
                    p1 = t1 % 4
                    v.wait_ge(s3sq, t1 + 1)
                    vstep(v, v.tensor_tensor(stat(p1, S_S2), stat(p1, S_SUM), stat(p1, S_SUM), AL.mult))
                    vstep(v, v.tensor_scalar(stat(p1, S_MN), stat(p1, S_SUM), 1.0 / D, None, AL.mult))
                    vstep(v, v.scalar_tensor_tensor(stat(p1, S_VN), stat(p1, S_S2), -1.0 / D,
                                                    stat(p1, S_SQ), AL.mult, AL.add))
                    vstep(v, v.tensor_scalar(stat(p1, S_VE), stat(p1, S_VN), 1.0 / D, EPS,
                                             AL.mult, AL.add))
                    vspace(v).then_inc(s3v1, 1)
                if 2 <= it <= NT + 1:
                    t2 = it - 2
                    p2 = t2 % 4
                    v.wait_ge(s3sq, min(t2 + 2, NT + 1))
                    vstep(v, v.reciprocal(stat(p2, S_RS), stat(p2, S_SD)))
                    vstep(v, v.scalar_tensor_tensor(stat(p2, S_NM), stat(p2, S_MN), -1.0,
                                                    stat(p2, S_RS), AL.mult, AL.mult))
                    vspace(v).then_inc(s3v2, 1)
            if PH < 4:
                return
            for th in range(2):
                for ht in range(NHT):
                    i = th * NHT + ht
                    v.wait_ge(s4sil, i + 1)
                    v.wait_ge(s4pe, 2 * i + 2)
                    v.scalar_tensor_tensor(HT[:, ht * 1024:(ht + 1) * 1024],
                                           bank((i % 2) * 4 + 2, 2)[:, 0:1024],
                                           SUC[:, ht:ht + 1], GS[i % 2],
                                           AL.mult, AL.mult).then_inc(s4stt, 1)
                for dc in range(4):
                    gidx = th * 4 + dc
                    for h2 in range(NHT):
                        i = gidx * NHT + h2
                        v.wait_ge(s4wd[i % 2], 16 * (i // 2 + 1))
                        if i >= 2:
                            v.wait_ge(s4dn, i - 1)
                        v.tensor_scalar(WDQ[i % 2], WDB[i % 2], -128.0, None,
                                        AL.add).then_inc(s4wq, 1)
                    v.wait_ge(s4dn, (gidx + 1) * NHT)
                    for tt in range(8):
                        v.tensor_tensor(DSB[:, tt * 2048 + dc * 512: tt * 2048 + (dc + 1) * 512],
                                        bank(tt)[:, 0:512], SDB[:, dc * 512:(dc + 1) * 512],
                                        AL.mult).then_inc(s4de, 1)
                for it in range(8 + 3):
                    g_ = th * 8 + it
                    if 2 <= it <= 9:
                        g2 = th * 8 + it - 2
                        p2 = g2 % 4
                        v.wait_ge(s4sq, th * 9 + min(it, 9))
                        vstep(v, v.reciprocal(stat4(p2, S_RS), stat4(p2, S_SD)))
                        vstep(v, v.scalar_tensor_tensor(stat4(p2, S_NM), stat4(p2, S_MN), -1.0,
                                                        stat4(p2, S_RS), AL.mult, AL.mult))
                        vspace(v).then_inc(s4v2, 1)
                    if 1 <= it <= 8:
                        g1 = th * 8 + it - 1
                        p1 = g1 % 4
                        v.wait_ge(s4sq, th * 9 + it)
                        vstep(v, v.tensor_tensor(stat4(p1, S_S2), stat4(p1, S_SUM), stat4(p1, S_SUM), AL.mult))
                        vstep(v, v.tensor_scalar(stat4(p1, S_MN), stat4(p1, S_SUM), 1.0 / D, None, AL.mult))
                        vstep(v, v.scalar_tensor_tensor(stat4(p1, S_VN), stat4(p1, S_S2), -1.0 / D,
                                                        stat4(p1, S_SQ), AL.mult, AL.add))
                        vstep(v, v.tensor_scalar(stat4(p1, S_VE), stat4(p1, S_VN), 1.0 / D, EPS,
                                                 AL.mult, AL.add))
                        vspace(v).then_inc(s4v1, 1)
                    if it < 8:
                        v.wait_ge(s4x, 16 * (g_ + 1))
                        v.wait_ge(s4de, 32 * (th + 1))
                        if g_ >= 3:
                            v.wait_ge(s4no, g_ - 2)
                        v.tensor_tensor(RO4[g_ % 3], DSB[:, it * 2048:(it + 1) * 2048],
                                        XR4[g_ % 4], AL.add).then_inc(s4ra, 1)

    return nc


def _host_prep(inputs):
    x = np.ascontiguousarray(np.asarray(inputs["L0_attn_x"], dtype=np.float32))
    wq = np.asarray(inputs["L0_attn_wq"], dtype=np.float32)
    wk = np.asarray(inputs["L0_attn_wk"], dtype=np.float32)
    wv = np.asarray(inputs["L0_attn_wv"], dtype=np.float32)
    wo = np.asarray(inputs["L0_attn_wo"], dtype=np.float32)
    wg = np.asarray(inputs["L0_ffn_wq_gate"], dtype=np.int32)
    wu = np.asarray(inputs["L0_ffn_wq_up"], dtype=np.int32)
    wd = np.asarray(inputs["L0_ffn_wq_down"], dtype=np.int32)
    sg = np.asarray(inputs["L0_ffn_s_gate"], dtype=np.float32)
    su = np.asarray(inputs["L0_ffn_s_up"], dtype=np.float32)
    sd = np.asarray(inputs["L0_ffn_s_down"], dtype=np.float32)

    def wblk(w):
        return np.ascontiguousarray(
            w.T.reshape(NDI, 128, NT, 128).transpose(2, 1, 0, 3).reshape(NT, 128, D))

    shared = {
        "wq_c": wblk(wq), "wk_c": wblk(wk),
        "wv_c": np.ascontiguousarray(
            wv.T.reshape(NDI, 128, 4, 512).transpose(2, 1, 0, 3).reshape(4, 128, NDI * 512)),
        "wo_T": np.ascontiguousarray(wo.T),
        "wg_c": np.ascontiguousarray(
            wg.T.reshape(NDI, 128, NHT, 128).transpose(2, 1, 0, 3).reshape(NHT, 128, 2048)),
        "wu_c": np.ascontiguousarray(
            wu.T.reshape(NDI, 128, NHT, 128).transpose(2, 1, 0, 3).reshape(NHT, 128, 2048)),
        "wd_c": np.ascontiguousarray(
            wd.T.reshape(NHT, 128, 4, 512).transpose(0, 2, 1, 3)),
        "sg_c": np.ascontiguousarray(sg.reshape(NHT, 128).T),
        "su_c": np.ascontiguousarray(su.reshape(NHT, 128).T),
        "sd_b": np.ascontiguousarray(np.tile(sd.reshape(1, 2048), (128, 1))),
    }
    in_maps = []
    for b in range(8):
        m = dict(shared)
        m["x"] = np.ascontiguousarray(x[b])
        m["xT"] = np.ascontiguousarray(x[b].T)
        in_maps.append(m)
    return in_maps


def kernel(**inputs):
    in_maps = _host_prep(inputs)
    nc = build_program()
    res = run_bass_kernel_spmd(nc, in_maps, core_ids=list(range(8)))
    return np.stack([r["out"] for r in res.results])


# version 07
# speedup vs baseline: 1.1093x; 1.0264x over previous
import math
import numpy as np
import ml_dtypes
import concourse.bass as bass
import concourse.mybir as mybir
from concourse.bass_utils import run_bass_kernel_spmd

F32 = mybir.dt.float32
BF16 = mybir.dt.bfloat16
I32 = mybir.dt.int32
AF = mybir.ActivationFunctionType
AL = mybir.AluOpType
AX = mybir.AxisListType
bfdt = ml_dtypes.bfloat16

T = 2048
D = 2048
NH = 16
HD = 128
HID = 5632
NT = 16
NDI = 16
NHT = 44
QC = 512
NCH = 4
NM = NH * NCH
C_SHIFT = 24.0
ASCL = 1.0 / math.sqrt(HD)
EPS = 1e-5

DEBUG = False
PH = 4

ARENA_F32_COLS = 53040

OFF_SG    = 202704
OFF_SU    = 202896
OFF_CVEC  = 203088
OFF_ONESR = 203120
OFF_ONESB = 203632
OFF_SDB   = 203888
OFF_SPC   = 200704
OFF_SPC2  = 201728

OFF_XT    = 0
OFF_QT    = 65536
OFF_COS   = 131072
OFF_SING  = 135168
OFF_WBUF  = 139264
OFF_KSTG  = 147456
OFF_RT1   = 155648
OFF_TMPG  = 163840
OFF_RT2   = 172032
OFF_VW    = 180224
OFF_VSTG  = 196608

OFF_YT    = 0
OFF_KVK   = 131072
OFF_KVV   = 139264
OFF_PT    = 147456
OFF_LSB   = 155648
OFF_LQ    = 160000
OFF_RLQ   = 160064
OFF_RLROW = 160128
OFF_YRAW  = 164224

OFF_XRES  = 131072
OFF_ROUT  = 147456
OFF_STAT  = 172032
OFF_X1O   = 172608

OFF_X1T   = 0
OFF_HT    = 32768
OFF_GBUF  = 122880
OFF_UBUF  = 131072
OFF_GDQ   = 139264
OFF_UDQ   = 147456
OFF_GS    = 155648
OFF_DSB   = 122880
OFF_WDB   = 188416
OFF_WDQ   = 190464
OFF_STAT4 = 192512
OFF_XRES4 = 0
OFF_ROUT4 = 32768
OFF_SUMD4 = 57344
OFF_SUMD3 = 188992

S_SUM, S_SQ, S_S2, S_VN, S_VE, S_MN, S_SD, S_RS, S_NM = range(9)


def build_program():
    nc = bass.Bass("TRN2", target_bir_lowering=False)

    x_h   = nc.dram_tensor("x",   [T, D], F32, kind="ExternalInput").ap()
    xT_h  = nc.dram_tensor("xT",  [D, T], F32, kind="ExternalInput").ap()
    wq_h  = nc.dram_tensor("wq_c", [NT, 128, D], F32, kind="ExternalInput").ap()
    wk_h  = nc.dram_tensor("wk_c", [NT, 128, D], F32, kind="ExternalInput").ap()
    wv_h  = nc.dram_tensor("wv_c", [4, 128, NDI * 512], F32, kind="ExternalInput").ap()
    wo_h  = nc.dram_tensor("wo_T", [D, D], F32, kind="ExternalInput").ap()
    wg_h  = nc.dram_tensor("wg_c", [NHT, 128, 2048], I32, kind="ExternalInput").ap()
    wu_h  = nc.dram_tensor("wu_c", [NHT, 128, 2048], I32, kind="ExternalInput").ap()
    wd_h  = nc.dram_tensor("wd_c", [NHT, 4, 128, 512], I32, kind="ExternalInput").ap()
    sg_h  = nc.dram_tensor("sg_c", [128, NHT], F32, kind="ExternalInput").ap()
    su_h  = nc.dram_tensor("su_c", [128, NHT], F32, kind="ExternalInput").ap()
    sd_h  = nc.dram_tensor("sd_b", [128, 2048], F32, kind="ExternalInput").ap()
    out_h = nc.dram_tensor("out", [T, D], F32, kind="ExternalOutput").ap()

    scr_kind = "ExternalOutput" if DEBUG else "Internal"
    kt_sp = nc.dram_tensor("kt_sp", [NH, 128, T], BF16, kind=scr_kind).ap()
    v_sp  = nc.dram_tensor("v_sp",  [NT, 128, D], BF16, kind=scr_kind).ap()
    x1f   = nc.dram_tensor("x1f", [T, D], F32, kind=scr_kind).ap()
    x1b   = nc.dram_tensor("x1b", [T, D], BF16, kind=scr_kind).ap()
    qt_dbg = nc.dram_tensor("qt_dbg", [NH, 128, T], BF16, kind="ExternalOutput").ap() if DEBUG else None
    yt_dbg = nc.dram_tensor("yt_dbg", [NH, 128, T], BF16, kind="ExternalOutput").ap() if DEBUG else None

    inv_freq = (1.0 / 10000.0 ** (np.arange(0, HD, 2, dtype=np.float32) / HD)).astype(np.float32)
    ang = np.arange(T, dtype=np.float32)[:, None] * inv_freq[None, :]
    cosT = np.cos(ang).T.astype(bfdt)
    sinT = np.sin(ang).T.astype(np.float32)
    gtab = np.concatenate([sinT[32:64], -sinT[0:32]], axis=0).astype(bfdt)
    cos_h  = nc.inline_tensor(np.ascontiguousarray(cosT), name="cosT")
    g_h    = nc.inline_tensor(np.ascontiguousarray(gtab), name="gtab")
    ones_h = nc.inline_tensor(np.ones((128, 128), dtype=bfdt), name="onesb")
    onesr_h = nc.inline_tensor(np.ones((1, 128), dtype=np.float32), name="onesr")
    cvec = np.zeros((128, 4), dtype=np.float32)
    cvec[:, 0] = -C_SHIFT
    cvec[:, 1] = -128.0
    cvec[:, 2] = EPS
    cvec_h = nc.inline_tensor(cvec, name="cvec")
    zeros_h = nc.inline_tensor(np.zeros((128, 320), dtype=np.float32), name="zeros")

    AB = nc.alloc_sbuf_tensor("AB", [128, ARENA_F32_COLS], F32)
    PS = nc.alloc_psum_tensor("PS", [128, 4096], F32)
    ps_ap = PS.ap()

    def f32(off_b, p, cols):
        return AB.ap()[0:p, off_b // 4: off_b // 4 + cols]

    def bf16(off_b, p, cols):
        return AB.ap()[0:p, off_b // 4: off_b // 4 + cols // 2].bitcast(BF16)

    def bank(b, n=1):
        return ps_ap[:, b * 512:(b + n) * 512]

    XT  = bf16(OFF_XT, 128, NDI * 2048)
    QT  = bf16(OFF_QT, 128, NH * 2048)
    COS = bf16(OFF_COS, 64, 2048)
    GTB = bf16(OFF_SING, 64, 2048)
    WB  = [bf16(OFF_WBUF + i * 4096, 128, 2048) for i in range(2)]
    KST = [bf16(OFF_KSTG + i * 4096, 128, 2048) for i in range(2)]
    RT1 = [bf16(OFF_RT1 + i * 4096, 64, 2048) for i in range(2)]
    TMG = [bf16(OFF_TMPG + i * 4096, 64, 2048) for i in range(2)]
    RT2 = [bf16(OFF_RT2 + i * 4096, 64, 2048) for i in range(2)]
    VW  = [bf16(OFF_VW, 128, 8192), bf16(OFF_WBUF, 128, 8192)]
    VST = [bf16(OFF_VSTG + i * 1024, 128, 512) for i in range(4)]

    YT  = bf16(OFF_YT, 128, NH * 2048)
    KVK = [bf16(OFF_KVK + i * 4096, 128, 2048) for i in range(2)]
    KVV = [bf16(OFF_KVV + i * 4096, 128, 2048) for i in range(2)]
    PT  = [bf16(OFF_PT + i * 2048, 128, 1024) for i in range(4)]
    LSB = [f32(OFF_LSB + i * 2048, 1, 512) for i in range(2)]
    LQ  = [f32(OFF_LQ + i * 16, 128, 4) for i in range(2)]
    RLQ = [f32(OFF_RLQ + i * 16, 128, 4) for i in range(2)]
    RLR = [f32(OFF_RLROW + i * 2048, 1, 512) for i in range(2)]
    YRW = [f32(OFF_YRAW + i * 2048, 128, 512) for i in range(2)]
    WOT = QT

    XRS = [f32(OFF_XRES + i * 8192, 128, 2048) for i in range(2)]
    XR8 = [f32(g * 8192, 128, 2048) for g in range(8)]
    S4B = f32(OFF_STAT4, 128, 80)
    ROU = [f32(OFF_ROUT + i * 8192, 128, 2048) for i in range(3)]
    X1O = [f32(OFF_X1O + i * 8192, 128, 2048) for i in range(3)]

    X1T = bf16(OFF_X1T, 128, NDI * 1024)
    HT  = bf16(OFF_HT, 128, NHT * 1024)
    GBF = [bf16(OFF_GBUF + i * 4096, 128, 2048) for i in range(2)]
    UBF = [bf16(OFF_UBUF + i * 4096, 128, 2048) for i in range(2)]
    GDQ = [bf16(OFF_GDQ + i * 4096, 128, 2048) for i in range(2)]
    UDQ = [bf16(OFF_UDQ + i * 4096, 128, 2048) for i in range(2)]
    GS  = [f32(OFF_GS + i * 4096, 128, 1024) for i in range(2)]
    DSB = f32(OFF_DSB, 128, 8 * 2048)
    WDB = [bf16(OFF_WDB + i * 1024, 128, 512) for i in range(2)]
    WDQ = [bf16(OFF_WDQ + i * 1024, 128, 512) for i in range(2)]
    RO4 = [f32(OFF_ROUT4 + i * 8192, 128, 2048) for i in range(3)]
    XR4 = [f32(OFF_XRES4 + i * 8192, 128, 2048) for i in range(4)]
    SD3 = f32(OFF_SUMD3, 128, 2048)
    SD4 = f32(OFF_SUMD4, 128, 2048)

    SGC = f32(OFF_SG, 128, NHT)
    SUC = f32(OFF_SU, 128, NHT)
    CVC = f32(OFF_CVEC, 128, 4)
    ONR = f32(OFF_ONESR, 1, 128)
    ONB = bf16(OFF_ONESB, 128, 128)
    ONRB = bf16(OFF_ONESB, 1, 128)
    RLRB = [bf16(OFF_RLROW + i * 2048, 1, 512) for i in range(2)]
    SDB = f32(OFF_SDB, 128, 2048)
    SPC = f32(OFF_SPC, 128, 256)
    SPC2 = f32(OFF_SPC2, 128, 64)

    vf_sem = nc.alloc_semaphore("vfence")
    vf_n = [0]
    def vstep(v, instr, spacers=2):
        instr.then_inc(vf_sem, 1)
        vf_n[0] += 1
        for _ in range(spacers):
            vspace(v)
        v.wait_ge(vf_sem, vf_n[0])

    vsp_n = [0]
    def vspace(v):
        n = vsp_n[0] % 8
        vsp_n[0] += 1
        return v.tensor_copy(SPC[:, 32 * n:32 * n + 32], SDB[:, 0:32])

    ssp_n = [0]
    def sspace(sc):
        n = ssp_n[0] % 2
        ssp_n[0] += 1
        return sc.copy(SPC2[:, 32 * n:32 * n + 32], SDB[:, 32:64])

    def stat(par, idx):
        return f32(OFF_STAT + par * 144, 128, 36)[:, idx:idx + 1]

    def stat4(par, idx):
        return f32(OFF_STAT4 + par * 144, 128, 36)[:, idx:idx + 1]

    def sem(name):
        return nc.alloc_semaphore(name)

    s0c = sem("s0c"); s0x = sem("s0x"); s0sd = sem("s0sd"); s0d = sem("s0d")
    s1w = [sem("s1wA"), sem("s1wB")]; s1pe = sem("s1pe"); s1sc = sem("s1sc")
    s1v1 = sem("s1v1"); s1sw = sem("s1sw"); s1v2 = sem("s1v2"); s1sp = sem("s1sp")
    s1vw = [sem("s1vwA"), sem("s1vwB")]; s1pb = sem("s1pb"); s1scb = sem("s1scb"); s1spb = sem("s1spb"); s1spv = sem("s1spv")
    s2kv = [sem("s2kvA"), sem("s2kvB")]; s2st = sem("s2st"); s2pv = sem("s2pv"); s2sc = sem("s2sc")
    s2yr = sem("s2yr"); s2l = sem("s2l"); s2r = sem("s2r"); s2g1 = sem("s2g1")
    s2g2 = sem("s2g2"); s2bc = sem("s2bc"); s2mu = sem("s2mu")
    s2wo = sem("s2wo")
    s3x = sem("s3x"); s3pe = sem("s3pe"); s3ra = sem("s3ra"); s3sq = sem("s3sq")
    s3v1 = sem("s3v1"); s3v2 = sem("s3v2"); s3no = sem("s3no")
    s3spf = sem("s3spf"); s3spb = sem("s3spb")
    s4xt = [sem("s4xtA"), sem("s4xtB")]; s4w = [sem("s4wA"), sem("s4wB")]; s4dq = sem("s4dq"); s4pe = sem("s4pe")
    s4sil = sem("s4sil"); s4stt = sem("s4stt"); s4wd = [sem("s4wdA"), sem("s4wdB")]; s4wq = sem("s4wq")
    s4dn = sem("s4dn"); s4de = sem("s4de"); s4x = sem("s4x"); s4ra = sem("s4ra")
    s4sq = sem("s4sq"); s4v1 = sem("s4v1"); s4v2 = sem("s4v2"); s4sd = sem("s4sd")
    s4no = sem("s4no"); s4out = [sem("s4outA"), sem("s4outB")]

    with nc.Block() as blk:

        @blk.sync
        def _(s):
            s.dma_start(out=COS, in_=cos_h.ap()).then_inc(s0c, 16)
            s.dma_start(out=GTB, in_=g_h.ap()).then_inc(s0c, 16)
            s.dma_start(out=ONB, in_=ones_h.ap()).then_inc(s0c, 16)
            s.dma_start(out=ONR, in_=onesr_h.ap()).then_inc(s0c, 16)
            s.dma_start(out=CVC, in_=cvec_h.ap()).then_inc(s0c, 16)
            s.dma_start(out=SGC, in_=sg_h).then_inc(s0c, 16)
            s.dma_start(out=SUC, in_=su_h).then_inc(s0c, 16)
            s.dma_start(out=SDB, in_=sd_h).then_inc(s0c, 16)
            s.dma_start(out=f32(OFF_SPC, 128, 320), in_=zeros_h.ap()).then_inc(s0c, 16)
            if PH < 1:
                s.wait_ge(s0x, 16 * NDI)
                s.wait_ge(s0sd, 1)
                return
            for j in range(32):
                par = j % 2
                s.wait_ge(s1v1, j + 1)
                if j >= 2:
                    s.wait_ge(s1v2, j - 1)
                s.dma_start(out=RT2[par][0:32, :], in_=TMG[par][32:64, :]).then_inc(s1sw, 16)
                s.dma_start(out=RT2[par][32:64, :], in_=TMG[par][0:32, :]).then_inc(s1sw, 16)
                if j >= 16:
                    ot = j - 16
                    s.wait_ge(s1v2, j + 1)
                    if ot >= 1:
                        s.wait_ge(s1sp, 16 * ot)
                    s.dma_start(out=kt_sp[ot], in_=KST[par]).then_inc(s1sp, 16)
            for i2 in range(32):
                i = 2 * i2
                j2, t2 = divmod(i, 4)
                dc, q = divmod(j2, 4)
                s.wait_ge(s1scb, i + 1)
                if i2 >= 1:
                    s.wait_ge(s1spb, 16 * i2)
                s.dma_start(out=v_sp[q * 4 + t2][:, dc * 512:(dc + 1) * 512],
                            in_=VST[i % 4]).then_inc(s1spb, 16)
            if DEBUG:
                s.wait_ge(s1v2, 32)
                for h in range(NH):
                    s.dma_start(out=qt_dbg[h], in_=QT[:, h * 2048:(h + 1) * 2048]).then_inc(s0d, 16)
            if PH < 2:
                s.wait_ge(s1sp, 16 * 16)
                s.wait_ge(s1spb, 16 * 32)
                s.wait_ge(s1spv, 16 * 32)
                if DEBUG:
                    s.wait_ge(s0d, 16 * NH)
                return
            s.wait_ge(s1sp, 16 * 16)
            s.wait_ge(s1spb, 16 * 32)
            s.wait_ge(s1spv, 16 * 32)
            for h in (0, 1):
                s.dma_start(out=KVK[h % 2], in_=kt_sp[h]).then_inc(s2kv[h % 2], 16)
                s.dma_start(out=KVV[h % 2],
                            in_=v_sp[:, :, h * 128:(h + 1) * 128].rearrange("k p b -> p k b")).then_inc(s2kv[h % 2], 16)
            for m in range(NM):
                h, c = divmod(m, NCH)
                s.wait_ge(s2l, m + 1)
                if m >= 1:
                    s.wait_ge(s2g1, 16 * m)
                if m >= 2:
                    s.wait_ge(s2r, m - 1)
                s.dma_start(out=LQ[m % 2], in_=LSB[m % 2]).then_inc(s2g1, 16)
                if c == NCH - 1 and h + 2 < NH:
                    hh = h + 2
                    s.wait_ge(s2pv, 32 * (hh - 1))
                    s.wait_ge(s2kv[hh % 2], 32 * (hh // 2))
                    s.dma_start(out=KVK[hh % 2], in_=kt_sp[hh]).then_inc(s2kv[hh % 2], 16)
                    s.dma_start(out=KVV[hh % 2],
                                in_=v_sp[:, :, hh * 128:(hh + 1) * 128].rearrange("k p b -> p k b")).then_inc(s2kv[hh % 2], 16)
            if DEBUG:
                s.wait_ge(s2mu, NM)
                for h in range(NH):
                    s.dma_start(out=yt_dbg[h], in_=YT[:, h * 2048:(h + 1) * 2048]).then_inc(s0d, 16)
            if PH < 3:
                s.wait_ge(s2mu, NM)
                if DEBUG:
                    s.wait_ge(s0d, 16 * 2 * NH)
                return
            s.wait_ge(s2mu, NM)
            for it in range(NT + 4):
                if it < NT:
                    if it >= 2:
                        s.wait_ge(s3ra, it - 1)
                        s.wait_ge(s3sq, it - 1)
                    if it >= 1:
                        s.wait_ge(s3x, 16 * it)
                    s.dma_start(out=XRS[it % 2], in_=x_h[it * 128:(it + 1) * 128, :]).then_inc(s3x, 16)
                sp = it - 3
                if 0 <= sp < NT:
                    s.wait_ge(s3no, sp + 1)
                    if sp >= 1:
                        s.wait_ge(s3spf, 16 * sp)
                    s.dma_start(out=x1f[sp * 128:(sp + 1) * 128, :], in_=X1O[sp % 3]).then_inc(s3spf, 16)
            if PH < 4:
                s.wait_ge(s3spf, 16 * NT)
                s.wait_ge(s3spb, 16 * NT)
                if DEBUG:
                    s.wait_ge(s0d, 16 * 2 * NH)
                return
            for th in range(2):
                if th == 0:
                    s.wait_ge(s3pe, NT)
                    s.wait_ge(s3spb, 16 * 8)
                    if DEBUG:
                        s.wait_ge(s0d, 16 * 2 * NH)
                else:
                    s.wait_ge(s4out[0], 16 * 2)
                    s.wait_ge(s4out[1], 16 * 2)
                for dt in range(NDI):
                    gi = th * NDI + dt
                    if th == 1 and dt == 0:
                        s.wait_ge(s3spb, 16 * NT)
                    if gi >= 2:
                        s.wait_ge(s4xt[gi % 2], 16 * (gi // 2))
                    s.dma_start_transpose(
                        X1T[:, dt * 1024:(dt + 1) * 1024],
                        x1b[th * 1024:(th + 1) * 1024, dt * 128:(dt + 1) * 128],
                    ).then_inc(s4xt[gi % 2], 16)
                if th == 0:
                    s.wait_ge(s3spf, 16 * NT)
                for g8 in range(8):
                    g = th * 8 + g8
                    if g8 == 0:
                        s.wait_ge(s4pe, 2 * NHT * (th + 1))
                    if g8 == 4:
                        s.wait_ge(s4dn, 4 * NHT * (th + 1))
                    if g >= 1:
                        s.wait_ge(s4x, 16 * g)
                    s.dma_start(out=XR8[g8], in_=x1f[g * 128:(g + 1) * 128, :]).then_inc(s4x, 16)
                for g8 in range(8):
                    g = th * 8 + g8
                    s.wait_ge(s4no, g + 1)
                    if g >= 2:
                        s.wait_ge(s4out[g % 2], 16 * (g // 2))
                    s.dma_start(out=out_h[g * 128:(g + 1) * 128, :],
                                in_=XR8[g8]).then_inc(s4out[g % 2], 16)
            s.wait_ge(s4out[0], 16 * 8)
            s.wait_ge(s4out[1], 16 * 8)
            if DEBUG:
                s.wait_ge(s0d, 16 * (2 * NH))

        @blk.gpsimd
        def _(g):
            for dt in range(4):
                g.dma_start(out=XT[:, dt * 2048:(dt + 1) * 2048],
                            in_=xT_h[dt * 128:(dt + 1) * 128, :]).then_inc(s0x, 16)
            if PH >= 1:
                g.dma_start(out=WB[0], in_=wq_h[0]).then_inc(s1w[0], 16)
                g.dma_start(out=WB[1], in_=wq_h[1]).then_inc(s1w[1], 16)
            for b in range(1, 4):
                g.wait_ge(s0x, 16 * 4 * b)
                for dt in range(4 * b, 4 * b + 4):
                    g.dma_start(out=XT[:, dt * 2048:(dt + 1) * 2048],
                                in_=xT_h[dt * 128:(dt + 1) * 128, :]).then_inc(s0x, 16)
            if PH < 1:
                return
            g.dma_start(out=VW[0], in_=wv_h[0]).then_inc(s1vw[0], 16)
            for j in range(2, 32):
                src = wq_h[j] if j < 16 else wk_h[j - 16]
                g.wait_ge(s1pe, j - 1)
                g.wait_ge(s1w[j % 2], 16 * (j // 2))
                g.dma_start(out=WB[j % 2], in_=src).then_inc(s1w[j % 2], 16)
            for dc in range(1, 4):
                if dc == 1:
                    g.wait_ge(s1pe, 32)
                    g.wait_ge(s1sp, 16 * 16)
                if dc >= 2:
                    g.wait_ge(s1pb, 4 * (dc - 1))
                    g.wait_ge(s1vw[dc % 2], 16 * (dc // 2))
                g.dma_start(out=VW[dc % 2], in_=wv_h[dc]).then_inc(s1vw[dc % 2], 16)
            if PH < 2:
                return
            for m in range(NM):
                h, c = divmod(m, NCH)
                g.wait_ge(s2r, m + 1)
                if m >= 1:
                    g.wait_ge(s2g2, 16 * m)
                if m >= 2:
                    g.wait_ge(s2bc, m - 1)
                g.dma_start(out=RLRB[m % 2], in_=RLQ[m % 2]).then_inc(s2g2, 16)
                if c == NCH - 1:
                    g.wait_ge(s2pv, 32 * (h + 1))
                    if DEBUG and h == 0:
                        g.wait_ge(s0d, 16 * NH)
                    if h >= 1:
                        g.wait_ge(s2wo, 16 * h)
                    g.dma_start(out=WOT[:, h * 2048:(h + 1) * 2048],
                                in_=wo_h[h * 128:(h + 1) * 128, :]).then_inc(s2wo, 16)
            if PH < 3:
                return
            for tt in range(NT):
                g.wait_ge(s3no, tt + 1)
                if tt >= 1:
                    g.wait_ge(s3spb, 16 * tt)
                g.dma_start(out=x1b[tt * 128:(tt + 1) * 128, :], in_=X1O[tt % 3]).then_inc(s3spb, 16)
                if tt == 12:
                    g.wait_ge(s3pe, NT)
                    g.wait_ge(s3ra, NT - 1)
                    g.wait_ge(s3sq, NT - 1)
                    for i0 in range(2):
                        g.dma_start(out=GBF[i0 % 2], in_=wg_h[i0]).then_inc(s4w[i0 % 2], 16)
                        g.dma_start(out=UBF[i0 % 2], in_=wu_h[i0]).then_inc(s4w[i0 % 2], 16)
            if PH < 4:
                return
            for th in range(2):
                for ht in range(NHT):
                    i = th * NHT + ht
                    if i < 2:
                        continue
                    if th == 1 and ht == 0:
                        g.wait_ge(s4ra, 2)
                        g.wait_ge(s4sq, 6)
                    g.wait_ge(s4dq, 2 * (i - 1))
                    g.wait_ge(s4w[i % 2], 32 * (i // 2))
                    g.dma_start(out=GBF[i % 2], in_=wg_h[ht]).then_inc(s4w[i % 2], 16)
                    g.dma_start(out=UBF[i % 2], in_=wu_h[ht]).then_inc(s4w[i % 2], 16)
                for dc in range(4):
                    for h2 in range(NHT):
                        i = (th * 4 + dc) * NHT + h2
                        if i >= 2:
                            g.wait_ge(s4wq, i - 1)
                            g.wait_ge(s4wd[i % 2], 16 * (i // 2))
                        g.dma_start(out=WDB[i % 2], in_=wd_h[h2, dc]).then_inc(s4wd[i % 2], 16)

        @blk.tensor
        def _(t):
            t.wait_ge(s0c, 16 * 9)
            if PH < 1:
                return
            for j in range(32):
                t.wait_ge(s1w[j % 2], 16 * (j // 2 + 1))
                if j >= 2:
                    t.wait_ge(s1sc, j - 1)
                grp = (j % 2) * 4
                for di in range(NDI):
                    if j == 0 and di % 4 == 0:
                        t.wait_ge(s0x, 16 * min(4 * (di // 4 + 1), NDI))
                    for cch in range(4):
                        mm = t.matmul(bank(grp + cch)[:, 0:512],
                                      WB[j % 2][:, di * 128:(di + 1) * 128],
                                      XT[:, di * 2048 + cch * 512: di * 2048 + (cch + 1) * 512],
                                      start=(di == 0), stop=(di == NDI - 1))
                mm.then_inc(s1pe, 1)
            t.wait_ge(s1sc, 32)
            for j2 in range(16):
                dc, q = divmod(j2, 4)
                if q == 0:
                    t.wait_ge(s1vw[dc % 2], 16 * (dc // 2 + 1))
                if j2 >= 2:
                    t.wait_ge(s1scb, 4 * (j2 - 1))
                for di in range(NDI):
                    for t2 in range(4):
                        tg = q * 4 + t2
                        mm = t.matmul(bank((j2 % 2) * 4 + t2)[:, 0:512],
                                      XT[:, di * 2048 + tg * 128: di * 2048 + (tg + 1) * 128],
                                      VW[dc % 2][:, di * 512:(di + 1) * 512],
                                      start=(di == 0), stop=(di == NDI - 1))
                mm.then_inc(s1pb, 1)
            if PH < 2:
                return
            for m in range(NM):
                h, c = divmod(m, NCH)
                if c == 0:
                    t.wait_ge(s2kv[h % 2], 32 * (h // 2 + 1))
                if m >= 2:
                    t.wait_ge(s2yr, m - 1)
                    t.wait_ge(s2mu, m - 1)
                qs = h * 2048 + c * 512
                for u in range(8):
                    k0, k1 = 2 * u, 2 * u + 1
                    slot = u % 2
                    if u >= 2:
                        t.wait_ge(s2sc, 8 * m + u - 1)
                    elif m >= 1:
                        t.wait_ge(s2sc, 8 * (m - 1) + 7 + (1 if u == 1 else 0))
                    t.matmul(bank(2 * slot)[:, 0:512],
                             KVK[h % 2][:, k0 * 128:(k0 + 1) * 128],
                             QT[:, qs:qs + 512], start=True, stop=True)
                    t.matmul(bank(2 * slot + 1)[:, 0:512],
                             KVK[h % 2][:, k1 * 128:(k1 + 1) * 128],
                             QT[:, qs:qs + 512], start=True,
                             stop=True).then_inc(s2st, 1)
                    if u >= 1:
                        up = u - 1
                        t.wait_ge(s2sc, 8 * m + up + 1)
                        for kk in (2 * up, 2 * up + 1):
                            t.matmul(bank(4 + m % 2)[:, 0:512],
                                     KVV[h % 2][:, kk * 128:(kk + 1) * 128],
                                     PT[up % 4][:, (kk % 2) * 512:(kk % 2 + 1) * 512],
                                     start=(kk == 0), stop=False)
                            mm = t.matmul(bank(6 + m % 2)[0:1, 0:512],
                                          ONB[:, 0:1],
                                          PT[up % 4][:, (kk % 2) * 512:(kk % 2 + 1) * 512],
                                          start=(kk == 0), stop=False)
                        mm.then_inc(s2pv, 1)
                t.wait_ge(s2sc, 8 * m + 8)
                for kk in (14, 15):
                    t.matmul(bank(4 + m % 2)[:, 0:512],
                             KVV[h % 2][:, kk * 128:(kk + 1) * 128],
                             PT[3][:, (kk % 2) * 512:(kk % 2 + 1) * 512],
                             start=False, stop=(kk == 15))
                    mm = t.matmul(bank(6 + m % 2)[0:1, 0:512],
                                  ONB[:, 0:1],
                                  PT[3][:, (kk % 2) * 512:(kk % 2 + 1) * 512],
                                  start=False, stop=(kk == 15))
                mm.then_inc(s2pv, 1)
                if m >= 1:
                    t.wait_ge(s2g2, 16 * m)
                    t.matmul(bank(6 + (m - 1) % 2)[:, 0:512], ONRB, RLRB[(m - 1) % 2],
                             start=True, stop=True).then_inc(s2bc, 1)
            t.wait_ge(s2g2, 16 * NM)
            t.matmul(bank(6 + (NM - 1) % 2)[:, 0:512], ONRB, RLRB[(NM - 1) % 2],
                     start=True, stop=True).then_inc(s2bc, 1)
            if PH < 3:
                return
            t.wait_ge(s2mu, NM)
            for tt in range(NT):
                if tt >= 2:
                    t.wait_ge(s3ra, tt - 1)
                    t.wait_ge(s3sq, tt - 1)
                for y in range(NH):
                    t.wait_ge(s2wo, 16 * (y + 1))
                    for cch in range(4):
                        mm = t.matmul(bank((tt % 2) * 4 + cch)[:, 0:512],
                                      YT[:, y * 2048 + tt * 128: y * 2048 + (tt + 1) * 128],
                                      WOT[:, y * 2048 + cch * 512: y * 2048 + (cch + 1) * 512],
                                      start=(y == 0), stop=(y == NH - 1))
                mm.then_inc(s3pe, 1)
            if PH < 4:
                return
            t.wait_ge(s3ra, NT)
            for th in range(2):
                t.wait_ge(s4xt[0], 16 * (NDI // 2) * (th + 1))
                t.wait_ge(s4xt[1], 16 * (NDI // 2) * (th + 1))
                if th == 0:
                    t.wait_ge(s3spf, 16 * NT)
                    t.wait_ge(s3spb, 16 * NT)
                for ht in range(NHT):
                    i = th * NHT + ht
                    t.wait_ge(s4dq, 2 * (i + 1))
                    if i >= 2:
                        t.wait_ge(s4stt, i - 1)
                    par = i % 2
                    for di in range(NDI):
                        for cch in range(2):
                            mm = t.matmul(bank(par * 4 + cch)[:, 0:512],
                                          GDQ[par][:, di * 128:(di + 1) * 128],
                                          X1T[:, di * 1024 + cch * 512: di * 1024 + (cch + 1) * 512],
                                          start=(di == 0), stop=(di == NDI - 1))
                    mm.then_inc(s4pe, 1)
                    for di in range(NDI):
                        for cch in range(2):
                            mm = t.matmul(bank(par * 4 + 2 + cch)[:, 0:512],
                                          UDQ[par][:, di * 128:(di + 1) * 128],
                                          X1T[:, di * 1024 + cch * 512: di * 1024 + (cch + 1) * 512],
                                          start=(di == 0), stop=(di == NDI - 1))
                    mm.then_inc(s4pe, 1)
                t.wait_ge(s4stt, NHT * (th + 1))
                for dc in range(4):
                    gidx = th * 4 + dc
                    if gidx >= 1:
                        t.wait_ge(s4de, 8 * gidx)
                    for h2 in range(NHT):
                        i = gidx * NHT + h2
                        t.wait_ge(s4wq, i + 1)
                        for tt in range(8):
                            mm = t.matmul(bank(tt)[:, 0:512],
                                          HT[:, h2 * 1024 + tt * 128: h2 * 1024 + (tt + 1) * 128],
                                          WDQ[i % 2][:, 0:512],
                                          start=(h2 == 0), stop=(h2 == NHT - 1))
                        mm.then_inc(s4dn, 1)

        @blk.scalar
        def _(sc):
            zero_b = CVC[:, 3:4]
            if PH < 1:
                return
            for j in range(32):
                sc.wait_ge(s1pe, j + 1)
                if j >= 18:
                    sc.wait_ge(s1sp, 16 * (j - 17))
                grp = (j % 2) * 4
                dst = QT[:, (j % 16) * 2048:((j % 16) + 1) * 2048] if j < 16 else KST[j % 2]
                sc.copy(dst, bank(grp, 4)[:, 0:2048]).then_inc(s1sc, 1)
            def vspill(i):
                j2s, t2s = divmod(i, 4)
                dcs, qs = divmod(j2s, 4)
                sc.wait_ge(s1scb, i + 1)
                if i >= 3:
                    sc.wait_ge(s1spv, 16 * ((i - 1) // 2))
                sc.dma_start(out=v_sp[qs * 4 + t2s][:, dcs * 512:(dcs + 1) * 512],
                             in_=VST[i % 4]).then_inc(s1spv, 16)
            for j2 in range(16):
                sc.wait_ge(s1pb, j2 + 1)
                for t2 in range(4):
                    i = j2 * 4 + t2
                    if i >= 4:
                        sc.wait_ge(s1spb if i % 2 == 0 else s1spv,
                                   16 * ((i - 4) // 2 + 1))
                    sc.copy(VST[i % 4], bank((j2 % 2) * 4 + t2)[:, 0:512]).then_inc(s1scb, 1)
                    if i >= 2 and i % 2 == 0:
                        vspill(i - 1)
            vspill(63)
            if PH < 2:
                return
            sc.wait_ge(s1sp, 16 * 16)
            for m in range(NM):
                for u in range(8):
                    sc.wait_ge(s2st, 8 * m + u + 1)
                    if 8 * m + u >= 4:
                        sc.wait_ge(s2pv, 8 * m + u - 3)
                    sc.activation(PT[u % 4], bank(2 * (u % 2), 2)[:, 0:1024],
                                  AF.Exp, bias=CVC[:, 0:1],
                                  scale=ASCL).then_inc(s2sc, 1)
            if PH < 3:
                return
            for it in range(NT + 3):
                if 1 <= it <= NT:
                    t1 = it - 1
                    p1 = t1 % 4
                    sc.wait_ge(s3v1, t1 + 1)
                    sc.activation(stat(p1, S_SD), stat(p1, S_VE), AF.Sqrt)
                if 2 <= it <= NT + 1:
                    t2 = it - 2
                    p2 = t2 % 4
                    sc.wait_ge(s3v2, t2 + 1)
                    if t2 >= 3:
                        sc.wait_ge(s3spf, 16 * (t2 - 2))
                        sc.wait_ge(s3spb, 16 * (t2 - 2))
                    sc.activation(X1O[t2 % 3], ROU[t2 % 3], AF.Identity,
                                  bias=stat(p2, S_NM), scale=stat(p2, S_RS)).then_inc(s3no, 1)
                if it < NT:
                    sc.wait_ge(s3ra, it + 1)
                    sc.activation(bank((it % 2) * 4, 4)[:, 0:2048], ROU[it % 3], AF.Identity,
                                  bias=zero_b, scale=1.0, accum_out=stat(it % 4, S_SUM))
                    sc.activation(XRS[it % 2], ROU[it % 3], AF.Square,
                                  accum_out=stat(it % 4, S_SQ)).then_inc(s3sq, 1)
                elif it == NT:
                    sspace(sc).then_inc(s3sq, 1)
            if PH < 4:
                return
            sc.wait_ge(s3no, NT)
            sc.wait_ge(s3ra, NT)
            sc.wait_ge(s3sq, NT)
            for th in range(2):
                for it in range(NHT + 1):
                    if it < NHT:
                        i = th * NHT + it
                        sc.wait_ge(s4w[i % 2], 32 * (i // 2 + 1))
                        if i >= 2:
                            sc.wait_ge(s4pe, 2 * (i - 1))
                        sc.activation(GDQ[i % 2], GBF[i % 2], AF.Identity, bias=CVC[:, 1:2], scale=1.0)
                        sc.activation(UDQ[i % 2], UBF[i % 2], AF.Identity, bias=CVC[:, 1:2],
                                      scale=1.0).then_inc(s4dq, 2)
                    if it >= 1:
                        i1 = th * NHT + it - 1
                        ht1 = it - 1
                        sc.wait_ge(s4pe, 2 * i1 + 1)
                        sc.activation(GS[i1 % 2], bank((i1 % 2) * 4, 2)[:, 0:1024], AF.Silu,
                                      bias=zero_b, scale=SGC[:, ht1:ht1 + 1]).then_inc(s4sil, 1)
                for g8 in range(8):
                    d2 = (g8 + 4) % 8
                    sc.wait_ge(s4ra, th * 8 + max(g8, d2) + 1)
                    if g8 >= 4:
                        sc.wait_ge(s4sq, th * 8 + g8 - 3)
                    elif th == 1 and g8 == 0:
                        sc.wait_ge(s4sq, 8)
                    sc.activation(DSB[:, d2 * 2048:(d2 + 1) * 2048], XR8[g8], AF.Identity,
                                  bias=zero_b, scale=1.0, accum_out=S4B[:, g8:g8 + 1])
                    sc.activation(DSB[:, g8 * 2048:(g8 + 1) * 2048], XR8[g8], AF.Square,
                                  accum_out=S4B[:, 8 + g8:9 + g8]).then_inc(s4sq, 1)
                sc.wait_ge(s4v1, th + 1)
                sc.activation(S4B[:, 48:56], S4B[:, 40:48], AF.Sqrt)
                sspace(sc)
                sspace(sc).then_inc(s4sd, 1)
                sc.wait_ge(s4v2, th + 1)
                for g8 in range(8):
                    sc.activation(XR8[g8], XR8[g8], AF.Identity,
                                  bias=S4B[:, 64 + g8:65 + g8],
                                  scale=S4B[:, 56 + g8:57 + g8]).then_inc(s4no, 1)

        @blk.vector
        def _(v):
            if PH < 1:
                return
            for j in range(32):
                par = j % 2
                v.wait_ge(s1sc, j + 1)
                if j >= 2:
                    v.wait_ge(s1sw, 32 * (j - 1))
                blk_ap = QT[:, (j % 16) * 2048:((j % 16) + 1) * 2048] if j < 16 else KST[par]
                v.tensor_tensor(RT1[par], blk_ap[0:64, :], COS, AL.mult)
                v.tensor_tensor(TMG[par], blk_ap[0:64, :], GTB, AL.mult).then_inc(s1v1, 1)
                v.wait_ge(s1sw, 32 * (j + 1))
                v.tensor_tensor(blk_ap[0:64, :], RT1[par], RT2[par], AL.add).then_inc(s1v2, 1)
            if PH < 2:
                return
            for m in range(NM + 1):
                if m < NM:
                    v.wait_ge(s2pv, 8 * m + 8)
                    if m >= 2:
                        v.wait_ge(s2mu, m - 1)
                    v.tensor_copy(YRW[m % 2], bank(4 + m % 2)[:, 0:512]).then_inc(s2yr, 1)
                    if m >= 2:
                        v.wait_ge(s2g1, 16 * (m - 1))
                    v.tensor_copy(LSB[m % 2], bank(6 + m % 2)[0:1, 0:512]).then_inc(s2l, 1)
                    v.wait_ge(s2g1, 16 * (m + 1))
                    if m >= 2:
                        v.wait_ge(s2g2, 16 * (m - 1))
                    v.reciprocal(RLQ[m % 2], LQ[m % 2])
                    vspace(v).then_inc(s2r, 1)
                if 1 <= m:
                    mp = m - 1
                    hh, cc = divmod(mp, NCH)
                    v.wait_ge(s2bc, mp + 1)
                    v.tensor_tensor(YT[:, hh * 2048 + cc * 512: hh * 2048 + (cc + 1) * 512],
                                    YRW[mp % 2], bank(6 + mp % 2)[:, 0:512],
                                    AL.mult).then_inc(s2mu, 1)
            if PH < 3:
                return
            for it in range(NT + 3):
                if it < NT:
                    v.wait_ge(s3pe, it + 1)
                    v.wait_ge(s3x, 16 * (it + 1))
                    if it >= 3:
                        v.wait_ge(s3no, it - 2)
                    v.tensor_tensor(ROU[it % 3], bank((it % 2) * 4, 4)[:, 0:2048],
                                    XRS[it % 2], AL.add).then_inc(s3ra, 1)
                if 1 <= it <= NT:
                    t1 = it - 1
                    p1 = t1 % 4
                    v.wait_ge(s3sq, t1 + 1)
                    vstep(v, v.tensor_tensor(stat(p1, S_S2), stat(p1, S_SUM), stat(p1, S_SUM), AL.mult))
                    vstep(v, v.tensor_scalar(stat(p1, S_MN), stat(p1, S_SUM), 1.0 / D, None, AL.mult))
                    vstep(v, v.scalar_tensor_tensor(stat(p1, S_VN), stat(p1, S_S2), -1.0 / D,
                                                    stat(p1, S_SQ), AL.mult, AL.add))
                    vstep(v, v.tensor_scalar(stat(p1, S_VE), stat(p1, S_VN), 1.0 / D, EPS,
                                             AL.mult, AL.add))
                    vspace(v).then_inc(s3v1, 1)
                if 2 <= it <= NT + 1:
                    t2 = it - 2
                    p2 = t2 % 4
                    v.wait_ge(s3sq, min(t2 + 2, NT + 1))
                    vstep(v, v.reciprocal(stat(p2, S_RS), stat(p2, S_SD)))
                    vstep(v, v.scalar_tensor_tensor(stat(p2, S_NM), stat(p2, S_MN), -1.0,
                                                    stat(p2, S_RS), AL.mult, AL.mult))
                    vspace(v).then_inc(s3v2, 1)
            if PH < 4:
                return
            for th in range(2):
                for ht in range(NHT):
                    i = th * NHT + ht
                    if th == 1 and ht % 4 == 0 and ht <= 12:
                        go = 4 + ht // 4
                        v.wait_ge(s4out[go % 2], 16 * (go // 2 + 1))
                    v.wait_ge(s4sil, i + 1)
                    v.wait_ge(s4pe, 2 * i + 2)
                    v.scalar_tensor_tensor(HT[:, ht * 1024:(ht + 1) * 1024],
                                           bank((i % 2) * 4 + 2, 2)[:, 0:1024],
                                           SUC[:, ht:ht + 1], GS[i % 2],
                                           AL.mult, AL.mult).then_inc(s4stt, 1)
                for dc in range(4):
                    gidx = th * 4 + dc
                    for h2 in range(NHT):
                        i = gidx * NHT + h2
                        v.wait_ge(s4wd[i % 2], 16 * (i // 2 + 1))
                        if i >= 2:
                            v.wait_ge(s4dn, i - 1)
                        v.tensor_scalar(WDQ[i % 2], WDB[i % 2], -128.0, None,
                                        AL.add).then_inc(s4wq, 1)
                    v.wait_ge(s4dn, (gidx + 1) * NHT)
                    for tt in range(8):
                        v.tensor_tensor(DSB[:, tt * 2048 + dc * 512: tt * 2048 + (dc + 1) * 512],
                                        bank(tt)[:, 0:512], SDB[:, dc * 512:(dc + 1) * 512],
                                        AL.mult).then_inc(s4de, 1)
                for g8 in range(8):
                    g = th * 8 + g8
                    v.wait_ge(s4x, 16 * (g + 1))
                    v.wait_ge(s4de, 32 * (th + 1))
                    v.tensor_tensor(XR8[g8], XR8[g8], DSB[:, g8 * 2048:(g8 + 1) * 2048],
                                    AL.add).then_inc(s4ra, 1)
                v.wait_ge(s4sq, 8 * (th + 1))
                vstep(v, v.tensor_tensor(S4B[:, 16:24], S4B[:, 0:8], S4B[:, 0:8], AL.mult))
                vstep(v, v.tensor_scalar(S4B[:, 24:32], S4B[:, 0:8], 1.0 / D, None, AL.mult))
                vstep(v, v.scalar_tensor_tensor(S4B[:, 32:40], S4B[:, 16:24], -1.0 / D,
                                                S4B[:, 8:16], AL.mult, AL.add))
                vstep(v, v.tensor_scalar(S4B[:, 40:48], S4B[:, 32:40], 1.0 / D, EPS,
                                         AL.mult, AL.add))
                vspace(v).then_inc(s4v1, 1)
                v.wait_ge(s4sd, th + 1)
                vstep(v, v.reciprocal(S4B[:, 56:64], S4B[:, 48:56]))
                vstep(v, v.scalar_tensor_tensor(S4B[:, 64:72], S4B[:, 24:32], -1.0,
                                                S4B[:, 56:64], AL.mult, AL.mult))
                vspace(v).then_inc(s4v2, 1)

    return nc


def _host_prep(inputs):
    x = np.ascontiguousarray(np.asarray(inputs["L0_attn_x"], dtype=np.float32))
    wq = np.asarray(inputs["L0_attn_wq"], dtype=np.float32)
    wk = np.asarray(inputs["L0_attn_wk"], dtype=np.float32)
    wv = np.asarray(inputs["L0_attn_wv"], dtype=np.float32)
    wo = np.asarray(inputs["L0_attn_wo"], dtype=np.float32)
    wg = np.asarray(inputs["L0_ffn_wq_gate"], dtype=np.int32)
    wu = np.asarray(inputs["L0_ffn_wq_up"], dtype=np.int32)
    wd = np.asarray(inputs["L0_ffn_wq_down"], dtype=np.int32)
    sg = np.asarray(inputs["L0_ffn_s_gate"], dtype=np.float32)
    su = np.asarray(inputs["L0_ffn_s_up"], dtype=np.float32)
    sd = np.asarray(inputs["L0_ffn_s_down"], dtype=np.float32)

    def wblk(w):
        return np.ascontiguousarray(
            w.T.reshape(NDI, 128, NT, 128).transpose(2, 1, 0, 3).reshape(NT, 128, D))

    shared = {
        "wq_c": wblk(wq), "wk_c": wblk(wk),
        "wv_c": np.ascontiguousarray(
            wv.T.reshape(NDI, 128, 4, 512).transpose(2, 1, 0, 3).reshape(4, 128, NDI * 512)),
        "wo_T": np.ascontiguousarray(wo.T),
        "wg_c": np.ascontiguousarray(
            wg.T.reshape(NDI, 128, NHT, 128).transpose(2, 1, 0, 3).reshape(NHT, 128, 2048)),
        "wu_c": np.ascontiguousarray(
            wu.T.reshape(NDI, 128, NHT, 128).transpose(2, 1, 0, 3).reshape(NHT, 128, 2048)),
        "wd_c": np.ascontiguousarray(
            wd.T.reshape(NHT, 128, 4, 512).transpose(0, 2, 1, 3)),
        "sg_c": np.ascontiguousarray(sg.reshape(NHT, 128).T),
        "su_c": np.ascontiguousarray(su.reshape(NHT, 128).T),
        "sd_b": np.ascontiguousarray(np.tile(sd.reshape(1, 2048), (128, 1))),
    }
    in_maps = []
    for b in range(8):
        m = dict(shared)
        m["x"] = np.ascontiguousarray(x[b])
        m["xT"] = np.ascontiguousarray(x[b].T)
        in_maps.append(m)
    return in_maps


def kernel(**inputs):
    in_maps = _host_prep(inputs)
    nc = build_program()
    res = run_bass_kernel_spmd(nc, in_maps, core_ids=list(range(8)))
    return np.stack([r["out"] for r in res.results])


# version 08
# speedup vs baseline: 1.1254x; 1.0145x over previous
import math
import numpy as np
import ml_dtypes
import concourse.bass as bass
import concourse.mybir as mybir
from concourse.bass_utils import run_bass_kernel_spmd

F32 = mybir.dt.float32
BF16 = mybir.dt.bfloat16
I32 = mybir.dt.int32
AF = mybir.ActivationFunctionType
AL = mybir.AluOpType
AX = mybir.AxisListType
bfdt = ml_dtypes.bfloat16

T = 2048
D = 2048
NH = 16
HD = 128
HID = 5632
NT = 16
NDI = 16
NHT = 44
QC = 512
NCH = 4
NM = NH * NCH
C_SHIFT = 24.0
ASCL = 1.0 / math.sqrt(HD)
EPS = 1e-5

DEBUG = False
PH = 4

ARENA_F32_COLS = 53040

OFF_SG    = 202704
OFF_SU    = 202896
OFF_CVEC  = 203088
OFF_ONESR = 203120
OFF_ONESB = 203632
OFF_SDB   = 203888
OFF_SPC   = 200704
OFF_SPC2  = 201728

OFF_XT    = 0
OFF_QT    = 65536
OFF_COS   = 131072
OFF_SING  = 135168
OFF_WBUF  = 139264
OFF_KSTG  = 147456
OFF_RT1   = 155648
OFF_TMPG  = 163840
OFF_RT2   = 172032
OFF_VW    = 180224
OFF_VSTG  = 196608

OFF_YT    = 0
OFF_KVK   = 131072
OFF_KVV   = 139264
OFF_PT    = 147456
OFF_LSB   = 155648
OFF_LQ    = 160000
OFF_RLQ   = 160064
OFF_RLROW = 160128
OFF_YRAW  = 164224

OFF_XRES  = 131072
OFF_ROUT  = 147456
OFF_STAT  = 172032
OFF_X1O   = 172608

OFF_X1T   = 0
OFF_HT    = 32768
OFF_GBUF  = 122880
OFF_UBUF  = 131072
OFF_GDQ   = 139264
OFF_UDQ   = 147456
OFF_GS    = 155648
OFF_DSB   = 122880
OFF_WDB   = 188416
OFF_WDQ   = 190464
OFF_STAT4 = 192512
OFF_XRES4 = 0
OFF_ROUT4 = 32768
OFF_SUMD4 = 57344
OFF_SUMD3 = 188992

S_SUM, S_SQ, S_S2, S_VN, S_VE, S_MN, S_SD, S_RS, S_NM = range(9)


def build_program():
    nc = bass.Bass("TRN2", target_bir_lowering=False)

    x_h   = nc.dram_tensor("x",   [T, D], F32, kind="ExternalInput").ap()
    xT_h  = nc.dram_tensor("xT",  [D, T], F32, kind="ExternalInput").ap()
    wq_h  = nc.dram_tensor("wq_c", [NT, 128, D], F32, kind="ExternalInput").ap()
    wk_h  = nc.dram_tensor("wk_c", [NT, 128, D], F32, kind="ExternalInput").ap()
    wv_h  = nc.dram_tensor("wv_c", [4, 128, NDI * 512], F32, kind="ExternalInput").ap()
    wo_h  = nc.dram_tensor("wo_T", [D, D], F32, kind="ExternalInput").ap()
    wg_h  = nc.dram_tensor("wg_c", [NHT, 128, 2048], I32, kind="ExternalInput").ap()
    wu_h  = nc.dram_tensor("wu_c", [NHT, 128, 2048], I32, kind="ExternalInput").ap()
    wd_h  = nc.dram_tensor("wd_c", [NHT, 4, 128, 512], I32, kind="ExternalInput").ap()
    sg_h  = nc.dram_tensor("sg_c", [128, NHT], F32, kind="ExternalInput").ap()
    su_h  = nc.dram_tensor("su_c", [128, NHT], F32, kind="ExternalInput").ap()
    sd_h  = nc.dram_tensor("sd_b", [128, 2048], F32, kind="ExternalInput").ap()
    out_h = nc.dram_tensor("out", [T, D], F32, kind="ExternalOutput").ap()

    scr_kind = "ExternalOutput" if DEBUG else "Internal"
    kt_sp = nc.dram_tensor("kt_sp", [NH, 128, T], BF16, kind=scr_kind).ap()
    v_sp  = nc.dram_tensor("v_sp",  [NT, 128, D], BF16, kind=scr_kind).ap()
    x1f   = nc.dram_tensor("x1f", [T, D], F32, kind=scr_kind).ap()
    x1b   = nc.dram_tensor("x1b", [T, D], BF16, kind=scr_kind).ap()
    qt_dbg = nc.dram_tensor("qt_dbg", [NH, 128, T], BF16, kind="ExternalOutput").ap() if DEBUG else None
    yt_dbg = nc.dram_tensor("yt_dbg", [NH, 128, T], BF16, kind="ExternalOutput").ap() if DEBUG else None

    inv_freq = (1.0 / 10000.0 ** (np.arange(0, HD, 2, dtype=np.float32) / HD)).astype(np.float32)
    ang = np.arange(T, dtype=np.float32)[:, None] * inv_freq[None, :]
    cosT = np.cos(ang).T.astype(bfdt)
    sinT = np.sin(ang).T.astype(np.float32)
    gtab = np.concatenate([sinT[32:64], -sinT[0:32]], axis=0).astype(bfdt)
    cos_h  = nc.inline_tensor(np.ascontiguousarray(cosT), name="cosT")
    g_h    = nc.inline_tensor(np.ascontiguousarray(gtab), name="gtab")
    ones_h = nc.inline_tensor(np.ones((128, 128), dtype=bfdt), name="onesb")
    onesr_h = nc.inline_tensor(np.ones((1, 128), dtype=np.float32), name="onesr")
    cvec = np.zeros((128, 4), dtype=np.float32)
    cvec[:, 0] = -C_SHIFT
    cvec[:, 1] = -128.0
    cvec[:, 2] = EPS
    cvec_h = nc.inline_tensor(cvec, name="cvec")
    zeros_h = nc.inline_tensor(np.zeros((128, 320), dtype=np.float32), name="zeros")

    AB = nc.alloc_sbuf_tensor("AB", [128, ARENA_F32_COLS], F32)
    PS = nc.alloc_psum_tensor("PS", [128, 4096], F32)
    ps_ap = PS.ap()

    def f32(off_b, p, cols):
        return AB.ap()[0:p, off_b // 4: off_b // 4 + cols]

    def bf16(off_b, p, cols):
        return AB.ap()[0:p, off_b // 4: off_b // 4 + cols // 2].bitcast(BF16)

    def bank(b, n=1):
        return ps_ap[:, b * 512:(b + n) * 512]

    XT  = bf16(OFF_XT, 128, NDI * 2048)
    QT  = bf16(OFF_QT, 128, NH * 2048)
    COS = bf16(OFF_COS, 64, 2048)
    GTB = bf16(OFF_SING, 64, 2048)
    WB  = [bf16(OFF_WBUF + i * 4096, 128, 2048) for i in range(2)]
    KST = [bf16(OFF_KSTG + i * 4096, 128, 2048) for i in range(2)]
    RT1 = [bf16(OFF_RT1 + i * 4096, 64, 2048) for i in range(2)]
    TMG = [bf16(OFF_TMPG + i * 4096, 64, 2048) for i in range(2)]
    RT2 = [bf16(OFF_RT2 + i * 4096, 64, 2048) for i in range(2)]
    VW  = [bf16(OFF_VW, 128, 8192), bf16(OFF_WBUF, 128, 8192)]
    VST = [bf16(OFF_VSTG + i * 1024, 128, 512) for i in range(4)]

    YT  = bf16(OFF_YT, 128, NH * 2048)
    KVK = [bf16(OFF_KVK + i * 4096, 128, 2048) for i in range(2)]
    KVV = [bf16(OFF_KVV + i * 4096, 128, 2048) for i in range(2)]
    PT  = [bf16(OFF_PT + i * 2048, 128, 1024) for i in range(4)]
    LSB = [f32(OFF_LSB + i * 2048, 1, 512) for i in range(2)]
    LQ  = [f32(OFF_LQ + i * 16, 128, 4) for i in range(2)]
    RLQ = [f32(OFF_RLQ + i * 16, 128, 4) for i in range(2)]
    RLR = [f32(OFF_RLROW + i * 2048, 1, 512) for i in range(2)]
    YRW = [f32(OFF_YRAW + i * 2048, 128, 512) for i in range(2)]
    WOT = QT

    XRS = [f32(OFF_XRES + i * 8192, 128, 2048) for i in range(2)]
    XR8 = [f32(32768 + g * 8192, 128, 2048) for g in range(8)]
    S4B = f32(OFF_STAT4, 128, 80)
    ROU = [f32(OFF_ROUT + i * 8192, 128, 2048) for i in range(3)]
    X1O = [f32(OFF_X1O + i * 8192, 128, 2048) for i in range(3)]

    X1T = bf16(OFF_X1T, 128, NDI * 1024)
    HT  = bf16(OFF_HT, 128, NHT * 1024)
    GBF = [bf16(OFF_GBUF + i * 4096, 128, 2048) for i in range(2)]
    UBF = [bf16(OFF_UBUF + i * 4096, 128, 2048) for i in range(2)]
    GDQ = [bf16(OFF_GDQ + i * 4096, 128, 2048) for i in range(2)]
    UDQ = [bf16(OFF_UDQ + i * 4096, 128, 2048) for i in range(2)]
    GS  = [f32(OFF_GS + i * 4096, 128, 1024) for i in range(2)]
    DSB = f32(OFF_DSB, 128, 8 * 2048)
    WDB = [bf16(OFF_WDB + i * 1024, 128, 512) for i in range(2)]
    WDQ = [bf16(OFF_WDQ + i * 1024, 128, 512) for i in range(2)]
    RO4 = [f32(OFF_ROUT4 + i * 8192, 128, 2048) for i in range(3)]
    XR4 = [f32(OFF_XRES4 + i * 8192, 128, 2048) for i in range(4)]
    SD3 = f32(OFF_SUMD3, 128, 2048)
    SD4 = f32(OFF_SUMD4, 128, 2048)

    SGC = f32(OFF_SG, 128, NHT)
    SUC = f32(OFF_SU, 128, NHT)
    CVC = f32(OFF_CVEC, 128, 4)
    ONR = f32(OFF_ONESR, 1, 128)
    ONB = bf16(OFF_ONESB, 128, 128)
    ONRB = bf16(OFF_ONESB, 1, 128)
    RLRB = [bf16(OFF_RLROW + i * 2048, 1, 512) for i in range(2)]
    SDB = f32(OFF_SDB, 128, 2048)
    SPC = f32(OFF_SPC, 128, 256)
    SPC2 = f32(OFF_SPC2, 128, 64)

    vf_sem = nc.alloc_semaphore("vfence")
    vf_n = [0]
    def vstep(v, instr, spacers=2):
        instr.then_inc(vf_sem, 1)
        vf_n[0] += 1
        for _ in range(spacers):
            vspace(v)
        v.wait_ge(vf_sem, vf_n[0])

    vsp_n = [0]
    def vspace(v):
        n = vsp_n[0] % 8
        vsp_n[0] += 1
        return v.tensor_copy(SPC[:, 32 * n:32 * n + 32], SDB[:, 0:32])

    ssp_n = [0]
    def sspace(sc):
        n = ssp_n[0] % 2
        ssp_n[0] += 1
        return sc.copy(SPC2[:, 32 * n:32 * n + 32], SDB[:, 32:64])

    def stat(par, idx):
        return f32(OFF_STAT + par * 144, 128, 36)[:, idx:idx + 1]

    def stat4(par, idx):
        return f32(OFF_STAT4 + par * 144, 128, 36)[:, idx:idx + 1]

    def sem(name):
        return nc.alloc_semaphore(name)

    s0c = sem("s0c"); s0x = sem("s0x"); s0sd = sem("s0sd"); s0d = sem("s0d")
    s1w = [sem("s1wA"), sem("s1wB")]; s1pe = sem("s1pe"); s1sc = sem("s1sc")
    s1v1 = sem("s1v1"); s1sw = sem("s1sw"); s1v2 = sem("s1v2"); s1sp = sem("s1sp")
    s1vw = [sem("s1vwA"), sem("s1vwB")]; s1pb = sem("s1pb"); s1scb = sem("s1scb"); s1spb = sem("s1spb"); s1spv = sem("s1spv")
    s2kv = [sem("s2kvA"), sem("s2kvB")]; s2st = sem("s2st"); s2pv = sem("s2pv"); s2sc = sem("s2sc")
    s2yr = sem("s2yr"); s2l = sem("s2l"); s2r = sem("s2r"); s2g1 = sem("s2g1")
    s2g2 = sem("s2g2"); s2bc = sem("s2bc"); s2mu = sem("s2mu")
    s2wo = sem("s2wo")
    s3x = sem("s3x"); s3pe = sem("s3pe"); s3ra = sem("s3ra"); s3sq = sem("s3sq")
    s3v1 = sem("s3v1"); s3v2 = sem("s3v2"); s3no = sem("s3no")
    s3spf = sem("s3spf"); s3spb = sem("s3spb")
    s4xt = [sem("s4xtA"), sem("s4xtB")]; s4w = [sem("s4wA"), sem("s4wB")]; s4dq = sem("s4dq"); s4pe = sem("s4pe")
    s4sil = sem("s4sil"); s4stt = sem("s4stt"); s4wd = [sem("s4wdA"), sem("s4wdB")]; s4wq = sem("s4wq")
    s4dn = sem("s4dn"); s4de = sem("s4de"); s4x = sem("s4x"); s4ra = sem("s4ra")
    s4sq = sem("s4sq"); s4v1 = sem("s4v1"); s4v2 = sem("s4v2"); s4sd = sem("s4sd")
    s4no = sem("s4no"); s4out = [sem("s4outA"), sem("s4outB")]

    with nc.Block() as blk:

        @blk.sync
        def _(s):
            s.dma_start(out=COS, in_=cos_h.ap()).then_inc(s0c, 16)
            s.dma_start(out=GTB, in_=g_h.ap()).then_inc(s0c, 16)
            s.dma_start(out=ONB, in_=ones_h.ap()).then_inc(s0c, 16)
            s.dma_start(out=ONR, in_=onesr_h.ap()).then_inc(s0c, 16)
            s.dma_start(out=CVC, in_=cvec_h.ap()).then_inc(s0c, 16)
            s.dma_start(out=SGC, in_=sg_h).then_inc(s0c, 16)
            s.dma_start(out=SUC, in_=su_h).then_inc(s0c, 16)
            s.dma_start(out=SDB, in_=sd_h).then_inc(s0c, 16)
            s.dma_start(out=f32(OFF_SPC, 128, 320), in_=zeros_h.ap()).then_inc(s0c, 16)
            if PH < 1:
                s.wait_ge(s0x, 16 * NDI)
                s.wait_ge(s0sd, 1)
                return
            for j in range(32):
                par = j % 2
                s.wait_ge(s1v1, j + 1)
                if j >= 2:
                    s.wait_ge(s1v2, j - 1)
                s.dma_start(out=RT2[par][0:32, :], in_=TMG[par][32:64, :]).then_inc(s1sw, 16)
                s.dma_start(out=RT2[par][32:64, :], in_=TMG[par][0:32, :]).then_inc(s1sw, 16)
                if j >= 16:
                    ot = j - 16
                    s.wait_ge(s1v2, j + 1)
                    if ot >= 1:
                        s.wait_ge(s1sp, 16 * ot)
                    s.dma_start(out=kt_sp[ot], in_=KST[par]).then_inc(s1sp, 16)
            for i2 in range(32):
                i = 2 * i2
                j2, t2 = divmod(i, 4)
                dc, q = divmod(j2, 4)
                s.wait_ge(s1scb, i + 1)
                if i2 >= 1:
                    s.wait_ge(s1spb, 16 * i2)
                s.dma_start(out=v_sp[q * 4 + t2][:, dc * 512:(dc + 1) * 512],
                            in_=VST[i % 4]).then_inc(s1spb, 16)
            if DEBUG:
                s.wait_ge(s1v2, 32)
                for h in range(NH):
                    s.dma_start(out=qt_dbg[h], in_=QT[:, h * 2048:(h + 1) * 2048]).then_inc(s0d, 16)
            if PH < 2:
                s.wait_ge(s1sp, 16 * 16)
                s.wait_ge(s1spb, 16 * 32)
                s.wait_ge(s1spv, 16 * 32)
                if DEBUG:
                    s.wait_ge(s0d, 16 * NH)
                return
            s.wait_ge(s1sp, 16 * 16)
            s.wait_ge(s1spb, 16 * 32)
            s.wait_ge(s1spv, 16 * 32)
            for h in (0, 1):
                s.dma_start(out=KVK[h % 2], in_=kt_sp[h]).then_inc(s2kv[h % 2], 16)
                s.dma_start(out=KVV[h % 2],
                            in_=v_sp[:, :, h * 128:(h + 1) * 128].rearrange("k p b -> p k b")).then_inc(s2kv[h % 2], 16)
            for m in range(NM):
                h, c = divmod(m, NCH)
                s.wait_ge(s2l, m + 1)
                if m >= 1:
                    s.wait_ge(s2g1, 16 * m)
                if m >= 2:
                    s.wait_ge(s2r, m - 1)
                s.dma_start(out=LQ[m % 2], in_=LSB[m % 2]).then_inc(s2g1, 16)
                if c == NCH - 1 and h + 2 < NH:
                    hh = h + 2
                    s.wait_ge(s2pv, 32 * (hh - 1))
                    s.wait_ge(s2kv[hh % 2], 32 * (hh // 2))
                    s.dma_start(out=KVK[hh % 2], in_=kt_sp[hh]).then_inc(s2kv[hh % 2], 16)
                    s.dma_start(out=KVV[hh % 2],
                                in_=v_sp[:, :, hh * 128:(hh + 1) * 128].rearrange("k p b -> p k b")).then_inc(s2kv[hh % 2], 16)
            if DEBUG:
                s.wait_ge(s2mu, NM)
                for h in range(NH):
                    s.dma_start(out=yt_dbg[h], in_=YT[:, h * 2048:(h + 1) * 2048]).then_inc(s0d, 16)
            if PH < 3:
                s.wait_ge(s2mu, NM)
                if DEBUG:
                    s.wait_ge(s0d, 16 * 2 * NH)
                return
            s.wait_ge(s2mu, NM)
            for it in range(NT + 4):
                if it < NT:
                    if it >= 2:
                        s.wait_ge(s3ra, it - 1)
                        s.wait_ge(s3sq, it - 1)
                    if it >= 1:
                        s.wait_ge(s3x, 16 * it)
                    s.dma_start(out=XRS[it % 2], in_=x_h[it * 128:(it + 1) * 128, :]).then_inc(s3x, 16)
                sp = it - 3
                if 0 <= sp < NT:
                    s.wait_ge(s3no, sp + 1)
                    if sp >= 1:
                        s.wait_ge(s3spf, 16 * sp)
                    s.dma_start(out=x1f[sp * 128:(sp + 1) * 128, :], in_=X1O[sp % 3]).then_inc(s3spf, 16)
            if PH < 4:
                s.wait_ge(s3spf, 16 * NT)
                s.wait_ge(s3spb, 16 * NT)
                if DEBUG:
                    s.wait_ge(s0d, 16 * 2 * NH)
                return
            for th in range(2):
                if th == 0:
                    s.wait_ge(s3pe, NT)
                    s.wait_ge(s3spb, 16 * 8)
                    if DEBUG:
                        s.wait_ge(s0d, 16 * 2 * NH)
                else:
                    s.wait_ge(s4pe, 2 * NHT)
                    s.wait_ge(s3spb, 16 * NT)
                for dt in range(NDI):
                    gi = th * NDI + dt
                    if gi >= 2:
                        s.wait_ge(s4xt[gi % 2], 16 * (gi // 2))
                    s.dma_start_transpose(
                        X1T[:, dt * 1024:(dt + 1) * 1024],
                        x1b[th * 1024:(th + 1) * 1024, dt * 128:(dt + 1) * 128],
                    ).then_inc(s4xt[gi % 2], 16)
                if th == 0:
                    s.wait_ge(s3spf, 16 * NT)
            for th in range(2):
                for g8 in range(8):
                    g = th * 8 + g8
                    s.wait_ge(s4dn, (th * 4 + 3) * NHT + 4 * g8 + 4)
                    if g >= 1:
                        s.wait_ge(s4x, 16 * g)
                    s.dma_start(out=XR8[g8], in_=x1f[g * 128:(g + 1) * 128, :]).then_inc(s4x, 16)
                for g8 in range(8):
                    g = th * 8 + g8
                    s.wait_ge(s4no, g + 1)
                    if g >= 2:
                        s.wait_ge(s4out[g % 2], 16 * (g // 2))
                    s.dma_start(out=out_h[g * 128:(g + 1) * 128, :],
                                in_=XR8[g8]).then_inc(s4out[g % 2], 16)
            s.wait_ge(s4out[0], 16 * 8)
            s.wait_ge(s4out[1], 16 * 8)
            if DEBUG:
                s.wait_ge(s0d, 16 * (2 * NH))

        @blk.gpsimd
        def _(g):
            for dt in range(4):
                g.dma_start(out=XT[:, dt * 2048:(dt + 1) * 2048],
                            in_=xT_h[dt * 128:(dt + 1) * 128, :]).then_inc(s0x, 16)
            if PH >= 1:
                g.dma_start(out=WB[0], in_=wq_h[0]).then_inc(s1w[0], 16)
                g.dma_start(out=WB[1], in_=wq_h[1]).then_inc(s1w[1], 16)
            for b in range(1, 4):
                g.wait_ge(s0x, 16 * 4 * b)
                for dt in range(4 * b, 4 * b + 4):
                    g.dma_start(out=XT[:, dt * 2048:(dt + 1) * 2048],
                                in_=xT_h[dt * 128:(dt + 1) * 128, :]).then_inc(s0x, 16)
            if PH < 1:
                return
            g.dma_start(out=VW[0], in_=wv_h[0]).then_inc(s1vw[0], 16)
            for j in range(2, 32):
                src = wq_h[j] if j < 16 else wk_h[j - 16]
                g.wait_ge(s1pe, j - 1)
                g.wait_ge(s1w[j % 2], 16 * (j // 2))
                g.dma_start(out=WB[j % 2], in_=src).then_inc(s1w[j % 2], 16)
            for dc in range(1, 4):
                if dc == 1:
                    g.wait_ge(s1pe, 32)
                    g.wait_ge(s1sp, 16 * 16)
                if dc >= 2:
                    g.wait_ge(s1pb, 4 * (dc - 1))
                    g.wait_ge(s1vw[dc % 2], 16 * (dc // 2))
                g.dma_start(out=VW[dc % 2], in_=wv_h[dc]).then_inc(s1vw[dc % 2], 16)
            if PH < 2:
                return
            for m in range(NM):
                h, c = divmod(m, NCH)
                g.wait_ge(s2r, m + 1)
                if m >= 1:
                    g.wait_ge(s2g2, 16 * m)
                if m >= 2:
                    g.wait_ge(s2bc, m - 1)
                g.dma_start(out=RLRB[m % 2], in_=RLQ[m % 2]).then_inc(s2g2, 16)
                if c == NCH - 1:
                    g.wait_ge(s2pv, 32 * (h + 1))
                    if DEBUG and h == 0:
                        g.wait_ge(s0d, 16 * NH)
                    if h >= 1:
                        g.wait_ge(s2wo, 16 * h)
                    g.dma_start(out=WOT[:, h * 2048:(h + 1) * 2048],
                                in_=wo_h[h * 128:(h + 1) * 128, :]).then_inc(s2wo, 16)
            if PH < 3:
                return
            for tt in range(NT):
                g.wait_ge(s3no, tt + 1)
                if tt >= 1:
                    g.wait_ge(s3spb, 16 * tt)
                g.dma_start(out=x1b[tt * 128:(tt + 1) * 128, :], in_=X1O[tt % 3]).then_inc(s3spb, 16)
                if tt == 12:
                    g.wait_ge(s3pe, NT)
                    g.wait_ge(s3ra, NT - 1)
                    g.wait_ge(s3sq, NT - 1)
                    for i0 in range(2):
                        g.dma_start(out=GBF[i0 % 2], in_=wg_h[i0]).then_inc(s4w[i0 % 2], 16)
                        g.dma_start(out=UBF[i0 % 2], in_=wu_h[i0]).then_inc(s4w[i0 % 2], 16)
            if PH < 4:
                return
            for th in range(2):
                for ht in range(NHT):
                    i = th * NHT + ht
                    if i < 2:
                        continue
                    if th == 1 and ht == 0:
                        g.wait_ge(s4ra, 2)
                        g.wait_ge(s4sq, 6)
                    g.wait_ge(s4dq, 2 * (i - 1))
                    g.wait_ge(s4w[i % 2], 32 * (i // 2))
                    g.dma_start(out=GBF[i % 2], in_=wg_h[ht]).then_inc(s4w[i % 2], 16)
                    g.dma_start(out=UBF[i % 2], in_=wu_h[ht]).then_inc(s4w[i % 2], 16)
                for dc in range(4):
                    for h2 in range(NHT):
                        i = (th * 4 + dc) * NHT + h2
                        if i >= 2:
                            g.wait_ge(s4wq, i - 1)
                            g.wait_ge(s4wd[i % 2], 16 * (i // 2))
                        g.dma_start(out=WDB[i % 2], in_=wd_h[h2, dc]).then_inc(s4wd[i % 2], 16)

        @blk.tensor
        def _(t):
            t.wait_ge(s0c, 16 * 9)
            if PH < 1:
                return
            for j in range(32):
                t.wait_ge(s1w[j % 2], 16 * (j // 2 + 1))
                if j >= 2:
                    t.wait_ge(s1sc, j - 1)
                grp = (j % 2) * 4
                for di in range(NDI):
                    if j == 0 and di % 4 == 0:
                        t.wait_ge(s0x, 16 * min(4 * (di // 4 + 1), NDI))
                    for cch in range(4):
                        mm = t.matmul(bank(grp + cch)[:, 0:512],
                                      WB[j % 2][:, di * 128:(di + 1) * 128],
                                      XT[:, di * 2048 + cch * 512: di * 2048 + (cch + 1) * 512],
                                      start=(di == 0), stop=(di == NDI - 1))
                mm.then_inc(s1pe, 1)
            t.wait_ge(s1sc, 32)
            for j2 in range(16):
                dc, q = divmod(j2, 4)
                if q == 0:
                    t.wait_ge(s1vw[dc % 2], 16 * (dc // 2 + 1))
                if j2 >= 2:
                    t.wait_ge(s1scb, 4 * (j2 - 1))
                for di in range(NDI):
                    for t2 in range(4):
                        tg = q * 4 + t2
                        mm = t.matmul(bank((j2 % 2) * 4 + t2)[:, 0:512],
                                      XT[:, di * 2048 + tg * 128: di * 2048 + (tg + 1) * 128],
                                      VW[dc % 2][:, di * 512:(di + 1) * 512],
                                      start=(di == 0), stop=(di == NDI - 1))
                mm.then_inc(s1pb, 1)
            if PH < 2:
                return
            for m in range(NM):
                h, c = divmod(m, NCH)
                if c == 0:
                    t.wait_ge(s2kv[h % 2], 32 * (h // 2 + 1))
                if m >= 2:
                    t.wait_ge(s2yr, m - 1)
                    t.wait_ge(s2mu, m - 1)
                qs = h * 2048 + c * 512
                for u in range(8):
                    k0, k1 = 2 * u, 2 * u + 1
                    slot = u % 2
                    if u >= 2:
                        t.wait_ge(s2sc, 8 * m + u - 1)
                    elif m >= 1:
                        t.wait_ge(s2sc, 8 * (m - 1) + 7 + (1 if u == 1 else 0))
                    t.matmul(bank(2 * slot)[:, 0:512],
                             KVK[h % 2][:, k0 * 128:(k0 + 1) * 128],
                             QT[:, qs:qs + 512], start=True, stop=True)
                    t.matmul(bank(2 * slot + 1)[:, 0:512],
                             KVK[h % 2][:, k1 * 128:(k1 + 1) * 128],
                             QT[:, qs:qs + 512], start=True,
                             stop=True).then_inc(s2st, 1)
                    if u >= 1:
                        up = u - 1
                        t.wait_ge(s2sc, 8 * m + up + 1)
                        for kk in (2 * up, 2 * up + 1):
                            t.matmul(bank(4 + m % 2)[:, 0:512],
                                     KVV[h % 2][:, kk * 128:(kk + 1) * 128],
                                     PT[up % 4][:, (kk % 2) * 512:(kk % 2 + 1) * 512],
                                     start=(kk == 0), stop=False)
                            mm = t.matmul(bank(6 + m % 2)[0:1, 0:512],
                                          ONB[:, 0:1],
                                          PT[up % 4][:, (kk % 2) * 512:(kk % 2 + 1) * 512],
                                          start=(kk == 0), stop=False)
                        mm.then_inc(s2pv, 1)
                t.wait_ge(s2sc, 8 * m + 8)
                for kk in (14, 15):
                    t.matmul(bank(4 + m % 2)[:, 0:512],
                             KVV[h % 2][:, kk * 128:(kk + 1) * 128],
                             PT[3][:, (kk % 2) * 512:(kk % 2 + 1) * 512],
                             start=False, stop=(kk == 15))
                    mm = t.matmul(bank(6 + m % 2)[0:1, 0:512],
                                  ONB[:, 0:1],
                                  PT[3][:, (kk % 2) * 512:(kk % 2 + 1) * 512],
                                  start=False, stop=(kk == 15))
                mm.then_inc(s2pv, 1)
                if m >= 1:
                    t.wait_ge(s2g2, 16 * m)
                    t.matmul(bank(6 + (m - 1) % 2)[:, 0:512], ONRB, RLRB[(m - 1) % 2],
                             start=True, stop=True).then_inc(s2bc, 1)
            t.wait_ge(s2g2, 16 * NM)
            t.matmul(bank(6 + (NM - 1) % 2)[:, 0:512], ONRB, RLRB[(NM - 1) % 2],
                     start=True, stop=True).then_inc(s2bc, 1)
            if PH < 3:
                return
            t.wait_ge(s2mu, NM)
            for tt in range(NT):
                if tt >= 2:
                    t.wait_ge(s3ra, tt - 1)
                    t.wait_ge(s3sq, tt - 1)
                for y in range(NH):
                    t.wait_ge(s2wo, 16 * (y + 1))
                    for cch in range(4):
                        mm = t.matmul(bank((tt % 2) * 4 + cch)[:, 0:512],
                                      YT[:, y * 2048 + tt * 128: y * 2048 + (tt + 1) * 128],
                                      WOT[:, y * 2048 + cch * 512: y * 2048 + (cch + 1) * 512],
                                      start=(y == 0), stop=(y == NH - 1))
                mm.then_inc(s3pe, 1)
            if PH < 4:
                return
            t.wait_ge(s3ra, NT)
            for th in range(2):
                if th == 0:
                    t.wait_ge(s3spf, 16 * NT)
                    t.wait_ge(s3spb, 16 * NT)
                for ht in range(NHT):
                    i = th * NHT + ht
                    t.wait_ge(s4dq, 2 * (i + 1))
                    if i >= 2:
                        t.wait_ge(s4stt, i - 1)
                    if ht == 0 and th == 1:
                        t.wait_ge(s4de, 32)
                    par = i % 2
                    for di in range(NDI):
                        if ht == 0:
                            t.wait_ge(s4xt[di % 2], 16 * (th * 8 + di // 2 + 1))
                        for cch in range(2):
                            mm = t.matmul(bank(par * 4 + cch)[:, 0:512],
                                          GDQ[par][:, di * 128:(di + 1) * 128],
                                          X1T[:, di * 1024 + cch * 512: di * 1024 + (cch + 1) * 512],
                                          start=(di == 0), stop=(di == NDI - 1))
                    mm.then_inc(s4pe, 1)
                    for di in range(NDI):
                        for cch in range(2):
                            mm = t.matmul(bank(par * 4 + 2 + cch)[:, 0:512],
                                          UDQ[par][:, di * 128:(di + 1) * 128],
                                          X1T[:, di * 1024 + cch * 512: di * 1024 + (cch + 1) * 512],
                                          start=(di == 0), stop=(di == NDI - 1))
                    mm.then_inc(s4pe, 1)
                t.wait_ge(s4stt, NHT * (th + 1))
                for dc in range(4):
                    gidx = th * 4 + dc
                    if gidx >= 1:
                        t.wait_ge(s4de, 8 * gidx)
                    for h2 in range(NHT):
                        i = gidx * NHT + h2
                        t.wait_ge(s4wq, i + 1)
                        for tt in range(8):
                            mm = t.matmul(bank(tt)[:, 0:512],
                                          HT[:, h2 * 1024 + tt * 128: h2 * 1024 + (tt + 1) * 128],
                                          WDQ[i % 2][:, 0:512],
                                          start=(h2 == 0), stop=(h2 == NHT - 1))
                        mm.then_inc(s4dn, 1)

        @blk.scalar
        def _(sc):
            zero_b = CVC[:, 3:4]
            if PH < 1:
                return
            for j in range(32):
                sc.wait_ge(s1pe, j + 1)
                if j >= 18:
                    sc.wait_ge(s1sp, 16 * (j - 17))
                grp = (j % 2) * 4
                dst = QT[:, (j % 16) * 2048:((j % 16) + 1) * 2048] if j < 16 else KST[j % 2]
                sc.copy(dst, bank(grp, 4)[:, 0:2048]).then_inc(s1sc, 1)
            def vspill(i):
                j2s, t2s = divmod(i, 4)
                dcs, qs = divmod(j2s, 4)
                sc.wait_ge(s1scb, i + 1)
                if i >= 3:
                    sc.wait_ge(s1spv, 16 * ((i - 1) // 2))
                sc.dma_start(out=v_sp[qs * 4 + t2s][:, dcs * 512:(dcs + 1) * 512],
                             in_=VST[i % 4]).then_inc(s1spv, 16)
            for j2 in range(16):
                sc.wait_ge(s1pb, j2 + 1)
                for t2 in range(4):
                    i = j2 * 4 + t2
                    if i >= 4:
                        sc.wait_ge(s1spb if i % 2 == 0 else s1spv,
                                   16 * ((i - 4) // 2 + 1))
                    sc.copy(VST[i % 4], bank((j2 % 2) * 4 + t2)[:, 0:512]).then_inc(s1scb, 1)
                    if i >= 2 and i % 2 == 0:
                        vspill(i - 1)
            vspill(63)
            if PH < 2:
                return
            sc.wait_ge(s1sp, 16 * 16)
            for m in range(NM):
                for u in range(8):
                    sc.wait_ge(s2st, 8 * m + u + 1)
                    if 8 * m + u >= 4:
                        sc.wait_ge(s2pv, 8 * m + u - 3)
                    sc.activation(PT[u % 4], bank(2 * (u % 2), 2)[:, 0:1024],
                                  AF.Exp, bias=CVC[:, 0:1],
                                  scale=ASCL).then_inc(s2sc, 1)
            if PH < 3:
                return
            for it in range(NT + 3):
                if 1 <= it <= NT:
                    t1 = it - 1
                    p1 = t1 % 4
                    sc.wait_ge(s3v1, t1 + 1)
                    sc.activation(stat(p1, S_SD), stat(p1, S_VE), AF.Sqrt)
                if 2 <= it <= NT + 1:
                    t2 = it - 2
                    p2 = t2 % 4
                    sc.wait_ge(s3v2, t2 + 1)
                    if t2 >= 3:
                        sc.wait_ge(s3spf, 16 * (t2 - 2))
                        sc.wait_ge(s3spb, 16 * (t2 - 2))
                    sc.activation(X1O[t2 % 3], ROU[t2 % 3], AF.Identity,
                                  bias=stat(p2, S_NM), scale=stat(p2, S_RS)).then_inc(s3no, 1)
                if it < NT:
                    sc.wait_ge(s3ra, it + 1)
                    sc.activation(bank((it % 2) * 4, 4)[:, 0:2048], ROU[it % 3], AF.Identity,
                                  bias=zero_b, scale=1.0, accum_out=stat(it % 4, S_SUM))
                    sc.activation(XRS[it % 2], ROU[it % 3], AF.Square,
                                  accum_out=stat(it % 4, S_SQ)).then_inc(s3sq, 1)
                elif it == NT:
                    sspace(sc).then_inc(s3sq, 1)
            if PH < 4:
                return
            sc.wait_ge(s3no, NT)
            sc.wait_ge(s3ra, NT)
            sc.wait_ge(s3sq, NT)
            for th in range(2):
                for it in range(NHT + 1):
                    if it < NHT and not (th == 1 and it == 0):
                        i = th * NHT + it
                        sc.wait_ge(s4w[i % 2], 32 * (i // 2 + 1))
                        if i >= 2:
                            sc.wait_ge(s4pe, 2 * (i - 1))
                        sc.activation(GDQ[i % 2], GBF[i % 2], AF.Identity, bias=CVC[:, 1:2], scale=1.0)
                        sc.activation(UDQ[i % 2], UBF[i % 2], AF.Identity, bias=CVC[:, 1:2],
                                      scale=1.0).then_inc(s4dq, 2)
                    if it >= 1:
                        i1 = th * NHT + it - 1
                        ht1 = it - 1
                        sc.wait_ge(s4pe, 2 * i1 + 1)
                        sc.activation(GS[i1 % 2], bank((i1 % 2) * 4, 2)[:, 0:1024], AF.Silu,
                                      bias=zero_b, scale=SGC[:, ht1:ht1 + 1]).then_inc(s4sil, 1)
                for g8 in range(8):
                    d2 = (g8 + 4) % 8
                    sc.wait_ge(s4ra, th * 8 + max(g8, d2) + 1)
                    if g8 >= 4:
                        sc.wait_ge(s4sq, th * 8 + g8 - 3)
                    elif th == 1 and g8 == 0:
                        sc.wait_ge(s4sq, 8)
                    sc.activation(DSB[:, d2 * 2048:(d2 + 1) * 2048], XR8[g8], AF.Identity,
                                  bias=zero_b, scale=1.0, accum_out=S4B[:, g8:g8 + 1])
                    sc.activation(DSB[:, g8 * 2048:(g8 + 1) * 2048], XR8[g8], AF.Square,
                                  accum_out=S4B[:, 8 + g8:9 + g8]).then_inc(s4sq, 1)
                if th == 0:
                    i0 = NHT
                    sc.wait_ge(s4sq, 8)
                    sc.wait_ge(s4w[0], 32 * (i0 // 2 + 1))
                    sc.wait_ge(s4pe, 2 * (i0 - 1))
                    sc.activation(GDQ[0], GBF[0], AF.Identity, bias=CVC[:, 1:2], scale=1.0)
                    sc.activation(UDQ[0], UBF[0], AF.Identity, bias=CVC[:, 1:2],
                                  scale=1.0).then_inc(s4dq, 2)
                sc.wait_ge(s4v1, th + 1)
                sc.activation(S4B[:, 48:56], S4B[:, 40:48], AF.Sqrt)
                sspace(sc)
                sspace(sc).then_inc(s4sd, 1)
                sc.wait_ge(s4v2, th + 1)
                for g8 in range(8):
                    sc.activation(XR8[g8], XR8[g8], AF.Identity,
                                  bias=S4B[:, 64 + g8:65 + g8],
                                  scale=S4B[:, 56 + g8:57 + g8]).then_inc(s4no, 1)

        @blk.vector
        def _(v):
            if PH < 1:
                return
            for j in range(32):
                par = j % 2
                v.wait_ge(s1sc, j + 1)
                if j >= 2:
                    v.wait_ge(s1sw, 32 * (j - 1))
                blk_ap = QT[:, (j % 16) * 2048:((j % 16) + 1) * 2048] if j < 16 else KST[par]
                v.tensor_tensor(RT1[par], blk_ap[0:64, :], COS, AL.mult)
                v.tensor_tensor(TMG[par], blk_ap[0:64, :], GTB, AL.mult).then_inc(s1v1, 1)
                v.wait_ge(s1sw, 32 * (j + 1))
                v.tensor_tensor(blk_ap[0:64, :], RT1[par], RT2[par], AL.add).then_inc(s1v2, 1)
            if PH < 2:
                return
            for m in range(NM + 1):
                if m < NM:
                    v.wait_ge(s2pv, 8 * m + 8)
                    if m >= 2:
                        v.wait_ge(s2mu, m - 1)
                    v.tensor_copy(YRW[m % 2], bank(4 + m % 2)[:, 0:512]).then_inc(s2yr, 1)
                    if m >= 2:
                        v.wait_ge(s2g1, 16 * (m - 1))
                    v.tensor_copy(LSB[m % 2], bank(6 + m % 2)[0:1, 0:512]).then_inc(s2l, 1)
                    v.wait_ge(s2g1, 16 * (m + 1))
                    if m >= 2:
                        v.wait_ge(s2g2, 16 * (m - 1))
                    v.reciprocal(RLQ[m % 2], LQ[m % 2])
                    vspace(v).then_inc(s2r, 1)
                if 1 <= m:
                    mp = m - 1
                    hh, cc = divmod(mp, NCH)
                    v.wait_ge(s2bc, mp + 1)
                    v.tensor_tensor(YT[:, hh * 2048 + cc * 512: hh * 2048 + (cc + 1) * 512],
                                    YRW[mp % 2], bank(6 + mp % 2)[:, 0:512],
                                    AL.mult).then_inc(s2mu, 1)
            if PH < 3:
                return
            for it in range(NT + 3):
                if it < NT:
                    v.wait_ge(s3pe, it + 1)
                    v.wait_ge(s3x, 16 * (it + 1))
                    if it >= 3:
                        v.wait_ge(s3no, it - 2)
                    v.tensor_tensor(ROU[it % 3], bank((it % 2) * 4, 4)[:, 0:2048],
                                    XRS[it % 2], AL.add).then_inc(s3ra, 1)
                if 1 <= it <= NT:
                    t1 = it - 1
                    p1 = t1 % 4
                    v.wait_ge(s3sq, t1 + 1)
                    vstep(v, v.tensor_tensor(stat(p1, S_S2), stat(p1, S_SUM), stat(p1, S_SUM), AL.mult))
                    vstep(v, v.tensor_scalar(stat(p1, S_MN), stat(p1, S_SUM), 1.0 / D, None, AL.mult))
                    vstep(v, v.scalar_tensor_tensor(stat(p1, S_VN), stat(p1, S_S2), -1.0 / D,
                                                    stat(p1, S_SQ), AL.mult, AL.add))
                    vstep(v, v.tensor_scalar(stat(p1, S_VE), stat(p1, S_VN), 1.0 / D, EPS,
                                             AL.mult, AL.add))
                    vspace(v).then_inc(s3v1, 1)
                if 2 <= it <= NT + 1:
                    t2 = it - 2
                    p2 = t2 % 4
                    v.wait_ge(s3sq, min(t2 + 2, NT + 1))
                    vstep(v, v.reciprocal(stat(p2, S_RS), stat(p2, S_SD)))
                    vstep(v, v.scalar_tensor_tensor(stat(p2, S_NM), stat(p2, S_MN), -1.0,
                                                    stat(p2, S_RS), AL.mult, AL.mult))
                    vspace(v).then_inc(s3v2, 1)
            if PH < 4:
                return
            for th in range(2):
                for ht in range(NHT):
                    i = th * NHT + ht
                    if th == 1 and ht % 4 == 0 and ht <= 28:
                        go = ht // 4
                        v.wait_ge(s4out[go % 2], 16 * (go // 2 + 1))
                    v.wait_ge(s4sil, i + 1)
                    v.wait_ge(s4pe, 2 * i + 2)
                    v.scalar_tensor_tensor(HT[:, ht * 1024:(ht + 1) * 1024],
                                           bank((i % 2) * 4 + 2, 2)[:, 0:1024],
                                           SUC[:, ht:ht + 1], GS[i % 2],
                                           AL.mult, AL.mult).then_inc(s4stt, 1)
                for dc in range(4):
                    gidx = th * 4 + dc
                    for h2 in range(NHT):
                        i = gidx * NHT + h2
                        v.wait_ge(s4wd[i % 2], 16 * (i // 2 + 1))
                        if i >= 2:
                            v.wait_ge(s4dn, i - 1)
                        v.tensor_scalar(WDQ[i % 2], WDB[i % 2], -128.0, None,
                                        AL.add).then_inc(s4wq, 1)
                    v.wait_ge(s4dn, (gidx + 1) * NHT)
                    for tt in range(8):
                        v.tensor_tensor(DSB[:, tt * 2048 + dc * 512: tt * 2048 + (dc + 1) * 512],
                                        bank(tt)[:, 0:512], SDB[:, dc * 512:(dc + 1) * 512],
                                        AL.mult).then_inc(s4de, 1)
                for g8 in range(8):
                    g = th * 8 + g8
                    v.wait_ge(s4x, 16 * (g + 1))
                    v.wait_ge(s4de, 32 * (th + 1))
                    v.tensor_tensor(XR8[g8], XR8[g8], DSB[:, g8 * 2048:(g8 + 1) * 2048],
                                    AL.add).then_inc(s4ra, 1)
                v.wait_ge(s4sq, 8 * (th + 1))
                vstep(v, v.tensor_tensor(S4B[:, 16:24], S4B[:, 0:8], S4B[:, 0:8], AL.mult))
                vstep(v, v.tensor_scalar(S4B[:, 24:32], S4B[:, 0:8], 1.0 / D, None, AL.mult))
                vstep(v, v.scalar_tensor_tensor(S4B[:, 32:40], S4B[:, 16:24], -1.0 / D,
                                                S4B[:, 8:16], AL.mult, AL.add))
                vstep(v, v.tensor_scalar(S4B[:, 40:48], S4B[:, 32:40], 1.0 / D, EPS,
                                         AL.mult, AL.add))
                vspace(v).then_inc(s4v1, 1)
                v.wait_ge(s4sd, th + 1)
                vstep(v, v.reciprocal(S4B[:, 56:64], S4B[:, 48:56]))
                vstep(v, v.scalar_tensor_tensor(S4B[:, 64:72], S4B[:, 24:32], -1.0,
                                                S4B[:, 56:64], AL.mult, AL.mult))
                vspace(v).then_inc(s4v2, 1)

    return nc


def _host_prep(inputs):
    x = np.ascontiguousarray(np.asarray(inputs["L0_attn_x"], dtype=np.float32))
    wq = np.asarray(inputs["L0_attn_wq"], dtype=np.float32)
    wk = np.asarray(inputs["L0_attn_wk"], dtype=np.float32)
    wv = np.asarray(inputs["L0_attn_wv"], dtype=np.float32)
    wo = np.asarray(inputs["L0_attn_wo"], dtype=np.float32)
    wg = np.asarray(inputs["L0_ffn_wq_gate"], dtype=np.int32)
    wu = np.asarray(inputs["L0_ffn_wq_up"], dtype=np.int32)
    wd = np.asarray(inputs["L0_ffn_wq_down"], dtype=np.int32)
    sg = np.asarray(inputs["L0_ffn_s_gate"], dtype=np.float32)
    su = np.asarray(inputs["L0_ffn_s_up"], dtype=np.float32)
    sd = np.asarray(inputs["L0_ffn_s_down"], dtype=np.float32)

    def wblk(w):
        return np.ascontiguousarray(
            w.T.reshape(NDI, 128, NT, 128).transpose(2, 1, 0, 3).reshape(NT, 128, D))

    shared = {
        "wq_c": wblk(wq), "wk_c": wblk(wk),
        "wv_c": np.ascontiguousarray(
            wv.T.reshape(NDI, 128, 4, 512).transpose(2, 1, 0, 3).reshape(4, 128, NDI * 512)),
        "wo_T": np.ascontiguousarray(wo.T),
        "wg_c": np.ascontiguousarray(
            wg.T.reshape(NDI, 128, NHT, 128).transpose(2, 1, 0, 3).reshape(NHT, 128, 2048)),
        "wu_c": np.ascontiguousarray(
            wu.T.reshape(NDI, 128, NHT, 128).transpose(2, 1, 0, 3).reshape(NHT, 128, 2048)),
        "wd_c": np.ascontiguousarray(
            wd.T.reshape(NHT, 128, 4, 512).transpose(0, 2, 1, 3)),
        "sg_c": np.ascontiguousarray(sg.reshape(NHT, 128).T),
        "su_c": np.ascontiguousarray(su.reshape(NHT, 128).T),
        "sd_b": np.ascontiguousarray(np.tile(sd.reshape(1, 2048), (128, 1))),
    }
    in_maps = []
    for b in range(8):
        m = dict(shared)
        m["x"] = np.ascontiguousarray(x[b])
        m["xT"] = np.ascontiguousarray(x[b].T)
        in_maps.append(m)
    return in_maps


def kernel(**inputs):
    in_maps = _host_prep(inputs)
    nc = build_program()
    res = run_bass_kernel_spmd(nc, in_maps, core_ids=list(range(8)))
    return np.stack([r["out"] for r in res.results])


# version 09 (final)
# speedup vs baseline: 1.1573x; 1.0284x over previous
import math
import numpy as np
import ml_dtypes
import concourse.bass as bass
import concourse.mybir as mybir
from concourse.bass_utils import run_bass_kernel_spmd

F32 = mybir.dt.float32
BF16 = mybir.dt.bfloat16
I32 = mybir.dt.int32
AF = mybir.ActivationFunctionType
AL = mybir.AluOpType
AX = mybir.AxisListType
bfdt = ml_dtypes.bfloat16

T = 2048
D = 2048
NH = 16
HD = 128
HID = 5632
NT = 16
NDI = 16
NHT = 44
QC = 512
NCH = 4
NM = NH * NCH
C_SHIFT = 24.0
ASCL = 1.0 / math.sqrt(HD)
EPS = 1e-5

DEBUG = False
PH = 4

ARENA_F32_COLS = 53040

OFF_SG    = 202704
OFF_SU    = 202896
OFF_CVEC  = 203088
OFF_ONESR = 203120
OFF_ONESB = 203632
OFF_SDB   = 203888
OFF_SPC   = 200704
OFF_SPC2  = 201728

OFF_XT    = 0
OFF_QT    = 65536
OFF_COS   = 131072
OFF_SING  = 135168
OFF_WBUF  = 139264
OFF_KSTG  = 147456
OFF_RT1   = 155648
OFF_TMPG  = 163840
OFF_RT2   = 172032
OFF_VW    = 180224
OFF_VSTG  = 196608

OFF_YT    = 0
OFF_KVK   = 131072
OFF_KVV   = 139264
OFF_PT    = 147456
OFF_LSB   = 155648
OFF_LQ    = 160000
OFF_RLQ   = 160064
OFF_RLROW = 160128
OFF_YRAW  = 164224

OFF_XRES  = 131072
OFF_ROUT  = 147456
OFF_STAT  = 172032
OFF_X1O   = 172608

OFF_X1T   = 0
OFF_HT    = 32768
OFF_GBUF  = 122880
OFF_UBUF  = 131072
OFF_GDQ   = 139264
OFF_UDQ   = 147456
OFF_GS    = 155648
OFF_DSB   = 122880
OFF_WDB   = 188416
OFF_WDQ   = 192512
OFF_STAT4 = 196608
OFF_XRES4 = 0
OFF_ROUT4 = 32768
OFF_SUMD4 = 57344
OFF_SUMD3 = 188992

S_SUM, S_SQ, S_S2, S_VN, S_VE, S_MN, S_SD, S_RS, S_NM = range(9)


def build_program():
    nc = bass.Bass("TRN2", target_bir_lowering=False)

    x_h   = nc.dram_tensor("x",   [T, D], F32, kind="ExternalInput").ap()
    xT_h  = nc.dram_tensor("xT",  [D, T], F32, kind="ExternalInput").ap()
    wq_h  = nc.dram_tensor("wq_c", [NT, 128, D], F32, kind="ExternalInput").ap()
    wk_h  = nc.dram_tensor("wk_c", [NT, 128, D], F32, kind="ExternalInput").ap()
    wv_h  = nc.dram_tensor("wv_c", [4, 128, NDI * 512], F32, kind="ExternalInput").ap()
    wo_h  = nc.dram_tensor("wo_T", [D, D], F32, kind="ExternalInput").ap()
    wg_h  = nc.dram_tensor("wg_c", [NHT, 128, 2048], I32, kind="ExternalInput").ap()
    wu_h  = nc.dram_tensor("wu_c", [NHT, 128, 2048], I32, kind="ExternalInput").ap()
    wd_h  = nc.dram_tensor("wd_c", [NHT, 4, 128, 512], I32, kind="ExternalInput").ap()
    sg_h  = nc.dram_tensor("sg_c", [128, NHT], F32, kind="ExternalInput").ap()
    su_h  = nc.dram_tensor("su_c", [128, NHT], F32, kind="ExternalInput").ap()
    sd_h  = nc.dram_tensor("sd_b", [128, 2048], F32, kind="ExternalInput").ap()
    out_h = nc.dram_tensor("out", [T, D], F32, kind="ExternalOutput").ap()

    scr_kind = "ExternalOutput" if DEBUG else "Internal"
    kt_sp = nc.dram_tensor("kt_sp", [NH, 128, T], BF16, kind=scr_kind).ap()
    v_sp  = nc.dram_tensor("v_sp",  [NT, 128, D], BF16, kind=scr_kind).ap()
    x1f   = nc.dram_tensor("x1f", [T, D], F32, kind=scr_kind).ap()
    x1b   = nc.dram_tensor("x1b", [T, D], BF16, kind=scr_kind).ap()
    qt_dbg = nc.dram_tensor("qt_dbg", [NH, 128, T], BF16, kind="ExternalOutput").ap() if DEBUG else None
    yt_dbg = nc.dram_tensor("yt_dbg", [NH, 128, T], BF16, kind="ExternalOutput").ap() if DEBUG else None

    inv_freq = (1.0 / 10000.0 ** (np.arange(0, HD, 2, dtype=np.float32) / HD)).astype(np.float32)
    ang = np.arange(T, dtype=np.float32)[:, None] * inv_freq[None, :]
    cosT = np.cos(ang).T.astype(bfdt)
    sinT = np.sin(ang).T.astype(np.float32)
    gtab = np.concatenate([sinT[32:64], -sinT[0:32]], axis=0).astype(bfdt)
    cos_h  = nc.inline_tensor(np.ascontiguousarray(cosT), name="cosT")
    g_h    = nc.inline_tensor(np.ascontiguousarray(gtab), name="gtab")
    ones_h = nc.inline_tensor(np.ones((128, 128), dtype=bfdt), name="onesb")
    onesr_h = nc.inline_tensor(np.ones((1, 128), dtype=np.float32), name="onesr")
    cvec = np.zeros((128, 4), dtype=np.float32)
    cvec[:, 0] = -C_SHIFT
    cvec[:, 1] = -128.0
    cvec[:, 2] = EPS
    cvec_h = nc.inline_tensor(cvec, name="cvec")
    zeros_h = nc.inline_tensor(np.zeros((128, 320), dtype=np.float32), name="zeros")

    AB = nc.alloc_sbuf_tensor("AB", [128, ARENA_F32_COLS], F32)
    PS = nc.alloc_psum_tensor("PS", [128, 4096], F32)
    ps_ap = PS.ap()

    def f32(off_b, p, cols):
        return AB.ap()[0:p, off_b // 4: off_b // 4 + cols]

    def bf16(off_b, p, cols):
        return AB.ap()[0:p, off_b // 4: off_b // 4 + cols // 2].bitcast(BF16)

    def bank(b, n=1):
        return ps_ap[:, b * 512:(b + n) * 512]

    XT  = bf16(OFF_XT, 128, NDI * 2048)
    QT  = bf16(OFF_QT, 128, NH * 2048)
    COS = bf16(OFF_COS, 64, 2048)
    GTB = bf16(OFF_SING, 64, 2048)
    WB  = [bf16(OFF_WBUF + i * 4096, 128, 2048) for i in range(2)]
    KST = [bf16(OFF_KSTG + i * 4096, 128, 2048) for i in range(2)]
    RT1 = [bf16(OFF_RT1 + i * 4096, 64, 2048) for i in range(2)]
    TMG = [bf16(OFF_TMPG + i * 4096, 64, 2048) for i in range(2)]
    RT2 = [bf16(OFF_RT2 + i * 4096, 64, 2048) for i in range(2)]
    VW  = [bf16(OFF_VW, 128, 8192), bf16(OFF_WBUF, 128, 8192)]
    VST = [bf16(OFF_VSTG + i * 1024, 128, 512) for i in range(4)]

    YT  = bf16(OFF_YT, 128, NH * 2048)
    KVK = [bf16(OFF_KVK + i * 4096, 128, 2048) for i in range(2)]
    KVV = [bf16(OFF_KVV + i * 4096, 128, 2048) for i in range(2)]
    PT  = [bf16(OFF_PT + i * 2048, 128, 1024) for i in range(4)]
    LSB = [f32(OFF_LSB + i * 2048, 1, 512) for i in range(2)]
    LQ  = [f32(OFF_LQ + i * 16, 128, 4) for i in range(2)]
    RLQ = [f32(OFF_RLQ + i * 16, 128, 4) for i in range(2)]
    RLR = [f32(OFF_RLROW + i * 2048, 1, 512) for i in range(2)]
    YRW = [f32(OFF_YRAW + i * 2048, 128, 512) for i in range(2)]
    WOT = QT

    XRS = [f32(OFF_XRES + i * 8192, 128, 2048) for i in range(2)]
    XR8 = [f32(32768 + g * 8192, 128, 2048) for g in range(8)]
    S4B = f32(OFF_STAT4, 128, 80)
    ROU = [f32(OFF_ROUT + i * 8192, 128, 2048) for i in range(3)]
    X1O = [f32(OFF_X1O + i * 8192, 128, 2048) for i in range(3)]

    X1T = bf16(OFF_X1T, 128, NDI * 1024)
    HT  = bf16(OFF_HT, 128, NHT * 1024)
    GBF = [bf16(OFF_GBUF + i * 4096, 128, 2048) for i in range(2)]
    UBF = [bf16(OFF_UBUF + i * 4096, 128, 2048) for i in range(2)]
    GDQ = [bf16(OFF_GDQ + i * 4096, 128, 2048) for i in range(2)]
    UDQ = [bf16(OFF_UDQ + i * 4096, 128, 2048) for i in range(2)]
    GS  = [f32(OFF_GS + i * 4096, 128, 1024) for i in range(2)]
    DSB = f32(OFF_DSB, 128, 8 * 2048)
    WDB = [bf16(OFF_WDB + i * 1024, 128, 512) for i in range(4)]
    WDQ = [bf16(OFF_WDQ + i * 1024, 128, 512) for i in range(4)]
    RO4 = [f32(OFF_ROUT4 + i * 8192, 128, 2048) for i in range(3)]
    XR4 = [f32(OFF_XRES4 + i * 8192, 128, 2048) for i in range(4)]
    SD3 = f32(OFF_SUMD3, 128, 2048)
    SD4 = f32(OFF_SUMD4, 128, 2048)

    SGC = f32(OFF_SG, 128, NHT)
    SUC = f32(OFF_SU, 128, NHT)
    CVC = f32(OFF_CVEC, 128, 4)
    ONR = f32(OFF_ONESR, 1, 128)
    ONB = bf16(OFF_ONESB, 128, 128)
    ONRB = bf16(OFF_ONESB, 1, 128)
    RLRB = [bf16(OFF_RLROW + i * 2048, 1, 512) for i in range(2)]
    SDB = f32(OFF_SDB, 128, 2048)
    SPC = f32(OFF_SPC, 128, 256)
    SPC2 = f32(OFF_SPC2, 128, 64)

    vf_sem = nc.alloc_semaphore("vfence")
    vf_n = [0]
    def vstep(v, instr, spacers=2):
        instr.then_inc(vf_sem, 1)
        vf_n[0] += 1
        for _ in range(spacers):
            vspace(v)
        v.wait_ge(vf_sem, vf_n[0])

    vsp_n = [0]
    def vspace(v):
        n = vsp_n[0] % 8
        vsp_n[0] += 1
        return v.tensor_copy(SPC[:, 32 * n:32 * n + 32], SDB[:, 0:32])

    ssp_n = [0]
    def sspace(sc):
        n = ssp_n[0] % 2
        ssp_n[0] += 1
        return sc.copy(SPC2[:, 32 * n:32 * n + 32], SDB[:, 32:64])

    def stat(par, idx):
        return f32(OFF_STAT + par * 144, 128, 36)[:, idx:idx + 1]

    def stat4(par, idx):
        return f32(OFF_STAT4 + par * 144, 128, 36)[:, idx:idx + 1]

    def sem(name):
        return nc.alloc_semaphore(name)

    s0c = sem("s0c"); s0x = sem("s0x"); s0sd = sem("s0sd"); s0d = sem("s0d")
    s1w = [sem("s1wA"), sem("s1wB")]; s1pe = sem("s1pe"); s1sc = sem("s1sc")
    s1v1 = sem("s1v1"); s1sw = sem("s1sw"); s1v2 = sem("s1v2"); s1sp = sem("s1sp")
    s1vw = [sem("s1vwA"), sem("s1vwB")]; s1pb = sem("s1pb"); s1scb = sem("s1scb"); s1spb = sem("s1spb"); s1spv = sem("s1spv")
    s2kv = [sem("s2kvA"), sem("s2kvB")]; s2st = sem("s2st"); s2pv = sem("s2pv"); s2sc = sem("s2sc")
    s2yr = sem("s2yr"); s2l = sem("s2l"); s2r = sem("s2r"); s2g1 = sem("s2g1")
    s2g2 = sem("s2g2"); s2bc = sem("s2bc"); s2mu = sem("s2mu")
    s2wo = sem("s2wo")
    s3x = sem("s3x"); s3pe = sem("s3pe"); s3ra = sem("s3ra"); s3sq = sem("s3sq")
    s3v1 = sem("s3v1"); s3v2 = sem("s3v2"); s3no = sem("s3no")
    s3spf = sem("s3spf"); s3spb = sem("s3spb")
    s4xt = [sem("s4xtA"), sem("s4xtB")]; s4w = [sem("s4wA"), sem("s4wB")]; s4dq = sem("s4dq"); s4pe = sem("s4pe")
    s4sil = sem("s4sil"); s4stt = sem("s4stt"); s4wd = [sem("s4wdA"), sem("s4wdB")]; s4wq = sem("s4wq")
    s4dn = sem("s4dn"); s4de = sem("s4de"); s4x = sem("s4x"); s4ra = sem("s4ra")
    s4sq = sem("s4sq"); s4v1 = sem("s4v1"); s4v2 = sem("s4v2"); s4sd = sem("s4sd")
    s4no = sem("s4no"); s4out = [sem("s4outA"), sem("s4outB")]

    with nc.Block() as blk:

        @blk.sync
        def _(s):
            s.dma_start(out=COS, in_=cos_h.ap()).then_inc(s0c, 16)
            s.dma_start(out=GTB, in_=g_h.ap()).then_inc(s0c, 16)
            s.dma_start(out=ONB, in_=ones_h.ap()).then_inc(s0c, 16)
            s.dma_start(out=ONR, in_=onesr_h.ap()).then_inc(s0c, 16)
            s.dma_start(out=CVC, in_=cvec_h.ap()).then_inc(s0c, 16)
            s.dma_start(out=SGC, in_=sg_h).then_inc(s0c, 16)
            s.dma_start(out=SUC, in_=su_h).then_inc(s0c, 16)
            s.dma_start(out=SDB, in_=sd_h).then_inc(s0c, 16)
            s.dma_start(out=f32(OFF_SPC, 128, 320), in_=zeros_h.ap()).then_inc(s0c, 16)
            if PH < 1:
                s.wait_ge(s0x, 16 * NDI)
                s.wait_ge(s0sd, 1)
                return
            for j in range(32):
                par = j % 2
                s.wait_ge(s1v1, j + 1)
                if j >= 2:
                    s.wait_ge(s1v2, j - 1)
                s.dma_start(out=RT2[par][0:32, :], in_=TMG[par][32:64, :]).then_inc(s1sw, 16)
                s.dma_start(out=RT2[par][32:64, :], in_=TMG[par][0:32, :]).then_inc(s1sw, 16)
                if j >= 16:
                    ot = j - 16
                    s.wait_ge(s1v2, j + 1)
                    if ot >= 1:
                        s.wait_ge(s1sp, 16 * ot)
                    s.dma_start(out=kt_sp[ot], in_=KST[par]).then_inc(s1sp, 16)
            for i2 in range(32):
                i = 2 * i2
                j2, t2 = divmod(i, 4)
                dc, q = divmod(j2, 4)
                s.wait_ge(s1scb, i + 1)
                if i2 >= 1:
                    s.wait_ge(s1spb, 16 * i2)
                s.dma_start(out=v_sp[q * 4 + t2][:, dc * 512:(dc + 1) * 512],
                            in_=VST[i % 4]).then_inc(s1spb, 16)
            if DEBUG:
                s.wait_ge(s1v2, 32)
                for h in range(NH):
                    s.dma_start(out=qt_dbg[h], in_=QT[:, h * 2048:(h + 1) * 2048]).then_inc(s0d, 16)
            if PH < 2:
                s.wait_ge(s1sp, 16 * 16)
                s.wait_ge(s1spb, 16 * 32)
                s.wait_ge(s1spv, 16 * 32)
                if DEBUG:
                    s.wait_ge(s0d, 16 * NH)
                return
            s.wait_ge(s1sp, 16 * 16)
            s.wait_ge(s1spb, 16 * 32)
            s.wait_ge(s1spv, 16 * 32)
            for h in (0, 1):
                s.dma_start(out=KVK[h % 2], in_=kt_sp[h]).then_inc(s2kv[h % 2], 16)
                s.dma_start(out=KVV[h % 2],
                            in_=v_sp[:, :, h * 128:(h + 1) * 128].rearrange("k p b -> p k b")).then_inc(s2kv[h % 2], 16)
            for m in range(NM):
                h, c = divmod(m, NCH)
                s.wait_ge(s2l, m + 1)
                if m >= 1:
                    s.wait_ge(s2g1, 16 * m)
                if m >= 2:
                    s.wait_ge(s2r, m - 1)
                s.dma_start(out=LQ[m % 2], in_=LSB[m % 2]).then_inc(s2g1, 16)
                if c == NCH - 1 and h + 2 < NH:
                    hh = h + 2
                    s.wait_ge(s2pv, 32 * (hh - 1))
                    s.wait_ge(s2kv[hh % 2], 32 * (hh // 2))
                    s.dma_start(out=KVK[hh % 2], in_=kt_sp[hh]).then_inc(s2kv[hh % 2], 16)
                    s.dma_start(out=KVV[hh % 2],
                                in_=v_sp[:, :, hh * 128:(hh + 1) * 128].rearrange("k p b -> p k b")).then_inc(s2kv[hh % 2], 16)
            if DEBUG:
                s.wait_ge(s2mu, NM)
                for h in range(NH):
                    s.dma_start(out=yt_dbg[h], in_=YT[:, h * 2048:(h + 1) * 2048]).then_inc(s0d, 16)
            if PH < 3:
                s.wait_ge(s2mu, NM)
                if DEBUG:
                    s.wait_ge(s0d, 16 * 2 * NH)
                return
            s.wait_ge(s2mu, NM)
            for it in range(NT + 4):
                if it < NT:
                    if it >= 2:
                        s.wait_ge(s3ra, it - 1)
                        s.wait_ge(s3sq, it - 1)
                    if it >= 1:
                        s.wait_ge(s3x, 16 * it)
                    s.dma_start(out=XRS[it % 2], in_=x_h[it * 128:(it + 1) * 128, :]).then_inc(s3x, 16)
                sp = it - 3
                if 0 <= sp < NT:
                    s.wait_ge(s3no, sp + 1)
                    if sp >= 1:
                        s.wait_ge(s3spf, 16 * sp)
                    s.dma_start(out=x1f[sp * 128:(sp + 1) * 128, :], in_=X1O[sp % 3]).then_inc(s3spf, 16)
            if PH < 4:
                s.wait_ge(s3spf, 16 * NT)
                s.wait_ge(s3spb, 16 * NT)
                if DEBUG:
                    s.wait_ge(s0d, 16 * 2 * NH)
                return
            for th in range(2):
                if th == 0:
                    s.wait_ge(s3pe, NT)
                    s.wait_ge(s3spb, 16 * 8)
                    if DEBUG:
                        s.wait_ge(s0d, 16 * 2 * NH)
                else:
                    s.wait_ge(s4pe, 2 * NHT)
                    s.wait_ge(s3spb, 16 * NT)
                for dt in range(NDI):
                    gi = th * NDI + dt
                    if gi >= 2:
                        s.wait_ge(s4xt[gi % 2], 16 * (gi // 2))
                    s.dma_start_transpose(
                        X1T[:, dt * 1024:(dt + 1) * 1024],
                        x1b[th * 1024:(th + 1) * 1024, dt * 128:(dt + 1) * 128],
                    ).then_inc(s4xt[gi % 2], 16)
                if th == 0:
                    s.wait_ge(s3spf, 16 * NT)
            for th in range(2):
                for g8 in range(8):
                    g = th * 8 + g8
                    s.wait_ge(s4dn, (th * 4 + 3) * NHT + 4 * g8 + 4)
                    if g >= 1:
                        s.wait_ge(s4x, 16 * g)
                    s.dma_start(out=XR8[g8], in_=x1f[g * 128:(g + 1) * 128, :]).then_inc(s4x, 16)
                for g8 in range(8):
                    g = th * 8 + g8
                    s.wait_ge(s4no, g + 1)
                    if g >= 2:
                        s.wait_ge(s4out[g % 2], 16 * (g // 2))
                    s.dma_start(out=out_h[g * 128:(g + 1) * 128, :],
                                in_=XR8[g8]).then_inc(s4out[g % 2], 16)
            s.wait_ge(s4out[0], 16 * 8)
            s.wait_ge(s4out[1], 16 * 8)
            if DEBUG:
                s.wait_ge(s0d, 16 * (2 * NH))

        @blk.gpsimd
        def _(g):
            for dt in range(4):
                g.dma_start(out=XT[:, dt * 2048:(dt + 1) * 2048],
                            in_=xT_h[dt * 128:(dt + 1) * 128, :]).then_inc(s0x, 16)
            if PH >= 1:
                g.dma_start(out=WB[0], in_=wq_h[0]).then_inc(s1w[0], 16)
                g.dma_start(out=WB[1], in_=wq_h[1]).then_inc(s1w[1], 16)
            for b in range(1, 4):
                g.wait_ge(s0x, 16 * 4 * b)
                for dt in range(4 * b, 4 * b + 4):
                    g.dma_start(out=XT[:, dt * 2048:(dt + 1) * 2048],
                                in_=xT_h[dt * 128:(dt + 1) * 128, :]).then_inc(s0x, 16)
            if PH < 1:
                return
            g.dma_start(out=VW[0], in_=wv_h[0]).then_inc(s1vw[0], 16)
            for j in range(2, 32):
                src = wq_h[j] if j < 16 else wk_h[j - 16]
                g.wait_ge(s1pe, j - 1)
                g.wait_ge(s1w[j % 2], 16 * (j // 2))
                g.dma_start(out=WB[j % 2], in_=src).then_inc(s1w[j % 2], 16)
            for dc in range(1, 4):
                if dc == 1:
                    g.wait_ge(s1pe, 32)
                    g.wait_ge(s1sp, 16 * 16)
                if dc >= 2:
                    g.wait_ge(s1pb, 4 * (dc - 1))
                    g.wait_ge(s1vw[dc % 2], 16 * (dc // 2))
                g.dma_start(out=VW[dc % 2], in_=wv_h[dc]).then_inc(s1vw[dc % 2], 16)
            if PH < 2:
                return
            for m in range(NM):
                h, c = divmod(m, NCH)
                g.wait_ge(s2r, m + 1)
                if m >= 1:
                    g.wait_ge(s2g2, 16 * m)
                if m >= 2:
                    g.wait_ge(s2bc, m - 1)
                g.dma_start(out=RLRB[m % 2], in_=RLQ[m % 2]).then_inc(s2g2, 16)
                if c == NCH - 1:
                    g.wait_ge(s2pv, 32 * (h + 1))
                    if DEBUG and h == 0:
                        g.wait_ge(s0d, 16 * NH)
                    if h >= 1:
                        g.wait_ge(s2wo, 16 * h)
                    g.dma_start(out=WOT[:, h * 2048:(h + 1) * 2048],
                                in_=wo_h[h * 128:(h + 1) * 128, :]).then_inc(s2wo, 16)
            if PH < 3:
                return
            for tt in range(NT):
                g.wait_ge(s3no, tt + 1)
                if tt >= 1:
                    g.wait_ge(s3spb, 16 * tt)
                g.dma_start(out=x1b[tt * 128:(tt + 1) * 128, :], in_=X1O[tt % 3]).then_inc(s3spb, 16)
                if tt == 12:
                    g.wait_ge(s3pe, NT)
                    g.wait_ge(s3ra, NT - 1)
                    g.wait_ge(s3sq, NT - 1)
                    for i0 in range(2):
                        g.dma_start(out=GBF[i0 % 2], in_=wg_h[i0]).then_inc(s4w[i0 % 2], 16)
                        g.dma_start(out=UBF[i0 % 2], in_=wu_h[i0]).then_inc(s4w[i0 % 2], 16)
            if PH < 4:
                return
            for th in range(2):
                for ht in range(NHT):
                    i = th * NHT + ht
                    if i < 2:
                        continue
                    if th == 1 and ht == 0:
                        g.wait_ge(s4ra, 2)
                        g.wait_ge(s4sq, 6)
                    g.wait_ge(s4dq, 2 * (i - 1))
                    g.wait_ge(s4w[i % 2], 32 * (i // 2))
                    g.dma_start(out=GBF[i % 2], in_=wg_h[ht]).then_inc(s4w[i % 2], 16)
                    g.dma_start(out=UBF[i % 2], in_=wu_h[ht]).then_inc(s4w[i % 2], 16)
                for dc in range(4):
                    for h2 in range(NHT):
                        i = (th * 4 + dc) * NHT + h2
                        if i >= 4:
                            g.wait_ge(s4wq, i - 3)
                        if i >= 2:
                            g.wait_ge(s4wd[i % 2], 16 * (i // 2))
                        g.dma_start(out=WDB[i % 4], in_=wd_h[h2, dc]).then_inc(s4wd[i % 2], 16)

        @blk.tensor
        def _(t):
            t.wait_ge(s0c, 16 * 9)
            if PH < 1:
                return
            for j in range(32):
                t.wait_ge(s1w[j % 2], 16 * (j // 2 + 1))
                if j >= 2:
                    t.wait_ge(s1sc, j - 1)
                grp = (j % 2) * 4
                for di in range(NDI):
                    if j == 0 and di % 4 == 0:
                        t.wait_ge(s0x, 16 * min(4 * (di // 4 + 1), NDI))
                    for cch in range(4):
                        mm = t.matmul(bank(grp + cch)[:, 0:512],
                                      WB[j % 2][:, di * 128:(di + 1) * 128],
                                      XT[:, di * 2048 + cch * 512: di * 2048 + (cch + 1) * 512],
                                      start=(di == 0), stop=(di == NDI - 1))
                mm.then_inc(s1pe, 1)
            t.wait_ge(s1sc, 32)
            for j2 in range(16):
                dc, q = divmod(j2, 4)
                if q == 0:
                    t.wait_ge(s1vw[dc % 2], 16 * (dc // 2 + 1))
                if j2 >= 2:
                    t.wait_ge(s1scb, 4 * (j2 - 1))
                for di in range(NDI):
                    for t2 in range(4):
                        tg = q * 4 + t2
                        mm = t.matmul(bank((j2 % 2) * 4 + t2)[:, 0:512],
                                      XT[:, di * 2048 + tg * 128: di * 2048 + (tg + 1) * 128],
                                      VW[dc % 2][:, di * 512:(di + 1) * 512],
                                      start=(di == 0), stop=(di == NDI - 1))
                mm.then_inc(s1pb, 1)
            if PH < 2:
                return
            for m in range(NM):
                h, c = divmod(m, NCH)
                if c == 0:
                    t.wait_ge(s2kv[h % 2], 32 * (h // 2 + 1))
                if m >= 2:
                    t.wait_ge(s2yr, m - 1)
                    t.wait_ge(s2mu, m - 1)
                qs = h * 2048 + c * 512
                for u in range(8):
                    k0, k1 = 2 * u, 2 * u + 1
                    slot = u % 2
                    if u >= 2:
                        t.wait_ge(s2sc, 8 * m + u - 1)
                    elif m >= 1:
                        t.wait_ge(s2sc, 8 * (m - 1) + 7 + (1 if u == 1 else 0))
                    t.matmul(bank(2 * slot)[:, 0:512],
                             KVK[h % 2][:, k0 * 128:(k0 + 1) * 128],
                             QT[:, qs:qs + 512], start=True, stop=True)
                    t.matmul(bank(2 * slot + 1)[:, 0:512],
                             KVK[h % 2][:, k1 * 128:(k1 + 1) * 128],
                             QT[:, qs:qs + 512], start=True,
                             stop=True).then_inc(s2st, 1)
                    if u >= 1:
                        up = u - 1
                        t.wait_ge(s2sc, 8 * m + up + 1)
                        for kk in (2 * up, 2 * up + 1):
                            t.matmul(bank(4 + m % 2)[:, 0:512],
                                     KVV[h % 2][:, kk * 128:(kk + 1) * 128],
                                     PT[up % 4][:, (kk % 2) * 512:(kk % 2 + 1) * 512],
                                     start=(kk == 0), stop=False)
                            mm = t.matmul(bank(6 + m % 2)[0:1, 0:512],
                                          ONB[:, 0:1],
                                          PT[up % 4][:, (kk % 2) * 512:(kk % 2 + 1) * 512],
                                          start=(kk == 0), stop=False)
                        mm.then_inc(s2pv, 1)
                t.wait_ge(s2sc, 8 * m + 8)
                for kk in (14, 15):
                    t.matmul(bank(4 + m % 2)[:, 0:512],
                             KVV[h % 2][:, kk * 128:(kk + 1) * 128],
                             PT[3][:, (kk % 2) * 512:(kk % 2 + 1) * 512],
                             start=False, stop=(kk == 15))
                    mm = t.matmul(bank(6 + m % 2)[0:1, 0:512],
                                  ONB[:, 0:1],
                                  PT[3][:, (kk % 2) * 512:(kk % 2 + 1) * 512],
                                  start=False, stop=(kk == 15))
                mm.then_inc(s2pv, 1)
                if m >= 1:
                    t.wait_ge(s2g2, 16 * m)
                    t.matmul(bank(6 + (m - 1) % 2)[:, 0:512], ONRB, RLRB[(m - 1) % 2],
                             start=True, stop=True).then_inc(s2bc, 1)
            t.wait_ge(s2g2, 16 * NM)
            t.matmul(bank(6 + (NM - 1) % 2)[:, 0:512], ONRB, RLRB[(NM - 1) % 2],
                     start=True, stop=True).then_inc(s2bc, 1)
            if PH < 3:
                return
            t.wait_ge(s2mu, NM)
            for tt in range(NT):
                if tt >= 2:
                    t.wait_ge(s3ra, tt - 1)
                    t.wait_ge(s3sq, tt - 1)
                for y in range(NH):
                    t.wait_ge(s2wo, 16 * (y + 1))
                    for cch in range(4):
                        mm = t.matmul(bank((tt % 2) * 4 + cch)[:, 0:512],
                                      YT[:, y * 2048 + tt * 128: y * 2048 + (tt + 1) * 128],
                                      WOT[:, y * 2048 + cch * 512: y * 2048 + (cch + 1) * 512],
                                      start=(y == 0), stop=(y == NH - 1))
                mm.then_inc(s3pe, 1)
            if PH < 4:
                return
            t.wait_ge(s3ra, NT)
            for th in range(2):
                if th == 0:
                    t.wait_ge(s3spf, 16 * NT)
                    t.wait_ge(s3spb, 16 * NT)
                for ht in range(NHT):
                    i = th * NHT + ht
                    t.wait_ge(s4dq, 2 * (i + 1))
                    if i >= 2:
                        t.wait_ge(s4stt, i - 1)
                    if ht == 0 and th == 1:
                        t.wait_ge(s4de, 32)
                    par = i % 2
                    for di in range(NDI):
                        if ht == 0:
                            t.wait_ge(s4xt[di % 2], 16 * (th * 8 + di // 2 + 1))
                        for cch in range(2):
                            mm = t.matmul(bank(par * 4 + cch)[:, 0:512],
                                          GDQ[par][:, di * 128:(di + 1) * 128],
                                          X1T[:, di * 1024 + cch * 512: di * 1024 + (cch + 1) * 512],
                                          start=(di == 0), stop=(di == NDI - 1))
                    mm.then_inc(s4pe, 1)
                    for di in range(NDI):
                        for cch in range(2):
                            mm = t.matmul(bank(par * 4 + 2 + cch)[:, 0:512],
                                          UDQ[par][:, di * 128:(di + 1) * 128],
                                          X1T[:, di * 1024 + cch * 512: di * 1024 + (cch + 1) * 512],
                                          start=(di == 0), stop=(di == NDI - 1))
                    mm.then_inc(s4pe, 1)
                t.wait_ge(s4stt, NHT * (th + 1))
                for dc in range(4):
                    gidx = th * 4 + dc
                    if gidx >= 1:
                        t.wait_ge(s4de, 8 * gidx)
                    for h2 in range(NHT):
                        i = gidx * NHT + h2
                        t.wait_ge(s4wq, i + 1)
                        for tt in range(8):
                            mm = t.matmul(bank(tt)[:, 0:512],
                                          HT[:, h2 * 1024 + tt * 128: h2 * 1024 + (tt + 1) * 128],
                                          WDQ[i % 4][:, 0:512],
                                          start=(h2 == 0), stop=(h2 == NHT - 1))
                        mm.then_inc(s4dn, 1)

        @blk.scalar
        def _(sc):
            zero_b = CVC[:, 3:4]
            if PH < 1:
                return
            for j in range(32):
                sc.wait_ge(s1pe, j + 1)
                if j >= 18:
                    sc.wait_ge(s1sp, 16 * (j - 17))
                grp = (j % 2) * 4
                dst = QT[:, (j % 16) * 2048:((j % 16) + 1) * 2048] if j < 16 else KST[j % 2]
                sc.copy(dst, bank(grp, 4)[:, 0:2048]).then_inc(s1sc, 1)
            def vspill(i):
                j2s, t2s = divmod(i, 4)
                dcs, qs = divmod(j2s, 4)
                sc.wait_ge(s1scb, i + 1)
                if i >= 3:
                    sc.wait_ge(s1spv, 16 * ((i - 1) // 2))
                sc.dma_start(out=v_sp[qs * 4 + t2s][:, dcs * 512:(dcs + 1) * 512],
                             in_=VST[i % 4]).then_inc(s1spv, 16)
            for j2 in range(16):
                sc.wait_ge(s1pb, j2 + 1)
                for t2 in range(4):
                    i = j2 * 4 + t2
                    if i >= 4:
                        sc.wait_ge(s1spb if i % 2 == 0 else s1spv,
                                   16 * ((i - 4) // 2 + 1))
                    sc.copy(VST[i % 4], bank((j2 % 2) * 4 + t2)[:, 0:512]).then_inc(s1scb, 1)
                    if i >= 2 and i % 2 == 0:
                        vspill(i - 1)
            vspill(63)
            if PH < 2:
                return
            sc.wait_ge(s1sp, 16 * 16)
            for m in range(NM):
                for u in range(8):
                    sc.wait_ge(s2st, 8 * m + u + 1)
                    if 8 * m + u >= 4:
                        sc.wait_ge(s2pv, 8 * m + u - 3)
                    sc.activation(PT[u % 4], bank(2 * (u % 2), 2)[:, 0:1024],
                                  AF.Exp, bias=CVC[:, 0:1],
                                  scale=ASCL).then_inc(s2sc, 1)
            if PH < 3:
                return
            for it in range(NT + 3):
                if 1 <= it <= NT:
                    t1 = it - 1
                    p1 = t1 % 4
                    sc.wait_ge(s3v1, t1 + 1)
                    sc.activation(stat(p1, S_SD), stat(p1, S_VE), AF.Sqrt)
                if 2 <= it <= NT + 1:
                    t2 = it - 2
                    p2 = t2 % 4
                    sc.wait_ge(s3v2, t2 + 1)
                    if t2 >= 3:
                        sc.wait_ge(s3spf, 16 * (t2 - 2))
                        sc.wait_ge(s3spb, 16 * (t2 - 2))
                    sc.activation(X1O[t2 % 3], ROU[t2 % 3], AF.Identity,
                                  bias=stat(p2, S_NM), scale=stat(p2, S_RS)).then_inc(s3no, 1)
                if it < NT:
                    sc.wait_ge(s3ra, it + 1)
                    sc.activation(bank((it % 2) * 4, 4)[:, 0:2048], ROU[it % 3], AF.Identity,
                                  bias=zero_b, scale=1.0, accum_out=stat(it % 4, S_SUM))
                    sc.activation(XRS[it % 2], ROU[it % 3], AF.Square,
                                  accum_out=stat(it % 4, S_SQ)).then_inc(s3sq, 1)
                elif it == NT:
                    sspace(sc).then_inc(s3sq, 1)
            if PH < 4:
                return
            sc.wait_ge(s3no, NT)
            sc.wait_ge(s3ra, NT)
            sc.wait_ge(s3sq, NT)
            for th in range(2):
                for it in range(NHT + 1):
                    if it < NHT and not (th == 1 and it == 0):
                        i = th * NHT + it
                        sc.wait_ge(s4w[i % 2], 32 * (i // 2 + 1))
                        if i >= 2:
                            sc.wait_ge(s4pe, 2 * (i - 1))
                        sc.activation(GDQ[i % 2], GBF[i % 2], AF.Identity, bias=CVC[:, 1:2], scale=1.0)
                        sc.activation(UDQ[i % 2], UBF[i % 2], AF.Identity, bias=CVC[:, 1:2],
                                      scale=1.0).then_inc(s4dq, 2)
                    if it >= 1:
                        i1 = th * NHT + it - 1
                        ht1 = it - 1
                        sc.wait_ge(s4pe, 2 * i1 + 1)
                        sc.activation(GS[i1 % 2], bank((i1 % 2) * 4, 2)[:, 0:1024], AF.Silu,
                                      bias=zero_b, scale=SGC[:, ht1:ht1 + 1]).then_inc(s4sil, 1)
                for g8 in range(8):
                    d2 = (g8 + 4) % 8
                    sc.wait_ge(s4ra, th * 8 + max(g8, d2) + 1)
                    if g8 >= 4:
                        sc.wait_ge(s4sq, th * 8 + g8 - 3)
                    elif th == 1 and g8 == 0:
                        sc.wait_ge(s4sq, 8)
                    sc.activation(DSB[:, d2 * 2048:(d2 + 1) * 2048], XR8[g8], AF.Identity,
                                  bias=zero_b, scale=1.0, accum_out=S4B[:, g8:g8 + 1])
                    sc.activation(DSB[:, g8 * 2048:(g8 + 1) * 2048], XR8[g8], AF.Square,
                                  accum_out=S4B[:, 8 + g8:9 + g8]).then_inc(s4sq, 1)
                if th == 0:
                    i0 = NHT
                    sc.wait_ge(s4sq, 8)
                    sc.wait_ge(s4w[0], 32 * (i0 // 2 + 1))
                    sc.wait_ge(s4pe, 2 * (i0 - 1))
                    sc.activation(GDQ[0], GBF[0], AF.Identity, bias=CVC[:, 1:2], scale=1.0)
                    sc.activation(UDQ[0], UBF[0], AF.Identity, bias=CVC[:, 1:2],
                                  scale=1.0).then_inc(s4dq, 2)
                sc.wait_ge(s4v1, th + 1)
                sc.activation(S4B[:, 48:56], S4B[:, 40:48], AF.Sqrt)
                sspace(sc)
                sspace(sc).then_inc(s4sd, 1)
                sc.wait_ge(s4v2, th + 1)
                for g8 in range(8):
                    sc.activation(XR8[g8], XR8[g8], AF.Identity,
                                  bias=S4B[:, 64 + g8:65 + g8],
                                  scale=S4B[:, 56 + g8:57 + g8]).then_inc(s4no, 1)

        @blk.vector
        def _(v):
            if PH < 1:
                return
            for j in range(32):
                par = j % 2
                v.wait_ge(s1sc, j + 1)
                if j >= 2:
                    v.wait_ge(s1sw, 32 * (j - 1))
                blk_ap = QT[:, (j % 16) * 2048:((j % 16) + 1) * 2048] if j < 16 else KST[par]
                v.tensor_tensor(RT1[par], blk_ap[0:64, :], COS, AL.mult)
                v.tensor_tensor(TMG[par], blk_ap[0:64, :], GTB, AL.mult).then_inc(s1v1, 1)
                v.wait_ge(s1sw, 32 * (j + 1))
                v.tensor_tensor(blk_ap[0:64, :], RT1[par], RT2[par], AL.add).then_inc(s1v2, 1)
            if PH < 2:
                return
            for m in range(NM + 1):
                if m < NM:
                    v.wait_ge(s2pv, 8 * m + 8)
                    if m >= 2:
                        v.wait_ge(s2mu, m - 1)
                    v.tensor_copy(YRW[m % 2], bank(4 + m % 2)[:, 0:512]).then_inc(s2yr, 1)
                    if m >= 2:
                        v.wait_ge(s2g1, 16 * (m - 1))
                    v.tensor_copy(LSB[m % 2], bank(6 + m % 2)[0:1, 0:512]).then_inc(s2l, 1)
                    v.wait_ge(s2g1, 16 * (m + 1))
                    if m >= 2:
                        v.wait_ge(s2g2, 16 * (m - 1))
                    v.reciprocal(RLQ[m % 2], LQ[m % 2])
                    vspace(v).then_inc(s2r, 1)
                if 1 <= m:
                    mp = m - 1
                    hh, cc = divmod(mp, NCH)
                    v.wait_ge(s2bc, mp + 1)
                    v.tensor_tensor(YT[:, hh * 2048 + cc * 512: hh * 2048 + (cc + 1) * 512],
                                    YRW[mp % 2], bank(6 + mp % 2)[:, 0:512],
                                    AL.mult).then_inc(s2mu, 1)
            if PH < 3:
                return
            for it in range(NT + 3):
                if it < NT:
                    v.wait_ge(s3pe, it + 1)
                    v.wait_ge(s3x, 16 * (it + 1))
                    if it >= 3:
                        v.wait_ge(s3no, it - 2)
                    v.tensor_tensor(ROU[it % 3], bank((it % 2) * 4, 4)[:, 0:2048],
                                    XRS[it % 2], AL.add).then_inc(s3ra, 1)
                if 1 <= it <= NT:
                    t1 = it - 1
                    p1 = t1 % 4
                    v.wait_ge(s3sq, t1 + 1)
                    vstep(v, v.tensor_tensor(stat(p1, S_S2), stat(p1, S_SUM), stat(p1, S_SUM), AL.mult))
                    vstep(v, v.tensor_scalar(stat(p1, S_MN), stat(p1, S_SUM), 1.0 / D, None, AL.mult))
                    vstep(v, v.scalar_tensor_tensor(stat(p1, S_VN), stat(p1, S_S2), -1.0 / D,
                                                    stat(p1, S_SQ), AL.mult, AL.add))
                    vstep(v, v.tensor_scalar(stat(p1, S_VE), stat(p1, S_VN), 1.0 / D, EPS,
                                             AL.mult, AL.add))
                    vspace(v).then_inc(s3v1, 1)
                if 2 <= it <= NT + 1:
                    t2 = it - 2
                    p2 = t2 % 4
                    v.wait_ge(s3sq, min(t2 + 2, NT + 1))
                    vstep(v, v.reciprocal(stat(p2, S_RS), stat(p2, S_SD)))
                    vstep(v, v.scalar_tensor_tensor(stat(p2, S_NM), stat(p2, S_MN), -1.0,
                                                    stat(p2, S_RS), AL.mult, AL.mult))
                    vspace(v).then_inc(s3v2, 1)
            if PH < 4:
                return
            for th in range(2):
                for ht in range(NHT):
                    i = th * NHT + ht
                    if th == 1 and ht % 4 == 0 and ht <= 28:
                        go = ht // 4
                        v.wait_ge(s4out[go % 2], 16 * (go // 2 + 1))
                    v.wait_ge(s4sil, i + 1)
                    v.wait_ge(s4pe, 2 * i + 2)
                    v.scalar_tensor_tensor(HT[:, ht * 1024:(ht + 1) * 1024],
                                           bank((i % 2) * 4 + 2, 2)[:, 0:1024],
                                           SUC[:, ht:ht + 1], GS[i % 2],
                                           AL.mult, AL.mult).then_inc(s4stt, 1)
                for dc in range(4):
                    gidx = th * 4 + dc
                    for h2 in range(NHT):
                        i = gidx * NHT + h2
                        v.wait_ge(s4wd[i % 2], 16 * (i // 2 + 1))
                        if i >= 4:
                            v.wait_ge(s4dn, i - 3)
                        v.tensor_scalar(WDQ[i % 4], WDB[i % 4], -128.0, None,
                                        AL.add).then_inc(s4wq, 1)
                    v.wait_ge(s4dn, (gidx + 1) * NHT)
                    for tt in range(8):
                        v.tensor_tensor(DSB[:, tt * 2048 + dc * 512: tt * 2048 + (dc + 1) * 512],
                                        bank(tt)[:, 0:512], SDB[:, dc * 512:(dc + 1) * 512],
                                        AL.mult).then_inc(s4de, 1)
                for g8 in range(8):
                    g = th * 8 + g8
                    v.wait_ge(s4x, 16 * (g + 1))
                    v.wait_ge(s4de, 32 * (th + 1))
                    v.tensor_tensor(XR8[g8], XR8[g8], DSB[:, g8 * 2048:(g8 + 1) * 2048],
                                    AL.add).then_inc(s4ra, 1)
                v.wait_ge(s4sq, 8 * (th + 1))
                vstep(v, v.tensor_tensor(S4B[:, 16:24], S4B[:, 0:8], S4B[:, 0:8], AL.mult))
                vstep(v, v.tensor_scalar(S4B[:, 24:32], S4B[:, 0:8], 1.0 / D, None, AL.mult))
                vstep(v, v.scalar_tensor_tensor(S4B[:, 32:40], S4B[:, 16:24], -1.0 / D,
                                                S4B[:, 8:16], AL.mult, AL.add))
                vstep(v, v.tensor_scalar(S4B[:, 40:48], S4B[:, 32:40], 1.0 / D, EPS,
                                         AL.mult, AL.add))
                vspace(v).then_inc(s4v1, 1)
                v.wait_ge(s4sd, th + 1)
                vstep(v, v.reciprocal(S4B[:, 56:64], S4B[:, 48:56]))
                vstep(v, v.scalar_tensor_tensor(S4B[:, 64:72], S4B[:, 24:32], -1.0,
                                                S4B[:, 56:64], AL.mult, AL.mult))
                vspace(v).then_inc(s4v2, 1)

    return nc


def _host_prep(inputs):
    x = np.ascontiguousarray(np.asarray(inputs["L0_attn_x"], dtype=np.float32))
    wq = np.asarray(inputs["L0_attn_wq"], dtype=np.float32)
    wk = np.asarray(inputs["L0_attn_wk"], dtype=np.float32)
    wv = np.asarray(inputs["L0_attn_wv"], dtype=np.float32)
    wo = np.asarray(inputs["L0_attn_wo"], dtype=np.float32)
    wg = np.asarray(inputs["L0_ffn_wq_gate"], dtype=np.int32)
    wu = np.asarray(inputs["L0_ffn_wq_up"], dtype=np.int32)
    wd = np.asarray(inputs["L0_ffn_wq_down"], dtype=np.int32)
    sg = np.asarray(inputs["L0_ffn_s_gate"], dtype=np.float32)
    su = np.asarray(inputs["L0_ffn_s_up"], dtype=np.float32)
    sd = np.asarray(inputs["L0_ffn_s_down"], dtype=np.float32)

    def wblk(w):
        return np.ascontiguousarray(
            w.T.reshape(NDI, 128, NT, 128).transpose(2, 1, 0, 3).reshape(NT, 128, D))

    shared = {
        "wq_c": wblk(wq), "wk_c": wblk(wk),
        "wv_c": np.ascontiguousarray(
            wv.T.reshape(NDI, 128, 4, 512).transpose(2, 1, 0, 3).reshape(4, 128, NDI * 512)),
        "wo_T": np.ascontiguousarray(wo.T),
        "wg_c": np.ascontiguousarray(
            wg.T.reshape(NDI, 128, NHT, 128).transpose(2, 1, 0, 3).reshape(NHT, 128, 2048)),
        "wu_c": np.ascontiguousarray(
            wu.T.reshape(NDI, 128, NHT, 128).transpose(2, 1, 0, 3).reshape(NHT, 128, 2048)),
        "wd_c": np.ascontiguousarray(
            wd.T.reshape(NHT, 128, 4, 512).transpose(0, 2, 1, 3)),
        "sg_c": np.ascontiguousarray(sg.reshape(NHT, 128).T),
        "su_c": np.ascontiguousarray(su.reshape(NHT, 128).T),
        "sd_b": np.ascontiguousarray(np.tile(sd.reshape(1, 2048), (128, 1))),
    }
    in_maps = []
    for b in range(8):
        m = dict(shared)
        m["x"] = np.ascontiguousarray(x[b])
        m["xT"] = np.ascontiguousarray(x[b].T)
        in_maps.append(m)
    return in_maps


def kernel(**inputs):
    in_maps = _host_prep(inputs)
    nc = build_program()
    res = run_bass_kernel_spmd(nc, in_maps, core_ids=list(range(8)))
    return np.stack([r["out"] for r in res.results])


# version 10
# speedup vs baseline: 1.1632x; 1.0051x over previous
import math
import numpy as np
import ml_dtypes
import concourse.bass as bass
import concourse.mybir as mybir
from concourse.bass_utils import run_bass_kernel_spmd

F32 = mybir.dt.float32
BF16 = mybir.dt.bfloat16
I32 = mybir.dt.int32
AF = mybir.ActivationFunctionType
AL = mybir.AluOpType
AX = mybir.AxisListType
bfdt = ml_dtypes.bfloat16

T = 2048
D = 2048
NH = 16
HD = 128
HID = 5632
NT = 16
NDI = 16
NHT = 44
QC = 512
NCH = 4
NM = NH * NCH
C_SHIFT = 24.0
ASCL = 1.0 / math.sqrt(HD)
EPS = 1e-5

DEBUG = False
PH = 4

ARENA_F32_COLS = 53040

OFF_SG    = 202704
OFF_SU    = 202896
OFF_CVEC  = 203088
OFF_ONESR = 203120
OFF_ONESB = 203632
OFF_SDB   = 203888
OFF_SPC   = 200704
OFF_SPC2  = 201728

OFF_XT    = 0
OFF_QT    = 65536
OFF_COS   = 131072
OFF_SING  = 135168
OFF_WBUF  = 139264
OFF_KSTG  = 147456
OFF_RT1   = 155648
OFF_TMPG  = 163840
OFF_RT2   = 172032
OFF_VW    = 180224
OFF_VSTG  = 196608

OFF_YT    = 0
OFF_KVK   = 131072
OFF_KVV   = 139264
OFF_PT    = 147456
OFF_LSB   = 155648
OFF_LQ    = 160000
OFF_RLQ   = 160064
OFF_RLROW = 160128
OFF_YRAW  = 164224

OFF_XRES  = 131072
OFF_ROUT  = 147456
OFF_STAT  = 172032
OFF_X1O   = 172608

OFF_X1T   = 0
OFF_HT    = 32768
OFF_GBUF  = 122880
OFF_UBUF  = 131072
OFF_GDQ   = 139264
OFF_UDQ   = 147456
OFF_GS    = 155648
OFF_DSB   = 122880
OFF_WDB   = 188416
OFF_WDQ   = 192512
OFF_STAT4 = 196608
OFF_XRES4 = 0
OFF_ROUT4 = 32768
OFF_SUMD4 = 57344
OFF_SUMD3 = 188992

S_SUM, S_SQ, S_S2, S_VN, S_VE, S_MN, S_SD, S_RS, S_NM = range(9)


def build_program():
    nc = bass.Bass("TRN2", target_bir_lowering=False)

    x_h   = nc.dram_tensor("x",   [T, D], F32, kind="ExternalInput").ap()
    xT_h  = nc.dram_tensor("xT",  [D, T], F32, kind="ExternalInput").ap()
    wq_h  = nc.dram_tensor("wq_c", [NT, 128, D], F32, kind="ExternalInput").ap()
    wk_h  = nc.dram_tensor("wk_c", [NT, 128, D], F32, kind="ExternalInput").ap()
    wv_h  = nc.dram_tensor("wv_c", [4, 128, NDI * 512], F32, kind="ExternalInput").ap()
    wo_h  = nc.dram_tensor("wo_T", [D, D], F32, kind="ExternalInput").ap()
    wg_h  = nc.dram_tensor("wg_c", [NHT, 128, 2048], I32, kind="ExternalInput").ap()
    wu_h  = nc.dram_tensor("wu_c", [NHT, 128, 2048], I32, kind="ExternalInput").ap()
    wd_h  = nc.dram_tensor("wd_c", [NHT, 4, 128, 512], I32, kind="ExternalInput").ap()
    sg_h  = nc.dram_tensor("sg_c", [128, NHT], F32, kind="ExternalInput").ap()
    su_h  = nc.dram_tensor("su_c", [128, NHT], F32, kind="ExternalInput").ap()
    sd_h  = nc.dram_tensor("sd_b", [128, 2048], F32, kind="ExternalInput").ap()
    out_h = nc.dram_tensor("out", [T, D], F32, kind="ExternalOutput").ap()

    scr_kind = "ExternalOutput" if DEBUG else "Internal"
    kt_sp = nc.dram_tensor("kt_sp", [NH, 128, T], BF16, kind=scr_kind).ap()
    v_sp  = nc.dram_tensor("v_sp",  [NT, 128, D], BF16, kind=scr_kind).ap()
    x1f   = nc.dram_tensor("x1f", [T, D], F32, kind=scr_kind).ap()
    x1b   = nc.dram_tensor("x1b", [T, D], BF16, kind=scr_kind).ap()
    qt_dbg = nc.dram_tensor("qt_dbg", [NH, 128, T], BF16, kind="ExternalOutput").ap() if DEBUG else None
    yt_dbg = nc.dram_tensor("yt_dbg", [NH, 128, T], BF16, kind="ExternalOutput").ap() if DEBUG else None

    inv_freq = (1.0 / 10000.0 ** (np.arange(0, HD, 2, dtype=np.float32) / HD)).astype(np.float32)
    ang = np.arange(T, dtype=np.float32)[:, None] * inv_freq[None, :]
    cosT = np.cos(ang).T.astype(bfdt)
    sinT = np.sin(ang).T.astype(np.float32)
    gtab = np.concatenate([sinT[32:64], -sinT[0:32]], axis=0).astype(bfdt)
    cos_h  = nc.inline_tensor(np.ascontiguousarray(cosT), name="cosT")
    g_h    = nc.inline_tensor(np.ascontiguousarray(gtab), name="gtab")
    ones_h = nc.inline_tensor(np.ones((128, 128), dtype=bfdt), name="onesb")
    onesr_h = nc.inline_tensor(np.ones((1, 128), dtype=np.float32), name="onesr")
    cvec = np.zeros((128, 4), dtype=np.float32)
    cvec[:, 0] = -C_SHIFT
    cvec[:, 1] = -128.0
    cvec[:, 2] = EPS
    cvec_h = nc.inline_tensor(cvec, name="cvec")
    zeros_h = nc.inline_tensor(np.zeros((128, 320), dtype=np.float32), name="zeros")

    AB = nc.alloc_sbuf_tensor("AB", [128, ARENA_F32_COLS], F32)
    PS = nc.alloc_psum_tensor("PS", [128, 4096], F32)
    ps_ap = PS.ap()

    def f32(off_b, p, cols):
        return AB.ap()[0:p, off_b // 4: off_b // 4 + cols]

    def bf16(off_b, p, cols):
        return AB.ap()[0:p, off_b // 4: off_b // 4 + cols // 2].bitcast(BF16)

    def bank(b, n=1):
        return ps_ap[:, b * 512:(b + n) * 512]

    XT  = bf16(OFF_XT, 128, NDI * 2048)
    QT  = bf16(OFF_QT, 128, NH * 2048)
    COS = bf16(OFF_COS, 64, 2048)
    GTB = bf16(OFF_SING, 64, 2048)
    WB  = [bf16(OFF_WBUF + i * 4096, 128, 2048) for i in range(2)]
    KST = [bf16(OFF_KSTG + i * 4096, 128, 2048) for i in range(2)]
    RT1 = [bf16(OFF_RT1 + i * 4096, 64, 2048) for i in range(2)]
    TMG = [bf16(OFF_TMPG + i * 4096, 64, 2048) for i in range(2)]
    RT2 = [bf16(OFF_RT2 + i * 4096, 64, 2048) for i in range(2)]
    VW  = [bf16(OFF_VW, 128, 8192), bf16(OFF_WBUF, 128, 8192)]
    VST = [bf16(OFF_VSTG + i * 1024, 128, 512) for i in range(4)]

    YT  = bf16(OFF_YT, 128, NH * 2048)
    KVK = [bf16(OFF_KVK + i * 4096, 128, 2048) for i in range(2)]
    KVV = [bf16(OFF_KVV + i * 4096, 128, 2048) for i in range(2)]
    PT  = [bf16(OFF_PT + i * 2048, 128, 1024) for i in range(4)]
    LSB = [f32(OFF_LSB + i * 2048, 1, 512) for i in range(2)]
    LQ  = [f32(OFF_LQ + i * 16, 128, 4) for i in range(2)]
    RLQ = [f32(OFF_RLQ + i * 16, 128, 4) for i in range(2)]
    RLR = [f32(OFF_RLROW + i * 2048, 1, 512) for i in range(2)]
    YRW = [f32(OFF_YRAW + i * 2048, 128, 512) for i in range(2)]
    WOT = QT

    XRS = [f32(OFF_XRES + i * 8192, 128, 2048) for i in range(2)]
    XR8 = [f32(32768 + g * 8192, 128, 2048) for g in range(8)]
    S4B = f32(OFF_STAT4, 128, 80)
    ROU = [f32(OFF_ROUT + i * 8192, 128, 2048) for i in range(3)]
    X1O = [f32(OFF_X1O + i * 8192, 128, 2048) for i in range(3)]

    X1T = bf16(OFF_X1T, 128, NDI * 1024)
    HT  = bf16(OFF_HT, 128, NHT * 1024)
    GBF = [bf16(OFF_GBUF + i * 4096, 128, 2048) for i in range(2)]
    UBF = [bf16(OFF_UBUF + i * 4096, 128, 2048) for i in range(2)]
    GDQ = [bf16(OFF_GDQ + i * 4096, 128, 2048) for i in range(2)]
    UDQ = [bf16(OFF_UDQ + i * 4096, 128, 2048) for i in range(2)]
    GS  = [f32(OFF_GS + i * 4096, 128, 1024) for i in range(2)]
    DSB = f32(OFF_DSB, 128, 8 * 2048)
    WDB = [bf16(OFF_WDB + i * 1024, 128, 512) for i in range(4)]
    WDQ = [bf16(OFF_WDQ + i * 1024, 128, 512) for i in range(4)]
    RO4 = [f32(OFF_ROUT4 + i * 8192, 128, 2048) for i in range(3)]
    XR4 = [f32(OFF_XRES4 + i * 8192, 128, 2048) for i in range(4)]
    SD3 = f32(OFF_SUMD3, 128, 2048)
    SD4 = f32(OFF_SUMD4, 128, 2048)

    SGC = f32(OFF_SG, 128, NHT)
    SUC = f32(OFF_SU, 128, NHT)
    CVC = f32(OFF_CVEC, 128, 4)
    ONR = f32(OFF_ONESR, 1, 128)
    ONB = bf16(OFF_ONESB, 128, 128)
    ONRB = bf16(OFF_ONESB, 1, 128)
    RLRB = [bf16(OFF_RLROW + i * 2048, 1, 512) for i in range(2)]
    SDB = f32(OFF_SDB, 128, 2048)
    SPC = f32(OFF_SPC, 128, 256)
    SPC2 = f32(OFF_SPC2, 128, 64)

    vf_sem = nc.alloc_semaphore("vfence")
    vf_n = [0]
    def vstep(v, instr, spacers=2):
        instr.then_inc(vf_sem, 1)
        vf_n[0] += 1
        for _ in range(spacers):
            vspace(v)
        v.wait_ge(vf_sem, vf_n[0])

    vsp_n = [0]
    def vspace(v):
        n = vsp_n[0] % 8
        vsp_n[0] += 1
        return v.tensor_copy(SPC[:, 32 * n:32 * n + 32], SDB[:, 0:32])

    ssp_n = [0]
    def sspace(sc):
        n = ssp_n[0] % 2
        ssp_n[0] += 1
        return sc.copy(SPC2[:, 32 * n:32 * n + 32], SDB[:, 32:64])

    def stat(par, idx):
        return f32(OFF_STAT + par * 144, 128, 36)[:, idx:idx + 1]

    def stat4(par, idx):
        return f32(OFF_STAT4 + par * 144, 128, 36)[:, idx:idx + 1]

    def sem(name):
        return nc.alloc_semaphore(name)

    s0c = sem("s0c"); s0x = sem("s0x"); s0sd = sem("s0sd"); s0d = sem("s0d")
    s1w = [sem("s1wA"), sem("s1wB")]; s1pe = sem("s1pe"); s1sc = sem("s1sc")
    s1v1 = sem("s1v1"); s1sw = sem("s1sw"); s1v2 = sem("s1v2"); s1sp = sem("s1sp")
    s1vw = [sem("s1vwA"), sem("s1vwB")]; s1pb = sem("s1pb"); s1scb = sem("s1scb"); s1spb = sem("s1spb"); s1spv = sem("s1spv")
    s2kv = [sem("s2kvA"), sem("s2kvB")]; s2st = sem("s2st"); s2pv = sem("s2pv"); s2sc = sem("s2sc")
    s2yr = sem("s2yr"); s2l = sem("s2l"); s2r = sem("s2r"); s2g1 = sem("s2g1")
    s2g2 = sem("s2g2"); s2bc = sem("s2bc"); s2mu = sem("s2mu")
    s2wo = sem("s2wo")
    s3x = sem("s3x"); s3pe = sem("s3pe"); s3ra = sem("s3ra"); s3sq = sem("s3sq")
    s3v1 = sem("s3v1"); s3v2 = sem("s3v2"); s3no = sem("s3no")
    s3spf = sem("s3spf"); s3spb = sem("s3spb")
    s4xt = [sem("s4xtA"), sem("s4xtB")]; s4w = [sem("s4wA"), sem("s4wB")]; s4dq = sem("s4dq"); s4pe = sem("s4pe")
    s4sil = sem("s4sil"); s4stt = sem("s4stt"); s4wd = [sem("s4wdA"), sem("s4wdB")]; s4wq = sem("s4wq")
    s4dn = sem("s4dn"); s4de = sem("s4de"); s4x = sem("s4x"); s4ra = sem("s4ra")
    s4sq = sem("s4sq"); s4v1 = sem("s4v1"); s4v2 = sem("s4v2"); s4sd = sem("s4sd")
    s4no = sem("s4no"); s4out = [sem("s4outA"), sem("s4outB")]

    with nc.Block() as blk:

        @blk.sync
        def _(s):
            s.dma_start(out=COS, in_=cos_h.ap()).then_inc(s0c, 16)
            s.dma_start(out=GTB, in_=g_h.ap()).then_inc(s0c, 16)
            s.dma_start(out=ONB, in_=ones_h.ap()).then_inc(s0c, 16)
            s.dma_start(out=ONR, in_=onesr_h.ap()).then_inc(s0c, 16)
            s.dma_start(out=CVC, in_=cvec_h.ap()).then_inc(s0c, 16)
            s.dma_start(out=SGC, in_=sg_h).then_inc(s0c, 16)
            s.dma_start(out=SUC, in_=su_h).then_inc(s0c, 16)
            s.dma_start(out=SDB, in_=sd_h).then_inc(s0c, 16)
            s.dma_start(out=f32(OFF_SPC, 128, 320), in_=zeros_h.ap()).then_inc(s0c, 16)
            if PH < 1:
                s.wait_ge(s0x, 16 * NDI)
                s.wait_ge(s0sd, 1)
                return
            for j in range(32):
                par = j % 2
                s.wait_ge(s1v1, j + 1)
                if j >= 2:
                    s.wait_ge(s1v2, j - 1)
                s.dma_start(out=RT2[par][0:32, :], in_=TMG[par][32:64, :]).then_inc(s1sw, 16)
                s.dma_start(out=RT2[par][32:64, :], in_=TMG[par][0:32, :]).then_inc(s1sw, 16)
                if j >= 16:
                    ot = j - 16
                    s.wait_ge(s1v2, j + 1)
                    if ot >= 1:
                        s.wait_ge(s1sp, 16 * ot)
                    s.dma_start(out=kt_sp[ot], in_=KST[par]).then_inc(s1sp, 16)
            for i2 in range(32):
                i = 2 * i2
                j2, t2 = divmod(i, 4)
                dc, q = divmod(j2, 4)
                s.wait_ge(s1scb, i + 1)
                if i2 >= 1:
                    s.wait_ge(s1spb, 16 * i2)
                s.dma_start(out=v_sp[q * 4 + t2][:, dc * 512:(dc + 1) * 512],
                            in_=VST[i % 4]).then_inc(s1spb, 16)
            if DEBUG:
                s.wait_ge(s1v2, 32)
                for h in range(NH):
                    s.dma_start(out=qt_dbg[h], in_=QT[:, h * 2048:(h + 1) * 2048]).then_inc(s0d, 16)
            if PH < 2:
                s.wait_ge(s1sp, 16 * 16)
                s.wait_ge(s1spb, 16 * 32)
                s.wait_ge(s1spv, 16 * 32)
                if DEBUG:
                    s.wait_ge(s0d, 16 * NH)
                return
            s.wait_ge(s1sp, 16 * 16)
            s.wait_ge(s1spb, 16 * 32)
            s.wait_ge(s1spv, 16 * 32)
            for h in (0, 1):
                s.dma_start(out=KVK[h % 2], in_=kt_sp[h]).then_inc(s2kv[h % 2], 16)
                s.dma_start(out=KVV[h % 2],
                            in_=v_sp[:, :, h * 128:(h + 1) * 128].rearrange("k p b -> p k b")).then_inc(s2kv[h % 2], 16)
            for m in range(NM):
                h, c = divmod(m, NCH)
                s.wait_ge(s2l, m + 1)
                if m >= 1:
                    s.wait_ge(s2g1, 16 * m)
                if m >= 2:
                    s.wait_ge(s2r, m - 1)
                s.dma_start(out=LQ[m % 2], in_=LSB[m % 2]).then_inc(s2g1, 16)
                if c == NCH - 1 and h + 2 < NH:
                    hh = h + 2
                    s.wait_ge(s2pv, 32 * (hh - 1))
                    s.wait_ge(s2kv[hh % 2], 32 * (hh // 2))
                    s.dma_start(out=KVK[hh % 2], in_=kt_sp[hh]).then_inc(s2kv[hh % 2], 16)
                    s.dma_start(out=KVV[hh % 2],
                                in_=v_sp[:, :, hh * 128:(hh + 1) * 128].rearrange("k p b -> p k b")).then_inc(s2kv[hh % 2], 16)
            if DEBUG:
                s.wait_ge(s2mu, NM)
                for h in range(NH):
                    s.dma_start(out=yt_dbg[h], in_=YT[:, h * 2048:(h + 1) * 2048]).then_inc(s0d, 16)
            if PH < 3:
                s.wait_ge(s2mu, NM)
                if DEBUG:
                    s.wait_ge(s0d, 16 * 2 * NH)
                return
            s.wait_ge(s2mu, NM)
            for it in range(NT + 4):
                if it < NT:
                    if it >= 2:
                        s.wait_ge(s3ra, it - 1)
                        s.wait_ge(s3sq, it - 1)
                    if it >= 1:
                        s.wait_ge(s3x, 16 * it)
                    s.dma_start(out=XRS[it % 2], in_=x_h[it * 128:(it + 1) * 128, :]).then_inc(s3x, 16)
                sp = it - 3
                if 0 <= sp < NT:
                    s.wait_ge(s3no, sp + 1)
                    if sp >= 1:
                        s.wait_ge(s3spf, 16 * sp)
                    s.dma_start(out=x1f[sp * 128:(sp + 1) * 128, :], in_=X1O[sp % 3]).then_inc(s3spf, 16)
            if PH < 4:
                s.wait_ge(s3spf, 16 * NT)
                s.wait_ge(s3spb, 16 * NT)
                if DEBUG:
                    s.wait_ge(s0d, 16 * 2 * NH)
                return
            for th in range(2):
                if th == 0:
                    s.wait_ge(s3pe, NT)
                    s.wait_ge(s3spb, 16 * 8)
                    if DEBUG:
                        s.wait_ge(s0d, 16 * 2 * NH)
                else:
                    s.wait_ge(s4pe, 2 * NHT)
                    s.wait_ge(s3spb, 16 * NT)
                for dt in range(NDI):
                    gi = th * NDI + dt
                    if gi >= 2:
                        s.wait_ge(s4xt[gi % 2], 16 * (gi // 2))
                    s.dma_start_transpose(
                        X1T[:, dt * 1024:(dt + 1) * 1024],
                        x1b[th * 1024:(th + 1) * 1024, dt * 128:(dt + 1) * 128],
                    ).then_inc(s4xt[gi % 2], 16)
                if th == 0:
                    s.wait_ge(s3spf, 16 * NT)
            for th in range(2):
                for g8 in range(8):
                    g = th * 8 + g8
                    s.wait_ge(s4dn, (th * 4 + 3) * NHT + 4 * g8 + 4)
                    if g >= 1:
                        s.wait_ge(s4x, 16 * g)
                    s.dma_start(out=XR8[g8], in_=x1f[g * 128:(g + 1) * 128, :]).then_inc(s4x, 16)
                for g8 in range(8):
                    g = th * 8 + g8
                    s.wait_ge(s4no, g + 1)
                    if g >= 2:
                        s.wait_ge(s4out[g % 2], 16 * (g // 2))
                    s.dma_start(out=out_h[g * 128:(g + 1) * 128, :],
                                in_=XR8[g8]).then_inc(s4out[g % 2], 16)
            s.wait_ge(s4out[0], 16 * 8)
            s.wait_ge(s4out[1], 16 * 8)
            if DEBUG:
                s.wait_ge(s0d, 16 * (2 * NH))

        @blk.gpsimd
        def _(g):
            for dt in range(4):
                g.dma_start(out=XT[:, dt * 2048:(dt + 1) * 2048],
                            in_=xT_h[dt * 128:(dt + 1) * 128, :]).then_inc(s0x, 16)
            if PH >= 1:
                g.dma_start(out=WB[0], in_=wq_h[0]).then_inc(s1w[0], 16)
                g.dma_start(out=WB[1], in_=wq_h[1]).then_inc(s1w[1], 16)
            for b in range(1, 4):
                g.wait_ge(s0x, 16 * 4 * b)
                for dt in range(4 * b, 4 * b + 4):
                    g.dma_start(out=XT[:, dt * 2048:(dt + 1) * 2048],
                                in_=xT_h[dt * 128:(dt + 1) * 128, :]).then_inc(s0x, 16)
            if PH < 1:
                return
            g.dma_start(out=VW[0], in_=wv_h[0]).then_inc(s1vw[0], 16)
            for j in range(2, 32):
                src = wq_h[j] if j < 16 else wk_h[j - 16]
                g.wait_ge(s1pe, j - 1)
                g.wait_ge(s1w[j % 2], 16 * (j // 2))
                g.dma_start(out=WB[j % 2], in_=src).then_inc(s1w[j % 2], 16)
            for dc in range(1, 4):
                if dc == 1:
                    g.wait_ge(s1pe, 32)
                    g.wait_ge(s1sp, 16 * 16)
                if dc >= 2:
                    g.wait_ge(s1pb, 4 * (dc - 1))
                    g.wait_ge(s1vw[dc % 2], 16 * (dc // 2))
                g.dma_start(out=VW[dc % 2], in_=wv_h[dc]).then_inc(s1vw[dc % 2], 16)
            if PH < 2:
                return
            for m in range(NM):
                h, c = divmod(m, NCH)
                g.wait_ge(s2r, m + 1)
                if m >= 1:
                    g.wait_ge(s2g2, 16 * m)
                if m >= 2:
                    g.wait_ge(s2bc, m - 1)
                g.dma_start(out=RLRB[m % 2], in_=RLQ[m % 2]).then_inc(s2g2, 16)
                if c == NCH - 1:
                    g.wait_ge(s2pv, 32 * (h + 1))
                    if DEBUG and h == 0:
                        g.wait_ge(s0d, 16 * NH)
                    if h >= 1:
                        g.wait_ge(s2wo, 16 * h)
                    g.dma_start(out=WOT[:, h * 2048:(h + 1) * 2048],
                                in_=wo_h[h * 128:(h + 1) * 128, :]).then_inc(s2wo, 16)
            if PH < 3:
                return
            for tt in range(NT):
                g.wait_ge(s3no, tt + 1)
                if tt >= 1:
                    g.wait_ge(s3spb, 16 * tt)
                g.dma_start(out=x1b[tt * 128:(tt + 1) * 128, :], in_=X1O[tt % 3]).then_inc(s3spb, 16)
                if tt == 12:
                    g.wait_ge(s3pe, NT)
                    g.wait_ge(s3ra, NT - 1)
                    g.wait_ge(s3sq, NT - 1)
                    for i0 in range(2):
                        g.dma_start(out=GBF[i0 % 2], in_=wg_h[i0]).then_inc(s4w[i0 % 2], 16)
                        g.dma_start(out=UBF[i0 % 2], in_=wu_h[i0]).then_inc(s4w[i0 % 2], 16)
            if PH < 4:
                return
            for th in range(2):
                for ht in range(NHT):
                    i = th * NHT + ht
                    if i < 2:
                        continue
                    if th == 1 and ht == 0:
                        g.wait_ge(s4ra, 2)
                        g.wait_ge(s4sq, 6)
                    g.wait_ge(s4dq, 2 * (i - 1))
                    g.wait_ge(s4w[i % 2], 32 * (i // 2))
                    g.dma_start(out=GBF[i % 2], in_=wg_h[ht]).then_inc(s4w[i % 2], 16)
                    g.dma_start(out=UBF[i % 2], in_=wu_h[ht]).then_inc(s4w[i % 2], 16)
                for dc in range(4):
                    for h2 in range(NHT):
                        i = (th * 4 + dc) * NHT + h2
                        if i >= 4:
                            g.wait_ge(s4wq, i - 3)
                        if i >= 2:
                            g.wait_ge(s4wd[i % 2], 16 * (i // 2))
                        g.dma_start(out=WDB[i % 4], in_=wd_h[h2, dc]).then_inc(s4wd[i % 2], 16)

        @blk.tensor
        def _(t):
            t.wait_ge(s0c, 16 * 9)
            if PH < 1:
                return
            for j in range(32):
                t.wait_ge(s1w[j % 2], 16 * (j // 2 + 1))
                if j >= 2:
                    t.wait_ge(s1sc, j - 1)
                grp = (j % 2) * 4
                for di in range(NDI):
                    if j == 0 and di % 4 == 0:
                        t.wait_ge(s0x, 16 * min(4 * (di // 4 + 1), NDI))
                    for cch in range(4):
                        mm = t.matmul(bank(grp + cch)[:, 0:512],
                                      WB[j % 2][:, di * 128:(di + 1) * 128],
                                      XT[:, di * 2048 + cch * 512: di * 2048 + (cch + 1) * 512],
                                      start=(di == 0), stop=(di == NDI - 1))
                mm.then_inc(s1pe, 1)
            t.wait_ge(s1sc, 32)
            for j2 in range(16):
                dc, q = divmod(j2, 4)
                if q == 0:
                    t.wait_ge(s1vw[dc % 2], 16 * (dc // 2 + 1))
                if j2 >= 2:
                    t.wait_ge(s1scb, 4 * (j2 - 1))
                for di in range(NDI):
                    for t2 in range(4):
                        tg = q * 4 + t2
                        mm = t.matmul(bank((j2 % 2) * 4 + t2)[:, 0:512],
                                      XT[:, di * 2048 + tg * 128: di * 2048 + (tg + 1) * 128],
                                      VW[dc % 2][:, di * 512:(di + 1) * 512],
                                      start=(di == 0), stop=(di == NDI - 1))
                mm.then_inc(s1pb, 1)
            if PH < 2:
                return
            for m in range(NM):
                h, c = divmod(m, NCH)
                if c == 0:
                    t.wait_ge(s2kv[h % 2], 32 * (h // 2 + 1))
                if m >= 2:
                    t.wait_ge(s2yr, m - 1)
                    t.wait_ge(s2mu, m - 1)
                qs = h * 2048 + c * 512
                for u in range(8):
                    k0, k1 = 2 * u, 2 * u + 1
                    slot = u % 2
                    if u >= 2:
                        t.wait_ge(s2sc, 2 * (8 * m + u - 1))
                    elif m >= 1:
                        t.wait_ge(s2sc, 2 * (8 * (m - 1) + 7 + (1 if u == 1 else 0)))
                    t.matmul(bank(2 * slot)[:, 0:512],
                             KVK[h % 2][:, k0 * 128:(k0 + 1) * 128],
                             QT[:, qs:qs + 512], start=True, stop=True)
                    t.matmul(bank(2 * slot + 1)[:, 0:512],
                             KVK[h % 2][:, k1 * 128:(k1 + 1) * 128],
                             QT[:, qs:qs + 512], start=True,
                             stop=True).then_inc(s2st, 1)
                    if u >= 1:
                        up = u - 1
                        for kk in (2 * up, 2 * up + 1):
                            t.wait_ge(s2sc, 2 * (8 * m + up) + (kk % 2) + 1)
                            t.matmul(bank(4 + m % 2)[:, 0:512],
                                     KVV[h % 2][:, kk * 128:(kk + 1) * 128],
                                     PT[up % 4][:, (kk % 2) * 512:(kk % 2 + 1) * 512],
                                     start=(kk == 0), stop=False)
                            mm = t.matmul(bank(6 + m % 2)[0:1, 0:512],
                                          ONB[:, 0:1],
                                          PT[up % 4][:, (kk % 2) * 512:(kk % 2 + 1) * 512],
                                          start=(kk == 0), stop=False)
                        mm.then_inc(s2pv, 1)
                for kk in (14, 15):
                    t.wait_ge(s2sc, 2 * (8 * m + 7) + (kk % 2) + 1)
                    t.matmul(bank(4 + m % 2)[:, 0:512],
                             KVV[h % 2][:, kk * 128:(kk + 1) * 128],
                             PT[3][:, (kk % 2) * 512:(kk % 2 + 1) * 512],
                             start=False, stop=(kk == 15))
                    mm = t.matmul(bank(6 + m % 2)[0:1, 0:512],
                                  ONB[:, 0:1],
                                  PT[3][:, (kk % 2) * 512:(kk % 2 + 1) * 512],
                                  start=False, stop=(kk == 15))
                mm.then_inc(s2pv, 1)
                if m >= 1:
                    t.wait_ge(s2g2, 16 * m)
                    t.matmul(bank(6 + (m - 1) % 2)[:, 0:512], ONRB, RLRB[(m - 1) % 2],
                             start=True, stop=True).then_inc(s2bc, 1)
            t.wait_ge(s2g2, 16 * NM)
            t.matmul(bank(6 + (NM - 1) % 2)[:, 0:512], ONRB, RLRB[(NM - 1) % 2],
                     start=True, stop=True).then_inc(s2bc, 1)
            if PH < 3:
                return
            t.wait_ge(s2mu, NM)
            for tt in range(NT):
                if tt >= 2:
                    t.wait_ge(s3ra, tt - 1)
                    t.wait_ge(s3sq, tt - 1)
                for y in range(NH):
                    t.wait_ge(s2wo, 16 * (y + 1))
                    for cch in range(4):
                        mm = t.matmul(bank((tt % 2) * 4 + cch)[:, 0:512],
                                      YT[:, y * 2048 + tt * 128: y * 2048 + (tt + 1) * 128],
                                      WOT[:, y * 2048 + cch * 512: y * 2048 + (cch + 1) * 512],
                                      start=(y == 0), stop=(y == NH - 1))
                mm.then_inc(s3pe, 1)
            if PH < 4:
                return
            t.wait_ge(s3ra, NT)
            for th in range(2):
                if th == 0:
                    t.wait_ge(s3spf, 16 * NT)
                    t.wait_ge(s3spb, 16 * NT)
                for ht in range(NHT):
                    i = th * NHT + ht
                    t.wait_ge(s4dq, 2 * (i + 1))
                    if i >= 2:
                        t.wait_ge(s4stt, i - 1)
                    if ht == 0 and th == 1:
                        t.wait_ge(s4de, 32)
                    par = i % 2
                    for di in range(NDI):
                        if ht == 0:
                            t.wait_ge(s4xt[di % 2], 16 * (th * 8 + di // 2 + 1))
                        for cch in range(2):
                            mm = t.matmul(bank(par * 4 + cch)[:, 0:512],
                                          GDQ[par][:, di * 128:(di + 1) * 128],
                                          X1T[:, di * 1024 + cch * 512: di * 1024 + (cch + 1) * 512],
                                          start=(di == 0), stop=(di == NDI - 1))
                    mm.then_inc(s4pe, 1)
                    for di in range(NDI):
                        for cch in range(2):
                            mm = t.matmul(bank(par * 4 + 2 + cch)[:, 0:512],
                                          UDQ[par][:, di * 128:(di + 1) * 128],
                                          X1T[:, di * 1024 + cch * 512: di * 1024 + (cch + 1) * 512],
                                          start=(di == 0), stop=(di == NDI - 1))
                    mm.then_inc(s4pe, 1)
                t.wait_ge(s4stt, NHT * (th + 1))
                for dc in range(4):
                    gidx = th * 4 + dc
                    if gidx >= 1:
                        t.wait_ge(s4de, 8 * gidx)
                    for h2 in range(NHT):
                        i = gidx * NHT + h2
                        t.wait_ge(s4wq, i + 1)
                        for tt in range(8):
                            mm = t.matmul(bank(tt)[:, 0:512],
                                          HT[:, h2 * 1024 + tt * 128: h2 * 1024 + (tt + 1) * 128],
                                          WDQ[i % 4][:, 0:512],
                                          start=(h2 == 0), stop=(h2 == NHT - 1))
                        mm.then_inc(s4dn, 1)

        @blk.scalar
        def _(sc):
            zero_b = CVC[:, 3:4]
            if PH < 1:
                return
            for j in range(32):
                sc.wait_ge(s1pe, j + 1)
                if j >= 18:
                    sc.wait_ge(s1sp, 16 * (j - 17))
                grp = (j % 2) * 4
                dst = QT[:, (j % 16) * 2048:((j % 16) + 1) * 2048] if j < 16 else KST[j % 2]
                sc.copy(dst, bank(grp, 4)[:, 0:2048]).then_inc(s1sc, 1)
            def vspill(i):
                j2s, t2s = divmod(i, 4)
                dcs, qs = divmod(j2s, 4)
                sc.wait_ge(s1scb, i + 1)
                if i >= 3:
                    sc.wait_ge(s1spv, 16 * ((i - 1) // 2))
                sc.dma_start(out=v_sp[qs * 4 + t2s][:, dcs * 512:(dcs + 1) * 512],
                             in_=VST[i % 4]).then_inc(s1spv, 16)
            for j2 in range(16):
                sc.wait_ge(s1pb, j2 + 1)
                for t2 in range(4):
                    i = j2 * 4 + t2
                    if i >= 4:
                        sc.wait_ge(s1spb if i % 2 == 0 else s1spv,
                                   16 * ((i - 4) // 2 + 1))
                    sc.copy(VST[i % 4], bank((j2 % 2) * 4 + t2)[:, 0:512]).then_inc(s1scb, 1)
                    if i >= 2 and i % 2 == 0:
                        vspill(i - 1)
            vspill(63)
            if PH < 2:
                return
            sc.wait_ge(s1sp, 16 * 16)
            for m in range(NM):
                for u in range(8):
                    sc.wait_ge(s2st, 8 * m + u + 1)
                    if 8 * m + u >= 4:
                        sc.wait_ge(s2pv, 8 * m + u - 3)
                    sc.activation(PT[u % 4][:, 0:512], bank(2 * (u % 2))[:, 0:512],
                                  AF.Exp, bias=CVC[:, 0:1],
                                  scale=ASCL).then_inc(s2sc, 1)
                    sc.activation(PT[u % 4][:, 512:1024], bank(2 * (u % 2) + 1)[:, 0:512],
                                  AF.Exp, bias=CVC[:, 0:1],
                                  scale=ASCL).then_inc(s2sc, 1)
            if PH < 3:
                return
            for it in range(NT + 3):
                if 1 <= it <= NT:
                    t1 = it - 1
                    p1 = t1 % 4
                    sc.wait_ge(s3v1, t1 + 1)
                    sc.activation(stat(p1, S_SD), stat(p1, S_VE), AF.Sqrt)
                if 2 <= it <= NT + 1:
                    t2 = it - 2
                    p2 = t2 % 4
                    sc.wait_ge(s3v2, t2 + 1)
                    if t2 >= 3:
                        sc.wait_ge(s3spf, 16 * (t2 - 2))
                        sc.wait_ge(s3spb, 16 * (t2 - 2))
                    sc.activation(X1O[t2 % 3], ROU[t2 % 3], AF.Identity,
                                  bias=stat(p2, S_NM), scale=stat(p2, S_RS)).then_inc(s3no, 1)
                if it < NT:
                    sc.wait_ge(s3ra, it + 1)
                    sc.activation(bank((it % 2) * 4, 4)[:, 0:2048], ROU[it % 3], AF.Identity,
                                  bias=zero_b, scale=1.0, accum_out=stat(it % 4, S_SUM))
                    sc.activation(XRS[it % 2], ROU[it % 3], AF.Square,
                                  accum_out=stat(it % 4, S_SQ)).then_inc(s3sq, 1)
                elif it == NT:
                    sspace(sc).then_inc(s3sq, 1)
            if PH < 4:
                return
            sc.wait_ge(s3no, NT)
            sc.wait_ge(s3ra, NT)
            sc.wait_ge(s3sq, NT)
            for th in range(2):
                for it in range(NHT + 1):
                    if it < NHT and not (th == 1 and it == 0):
                        i = th * NHT + it
                        sc.wait_ge(s4w[i % 2], 32 * (i // 2 + 1))
                        if i >= 2:
                            sc.wait_ge(s4pe, 2 * (i - 1))
                        sc.activation(GDQ[i % 2], GBF[i % 2], AF.Identity, bias=CVC[:, 1:2], scale=1.0)
                        sc.activation(UDQ[i % 2], UBF[i % 2], AF.Identity, bias=CVC[:, 1:2],
                                      scale=1.0).then_inc(s4dq, 2)
                    if it >= 1:
                        i1 = th * NHT + it - 1
                        ht1 = it - 1
                        sc.wait_ge(s4pe, 2 * i1 + 1)
                        sc.activation(GS[i1 % 2], bank((i1 % 2) * 4, 2)[:, 0:1024], AF.Silu,
                                      bias=zero_b, scale=SGC[:, ht1:ht1 + 1]).then_inc(s4sil, 1)
                for g8 in range(8):
                    d2 = (g8 + 4) % 8
                    sc.wait_ge(s4ra, th * 8 + max(g8, d2) + 1)
                    if g8 >= 4:
                        sc.wait_ge(s4sq, th * 8 + g8 - 3)
                    elif th == 1 and g8 == 0:
                        sc.wait_ge(s4sq, 8)
                    sc.activation(DSB[:, d2 * 2048:(d2 + 1) * 2048], XR8[g8], AF.Identity,
                                  bias=zero_b, scale=1.0, accum_out=S4B[:, g8:g8 + 1])
                    sc.activation(DSB[:, g8 * 2048:(g8 + 1) * 2048], XR8[g8], AF.Square,
                                  accum_out=S4B[:, 8 + g8:9 + g8]).then_inc(s4sq, 1)
                if th == 0:
                    i0 = NHT
                    sc.wait_ge(s4sq, 8)
                    sc.wait_ge(s4w[0], 32 * (i0 // 2 + 1))
                    sc.wait_ge(s4pe, 2 * (i0 - 1))
                    sc.activation(GDQ[0], GBF[0], AF.Identity, bias=CVC[:, 1:2], scale=1.0)
                    sc.activation(UDQ[0], UBF[0], AF.Identity, bias=CVC[:, 1:2],
                                  scale=1.0).then_inc(s4dq, 2)
                sc.wait_ge(s4v1, th + 1)
                sc.activation(S4B[:, 48:56], S4B[:, 40:48], AF.Sqrt)
                sspace(sc)
                sspace(sc).then_inc(s4sd, 1)
                sc.wait_ge(s4v2, th + 1)
                for g8 in range(8):
                    sc.activation(XR8[g8], XR8[g8], AF.Identity,
                                  bias=S4B[:, 64 + g8:65 + g8],
                                  scale=S4B[:, 56 + g8:57 + g8]).then_inc(s4no, 1)

        @blk.vector
        def _(v):
            if PH < 1:
                return
            for j in range(32):
                par = j % 2
                v.wait_ge(s1sc, j + 1)
                if j >= 2:
                    v.wait_ge(s1sw, 32 * (j - 1))
                blk_ap = QT[:, (j % 16) * 2048:((j % 16) + 1) * 2048] if j < 16 else KST[par]
                v.tensor_tensor(RT1[par], blk_ap[0:64, :], COS, AL.mult)
                v.tensor_tensor(TMG[par], blk_ap[0:64, :], GTB, AL.mult).then_inc(s1v1, 1)
                v.wait_ge(s1sw, 32 * (j + 1))
                v.tensor_tensor(blk_ap[0:64, :], RT1[par], RT2[par], AL.add).then_inc(s1v2, 1)
            if PH < 2:
                return
            for m in range(NM + 1):
                if m < NM:
                    v.wait_ge(s2pv, 8 * m + 8)
                    if m >= 2:
                        v.wait_ge(s2mu, m - 1)
                    v.tensor_copy(YRW[m % 2], bank(4 + m % 2)[:, 0:512]).then_inc(s2yr, 1)
                    if m >= 2:
                        v.wait_ge(s2g1, 16 * (m - 1))
                    v.tensor_copy(LSB[m % 2], bank(6 + m % 2)[0:1, 0:512]).then_inc(s2l, 1)
                    v.wait_ge(s2g1, 16 * (m + 1))
                    if m >= 2:
                        v.wait_ge(s2g2, 16 * (m - 1))
                    v.reciprocal(RLQ[m % 2], LQ[m % 2])
                    vspace(v).then_inc(s2r, 1)
                if 1 <= m:
                    mp = m - 1
                    hh, cc = divmod(mp, NCH)
                    v.wait_ge(s2bc, mp + 1)
                    v.tensor_tensor(YT[:, hh * 2048 + cc * 512: hh * 2048 + (cc + 1) * 512],
                                    YRW[mp % 2], bank(6 + mp % 2)[:, 0:512],
                                    AL.mult).then_inc(s2mu, 1)
            if PH < 3:
                return
            for it in range(NT + 3):
                if it < NT:
                    v.wait_ge(s3pe, it + 1)
                    v.wait_ge(s3x, 16 * (it + 1))
                    if it >= 3:
                        v.wait_ge(s3no, it - 2)
                    v.tensor_tensor(ROU[it % 3], bank((it % 2) * 4, 4)[:, 0:2048],
                                    XRS[it % 2], AL.add).then_inc(s3ra, 1)
                if 1 <= it <= NT:
                    t1 = it - 1
                    p1 = t1 % 4
                    v.wait_ge(s3sq, t1 + 1)
                    vstep(v, v.tensor_tensor(stat(p1, S_S2), stat(p1, S_SUM), stat(p1, S_SUM), AL.mult))
                    vstep(v, v.tensor_scalar(stat(p1, S_MN), stat(p1, S_SUM), 1.0 / D, None, AL.mult))
                    vstep(v, v.scalar_tensor_tensor(stat(p1, S_VN), stat(p1, S_S2), -1.0 / D,
                                                    stat(p1, S_SQ), AL.mult, AL.add))
                    vstep(v, v.tensor_scalar(stat(p1, S_VE), stat(p1, S_VN), 1.0 / D, EPS,
                                             AL.mult, AL.add))
                    vspace(v).then_inc(s3v1, 1)
                if 2 <= it <= NT + 1:
                    t2 = it - 2
                    p2 = t2 % 4
                    v.wait_ge(s3sq, min(t2 + 2, NT + 1))
                    vstep(v, v.reciprocal(stat(p2, S_RS), stat(p2, S_SD)))
                    vstep(v, v.scalar_tensor_tensor(stat(p2, S_NM), stat(p2, S_MN), -1.0,
                                                    stat(p2, S_RS), AL.mult, AL.mult))
                    vspace(v).then_inc(s3v2, 1)
            if PH < 4:
                return
            for th in range(2):
                for ht in range(NHT):
                    i = th * NHT + ht
                    if th == 1 and ht % 4 == 0 and ht <= 28:
                        go = ht // 4
                        v.wait_ge(s4out[go % 2], 16 * (go // 2 + 1))
                    v.wait_ge(s4sil, i + 1)
                    v.wait_ge(s4pe, 2 * i + 2)
                    v.scalar_tensor_tensor(HT[:, ht * 1024:(ht + 1) * 1024],
                                           bank((i % 2) * 4 + 2, 2)[:, 0:1024],
                                           SUC[:, ht:ht + 1], GS[i % 2],
                                           AL.mult, AL.mult).then_inc(s4stt, 1)
                for dc in range(4):
                    gidx = th * 4 + dc
                    for h2 in range(NHT):
                        i = gidx * NHT + h2
                        v.wait_ge(s4wd[i % 2], 16 * (i // 2 + 1))
                        if i >= 4:
                            v.wait_ge(s4dn, i - 3)
                        v.tensor_scalar(WDQ[i % 4], WDB[i % 4], -128.0, None,
                                        AL.add).then_inc(s4wq, 1)
                    v.wait_ge(s4dn, (gidx + 1) * NHT)
                    for tt in range(8):
                        v.tensor_tensor(DSB[:, tt * 2048 + dc * 512: tt * 2048 + (dc + 1) * 512],
                                        bank(tt)[:, 0:512], SDB[:, dc * 512:(dc + 1) * 512],
                                        AL.mult).then_inc(s4de, 1)
                for g8 in range(8):
                    g = th * 8 + g8
                    v.wait_ge(s4x, 16 * (g + 1))
                    v.wait_ge(s4de, 32 * (th + 1))
                    v.tensor_tensor(XR8[g8], XR8[g8], DSB[:, g8 * 2048:(g8 + 1) * 2048],
                                    AL.add).then_inc(s4ra, 1)
                v.wait_ge(s4sq, 8 * (th + 1))
                vstep(v, v.tensor_tensor(S4B[:, 16:24], S4B[:, 0:8], S4B[:, 0:8], AL.mult))
                vstep(v, v.tensor_scalar(S4B[:, 24:32], S4B[:, 0:8], 1.0 / D, None, AL.mult))
                vstep(v, v.scalar_tensor_tensor(S4B[:, 32:40], S4B[:, 16:24], -1.0 / D,
                                                S4B[:, 8:16], AL.mult, AL.add))
                vstep(v, v.tensor_scalar(S4B[:, 40:48], S4B[:, 32:40], 1.0 / D, EPS,
                                         AL.mult, AL.add))
                vspace(v).then_inc(s4v1, 1)
                v.wait_ge(s4sd, th + 1)
                vstep(v, v.reciprocal(S4B[:, 56:64], S4B[:, 48:56]))
                vstep(v, v.scalar_tensor_tensor(S4B[:, 64:72], S4B[:, 24:32], -1.0,
                                                S4B[:, 56:64], AL.mult, AL.mult))
                vspace(v).then_inc(s4v2, 1)

    return nc


def _host_prep(inputs):
    x = np.ascontiguousarray(np.asarray(inputs["L0_attn_x"], dtype=np.float32))
    wq = np.asarray(inputs["L0_attn_wq"], dtype=np.float32)
    wk = np.asarray(inputs["L0_attn_wk"], dtype=np.float32)
    wv = np.asarray(inputs["L0_attn_wv"], dtype=np.float32)
    wo = np.asarray(inputs["L0_attn_wo"], dtype=np.float32)
    wg = np.asarray(inputs["L0_ffn_wq_gate"], dtype=np.int32)
    wu = np.asarray(inputs["L0_ffn_wq_up"], dtype=np.int32)
    wd = np.asarray(inputs["L0_ffn_wq_down"], dtype=np.int32)
    sg = np.asarray(inputs["L0_ffn_s_gate"], dtype=np.float32)
    su = np.asarray(inputs["L0_ffn_s_up"], dtype=np.float32)
    sd = np.asarray(inputs["L0_ffn_s_down"], dtype=np.float32)

    def wblk(w):
        return np.ascontiguousarray(
            w.T.reshape(NDI, 128, NT, 128).transpose(2, 1, 0, 3).reshape(NT, 128, D))

    shared = {
        "wq_c": wblk(wq), "wk_c": wblk(wk),
        "wv_c": np.ascontiguousarray(
            wv.T.reshape(NDI, 128, 4, 512).transpose(2, 1, 0, 3).reshape(4, 128, NDI * 512)),
        "wo_T": np.ascontiguousarray(wo.T),
        "wg_c": np.ascontiguousarray(
            wg.T.reshape(NDI, 128, NHT, 128).transpose(2, 1, 0, 3).reshape(NHT, 128, 2048)),
        "wu_c": np.ascontiguousarray(
            wu.T.reshape(NDI, 128, NHT, 128).transpose(2, 1, 0, 3).reshape(NHT, 128, 2048)),
        "wd_c": np.ascontiguousarray(
            wd.T.reshape(NHT, 128, 4, 512).transpose(0, 2, 1, 3)),
        "sg_c": np.ascontiguousarray(sg.reshape(NHT, 128).T),
        "su_c": np.ascontiguousarray(su.reshape(NHT, 128).T),
        "sd_b": np.ascontiguousarray(np.tile(sd.reshape(1, 2048), (128, 1))),
    }
    in_maps = []
    for b in range(8):
        m = dict(shared)
        m["x"] = np.ascontiguousarray(x[b])
        m["xT"] = np.ascontiguousarray(x[b].T)
        in_maps.append(m)
    return in_maps


def kernel(**inputs):
    in_maps = _host_prep(inputs)
    nc = build_program()
    res = run_bass_kernel_spmd(nc, in_maps, core_ids=list(range(8)))
    return np.stack([r["out"] for r in res.results])
